# version 14
# baseline (speedup 1.0000x reference)
import contextlib
import math
import numpy as np
import concourse.bass as bass
import concourse.mybir as mybir
from concourse.bass_utils import run_bass_kernel_spmd

F32 = mybir.dt.float32
BF16 = mybir.dt.bfloat16
AF = mybir.ActivationFunctionType
ALU = mybir.AluOpType
AX = mybir.AxisListType

B_PER, T, C, NH, HS, NL, NE, HD, SLAB = 2, 2048, 1024, 16, 64, 64, 8, 2048, 1024
NSLAB = (B_PER * T) // SLAB
EPS = 1e-5

_STOP_AFTER = None
_DBG_REQ = []
_N_ITER = 6
_N_EXPN = NH
_DO_PNV = True
_DO_DEN = True


class Ctr:
    __slots__ = ("h", "n")

    def __init__(self, h):
        self.h = h
        self.n = 0


class Emit:
    def __init__(self, blk):
        self.blk = blk

    def t(self, f):
        self.blk.tensor(f)

    def v(self, f):
        self.blk.vector(f)

    def s(self, f):
        self.blk.scalar(f)

    def q(self, f):
        self.blk.sync(f)

    def g(self, f):
        self.blk.gpsimd(f)


def build():
    nc = bass.Bass("TRN2", target_bir_lowering=False)
    ap = {}

    def din(name, shape, dt):
        ap[name] = nc.dram_tensor(name, shape, dt, kind="ExternalInput").ap()

    din("x", [B_PER * T, C], F32)
    din("W_attn", [C, 3 * C], F32)
    din("W_proj", [C, C], F32)
    din("abel_w", [C, C], F32)
    din("ew1", [NE * C, HD], F32)
    din("ew2", [NE * HD, C], F32)
    din("gens", [3 * C, C], F32)
    din("idb", [128, 128], BF16)
    din("idf", [128, 128], F32)
    din("onescol_b", [128, 1], BF16)
    din("onescol_f", [128, 1], F32)
    din("onesrow_b", [1, 128], BF16)
    din("onesrow_f", [1, 64], F32)
    din("eye7w", [128, 512], F32)
    din("eye15w", [128, 512], F32)
    din("eye13w", [128, 512], F32)
    din("eyeA", [128, 8 * C], BF16)
    din("cid0", [128, 1024], F32)
    din("cid1", [128, 1024], F32)
    din("bc2", [2, 128], F32)
    din("gcw1b", [128, C], F32)
    din("gcw2b", [128, C], F32)
    din("gcbb", [128, 1], F32)
    din("epsA", [128, 1], F32)
    din("epsB", [128, 1], F32)
    din("gamma_t", [128, 8], F32)
    din("beta_t", [128, 8], F32)
    din("eb1t", [128, 128], F32)
    din("eb2row", [1, NE * C], F32)
    din("abelbrow", [1, C], F32)
    din("gatebrow", [1, 8], F32)
    din("gatew_t", [128, 64], F32)

    out_ap = nc.dram_tensor("out", [B_PER * T, C], F32, kind="ExternalOutput").ap()
    s_x2 = nc.dram_tensor("s_x2", [B_PER * T, C], F32, kind="Internal").ap()
    s_eL = nc.dram_tensor("s_eL", [8 * 128, T], BF16, kind="Internal").ap()
    s_abel = nc.dram_tensor("s_abel", [B_PER * T, C], BF16, kind="Internal").ap()
    s_wab = nc.dram_tensor("s_wab", [C, 4 * C], BF16, kind="Internal").ap()
    s_ew1b = nc.dram_tensor("s_ew1b", [NE * C, HD], BF16, kind="Internal").ap()
    s_ew2b = nc.dram_tensor("s_ew2b", [NE * HD, C], BF16, kind="Internal").ap()

    dbg_aps = {}
    dbg_live = {}
    _dbg_shapes = {
        "u0": ([128, 8 * T], BF16), "k_sb": ([128, 8 * T], BF16),
        "v_sb": ([128, 16 * C], BF16), "klf": ([128, 512], F32),
        "qlf": ([128, 512], F32), "pair": ([128, T], BF16),
        "recipL": ([128, 256], F32), "S_all": ([64, 1024], F32),
        "V_all": ([64, 1024], F32), "U_all": ([64, 1024], F32),
        "PNv": ([128, 512], BF16), "y_sb": ([128, 16 * C], BF16),
        "stats": ([128, 128], F32), "R": ([128, 8 * C], BF16),
        "wd": ([128, 256], F32), "gd1": ([128, 32], F32),
        "acc": ([128, 8 * C], F32), "xn2s": ([128, 8 * SLAB], BF16),
        "hsb": ([128, 16 * SLAB], BF16),
    }
    for nm in _DBG_REQ:
        shp, dt = _dbg_shapes[nm]
        dbg_aps[nm] = nc.dram_tensor("dbg_" + nm, shp, dt, kind="ExternalOutput").ap()

    st = contextlib.ExitStack()
    with st:
        E = st.enter_context
        idb = E(nc.sbuf_tensor("c_idb", [128, 128], BF16))
        idf = E(nc.sbuf_tensor("c_idf", [128, 128], F32))
        onescol_b = E(nc.sbuf_tensor("c_ocb", [128, 1], BF16))
        onescol_f = E(nc.sbuf_tensor("c_ocf", [128, 1], F32))
        onesrow_b = E(nc.sbuf_tensor("c_orb", [1, 128], BF16))
        gcw2b = E(nc.sbuf_tensor("c_g2", [128, C], BF16))
        gcbb = E(nc.sbuf_tensor("c_gcb", [128, 1], F32))
        epsA = E(nc.sbuf_tensor("c_eA", [128, 1], F32))
        epsB = E(nc.sbuf_tensor("c_eB", [128, 1], F32))
        gamma_t = E(nc.sbuf_tensor("c_gam", [128, 8], F32))
        beta_t = E(nc.sbuf_tensor("c_bet", [128, 8], F32))
        eb1t = E(nc.sbuf_tensor("c_eb1", [128, 128], F32))
        R_bf = E(nc.sbuf_tensor("p_R", [128, 8 * C], BF16))
        stats = E(nc.sbuf_tensor("p_st", [128, 128], F32))
        wd_sb = E(nc.sbuf_tensor("p_wd", [128, 32 * 8], F32))
        gdot1 = E(nc.sbuf_tensor("p_gd1", [128, 32], F32))
        lnst = E(nc.sbuf_tensor("p_ln", [128, 8], F32))
        ps = [E(nc.psum_tensor(f"ps{i}", [128, 512], F32)) for i in range(6)]
        psb = [E(nc.psum_tensor(f"psb{i}", [128, 1024], BF16)) for i in range(2)]

        dS = Ctr(E(nc.semaphore("dS")))
        dG = Ctr(E(nc.semaphore("dG")))
        dE = Ctr(E(nc.semaphore("dE")))
        dP = Ctr(E(nc.semaphore("dP")))
        dQ = [Ctr(E(nc.semaphore("dQ0"))), Ctr(E(nc.semaphore("dQ1")))]
        dW0 = Ctr(E(nc.semaphore("dW0")))
        dW1 = Ctr(E(nc.semaphore("dW1")))
        dX = [Ctr(E(nc.semaphore("dX0"))), Ctr(E(nc.semaphore("dX1")))]
        oX = [Ctr(E(nc.semaphore("oX0"))), Ctr(E(nc.semaphore("oX1")))]
        pc = [Ctr(E(nc.semaphore("pc0"))), Ctr(E(nc.semaphore("pc1")))]
        pL = Ctr(E(nc.semaphore("pL")))
        pLv = Ctr(E(nc.semaphore("pLv")))
        pd = [Ctr(E(nc.semaphore("pd0"))), Ctr(E(nc.semaphore("pd1")))]
        eW = [Ctr(E(nc.semaphore("eW0"))), Ctr(E(nc.semaphore("eW1")))]
        vV = Ctr(E(nc.semaphore("vV")))
        sS = Ctr(E(nc.semaphore("sS")))
        hT = Ctr(E(nc.semaphore("hT")))
        hE = Ctr(E(nc.semaphore("hE")))
        oO = Ctr(E(nc.semaphore("oO")))

        def acc_group(t, out_sl, pairs, tp=None):
            n = len(pairs)
            for i, (lh, rh) in enumerate(pairs):
                ins = t.matmul(out_sl, lh, rh, start=(i == 0), stop=(i == n - 1),
                               tile_position=tp)
                if i == n - 1:
                    ins.then_inc(hT.h, 1)

        def end_fence(em):
            vals = [(c.h, c.n) for c in (dS, dG, dE, dP, dQ[0], dQ[1],
                                         dW0, dW1, dX[0], dX[1],
                                         oX[0], oX[1], pc[0], pc[1], eW[0], eW[1],
                                         pd[0], pd[1], pL, pLv, oO)]

            def _(q):
                for h_, n_ in vals:
                    q.wait_ge(h_, n_)
            em.q(_)

        def stop_block(phase):
            if _STOP_AFTER != phase:
                return False
            with nc.Block() as blk:
                em = Emit(blk)

                def _(q):
                    q.wait_ge(hT.h, hT.n)
                    q.wait_ge(vV.h, vV.n)
                    q.wait_ge(sS.h, sS.n)
                    q.wait_ge(hE.h, hE.n)
                    q.wait_ge(dG.h, dG.n)
                    q.wait_ge(dS.h, dS.n)
                    q.wait_ge(dE.h, dE.n)
                    q.wait_ge(dP.h, dP.n)
                    q.wait_ge(dQ[0].h, dQ[0].n)
                    q.wait_ge(dQ[1].h, dQ[1].n)
                    q.wait_ge(dW0.h, dW0.n)
                    q.wait_ge(dW1.h, dW1.n)
                    for c_ in dX + oX + pc + eW:
                        q.wait_ge(c_.h, c_.n)
                    q.wait_ge(oO.h, oO.n)
                    for nm in _DBG_REQ:
                        src = dbg_live.get(nm)
                        if src is None:
                            continue
                        q.dma_start(out=dbg_aps[nm], in_=src).then_inc(oO.h, 16)
                        oO.n += 16
                    q.wait_ge(oO.h, oO.n)
                em.q(_)
            return True

        with contextlib.ExitStack() as b0st:
            E0 = b0st.enter_context
            wst = [E0(nc.sbuf_tensor(f"w0_f{i}", [128, 4 * C], F32)) for i in range(2)]
            wsb = [E0(nc.sbuf_tensor(f"w0_b{i}", [128, 4 * C], BF16)) for i in range(2)]
            with nc.Block() as blk:
                em = Emit(blk)
                bv0 = vV.n

                def _(q):
                    for t_sb, srcn in [(idb, "idb"), (idf, "idf"), (onescol_b, "onescol_b"),
                                       (onescol_f, "onescol_f"), (onesrow_b, "onesrow_b"),
                                       (gcbb, "gcbb"), (epsA, "epsA"),
                                       (epsB, "epsB"), (gamma_t, "gamma_t"),
                                       (beta_t, "beta_t"), (eb1t, "eb1t")]:
                        q.dma_start(out=t_sb[:, :], in_=ap[srcn]).then_inc(dS.h, 16)
                        dS.n += 16
                    for i in range(10):
                        p2 = i % 2
                        if i >= 2:
                            j = i - 2
                            q.wait_ge(vV.h, bv0 + j + 1)
                            q.dma_start(out=s_wab[j * 128:(j + 1) * 128, :],
                                        in_=wsb[p2][:, :]).then_inc(pc[p2].h, 16)
                        if i < 8:
                            q.dma_start(out=wst[p2][:, 0:3 * C],
                                        in_=ap["W_attn"][i * 128:(i + 1) * 128, :]
                                        ).then_inc(pc[p2].h, 16)
                            q.dma_start(out=wst[p2][:, 3 * C:4 * C],
                                        in_=ap["W_proj"][i * 128:(i + 1) * 128, :]
                                        ).then_inc(pc[p2].h, 16)
                em.q(_)
                for i in range(10):
                    p2 = i % 2
                    if i >= 2:
                        pc[p2].n += 16
                    if i < 8:
                        pc[p2].n += 32

                for i in range(8):
                    p2 = i % 2

                    def _(v, p2=p2, w=48 * (i // 2) + 32):
                        v.wait_ge(pc[p2].h, w)
                        v.tensor_copy(wsb[p2][:, :], wst[p2][:, :])
                        v.drain().then_inc(vV.h, 1)
                    em.v(_)
                    vV.n += 1

                def _(g):
                    for t_sb, srcn in [(gcw2b, "gcw2b")]:
                        g.dma_start(out=t_sb[:, :], in_=ap[srcn]).then_inc(dG.h, 16)
                        dG.n += 16
                em.g(_)

                def _(t, w1=dS.n, w2=dG.n):
                    t.wait_ge(dS.h, w1)
                    t.wait_ge(dG.h, w2)
                em.t(_)
                end_fence(em)

        with contextlib.ExitStack() as ast:
            Ea = ast.enter_context
            Wb = [Ea(nc.sbuf_tensor(f"a_Wb{i}", [128, 8 * C], BF16)) for i in range(2)]
            eye7w = Ea(nc.sbuf_tensor("c_e7", [128, 512], F32))
            eye15w = Ea(nc.sbuf_tensor("c_e15", [128, 512], F32))
            eye13w = Ea(nc.sbuf_tensor("c_e13", [128, 512], F32))
            onesrow_f64 = Ea(nc.sbuf_tensor("c_orf", [1, 64], F32))
            k_sb = Ea(nc.sbuf_tensor("a_k", [128, 8 * T], BF16))
            v_sb = Ea(nc.sbuf_tensor("a_v", [128, 16 * C], BF16))
            q_l_f = Ea(nc.sbuf_tensor("a_qlf", [128, 512], F32))
            k_l_f = Ea(nc.sbuf_tensor("a_klf", [128, 512], F32))
            q_l_b = Ea(nc.sbuf_tensor("a_qlb", [128, 512], BF16))
            k_l_b = Ea(nc.sbuf_tensor("a_klb", [128, 512], BF16))
            U_all = Ea(nc.sbuf_tensor("a_U", [64, 1024], F32))
            PNv_b = Ea(nc.sbuf_tensor("a_PNv", [128, 512], BF16))
            recipL = Ea(nc.sbuf_tensor("a_rL", [128, 256], F32))
            rN = Ea(nc.sbuf_tensor("a_rN", [64, 16], F32))

            for b in range(B_PER):
                t0g = b * T

                with contextlib.ExitStack() as a1:
                    E1 = a1.enter_context
                    u0 = E1(nc.sbuf_tensor(f"a_u0_{b}", [128, 8 * T], BF16))
                    pairs = [E1(nc.sbuf_tensor(f"a_pair_{b}{i}", [128, T], BF16))
                             for i in range(2)]
                    tbufs = [E1(nc.sbuf_tensor(f"a_tb_{b}{i}", [128, T], BF16))
                             for i in range(2)]
                    xst = [E1(nc.sbuf_tensor(f"a_x{b}{i}", [128, C], F32)) for i in range(2)]
                    xnt_t = E1(nc.sbuf_tensor(f"a_n{b}", [128, C], BF16))
                    xnt = [xnt_t, xnt_t]

                    with nc.Block() as blk:
                        em = Emit(blk)
                        bx = dS.n
                        bv0 = vV.n

                        bX = [dX[0].n, dX[1].n]

                        def _(q, b=b, bX=bX):
                            if b == 0:
                                for t_sb, srcn in [(eye7w, "eye7w"), (eye15w, "eye15w"),
                                                   (eye13w, "eye13w"), (onesrow_f64, "onesrow_f")]:
                                    q.dma_start(out=t_sb[:, :], in_=ap[srcn]
                                                ).then_inc(dE.h, 16)
                            for i in range(16):
                                if i >= 2:
                                    q.wait_ge(vV.h, bv0 + 4 * (i - 1))
                                q.dma_start(out=xst[i % 2][:, :],
                                            in_=ap["x"][t0g + i * 128:t0g + (i + 1) * 128, :]
                                            ).then_inc(dX[i % 2].h, 16)
                        em.q(_)
                        if b == 0:
                            dE.n += 64
                        dX[0].n += 128
                        dX[1].n += 128

                        bW0 = dW0.n
                        bW1 = dW1.n
                        btM0 = hT.n

                        def _(g):
                            for dc in range(8):
                                g.dma_start(out=Wb[0][:, dc * C:(dc + 1) * C],
                                            in_=s_wab[dc * 128:(dc + 1) * 128, C:2 * C]
                                            ).then_inc(dW0.h, 16)
                            for dc in range(8):
                                g.dma_start(out=Wb[1][:, dc * C:(dc + 1) * C],
                                            in_=s_wab[dc * 128:(dc + 1) * 128, 0:C]
                                            ).then_inc(dW1.h, 16)
                            g.wait_ge(hT.h, btM0 + 128 + 32)
                            for dc in range(8):
                                g.dma_start(out=Wb[0][:, dc * C:(dc + 1) * C],
                                            in_=s_wab[dc * 128:(dc + 1) * 128, 2 * C:3 * C]
                                            ).then_inc(dW0.h, 16)
                            g.wait_ge(hT.h, btM0 + 160 + 7 * 44 + 4)
                            for dc in range(8):
                                g.dma_start(out=Wb[1][:, dc * C:(dc + 1) * C],
                                            in_=s_wab[dc * 128:(dc + 1) * 128, 3 * C:4 * C]
                                            ).then_inc(dW1.h, 16)
                        em.g(_)
                        dW0.n += 256
                        dW1.n += 256

                        bhT = hT.n
                        bhE = hE.n
                        for i in range(16):
                            def _(v, i=i):
                                v.wait_ge(dX[i % 2].h, bX[i % 2] + 16 * (i // 2 + 1))
                                v.reduce_sum(lnst[:, 0:1], xst[i % 2][:, :], axis=AX.X)
                                v.drain()
                                v.tensor_scalar(lnst[:, 1:2], lnst[:, 0:1], -1.0 / C, None,
                                                op0=ALU.mult)
                                v.drain().then_inc(vV.h, 1)
                            em.v(_)
                            vV.n += 1

                            def _(s, i=i):
                                s.wait_ge(dX[i % 2].h, bX[i % 2] + 16 * (i // 2 + 1))
                                s.activation(xnt[i % 2][:, :], xst[i % 2][:, :], AF.Square,
                                             scale=1.0 / 32.0, accum_out=lnst[:, 3:4])
                                s.drain().then_inc(sS.h, 1)
                            em.s(_)
                            sS.n += 1

                            def _(v, w=sS.n):
                                v.wait_ge(sS.h, w)
                                v.tensor_scalar(lnst[:, 2:3], lnst[:, 1:2], -1.0, None,
                                                op0=ALU.mult)
                                v.drain()
                                v.scalar_tensor_tensor(lnst[:, 4:5], lnst[:, 2:3], lnst[:, 1:2],
                                                       lnst[:, 3:4], op0=ALU.mult, op1=ALU.add)
                                v.drain().then_inc(vV.h, 1)
                            em.v(_)
                            vV.n += 1

                            def _(s, w=vV.n):
                                s.wait_ge(vV.h, w)
                                s.activation(lnst[:, 5:6], lnst[:, 4:5], AF.Ln, bias=epsA[:, 0:1])
                                s.drain()
                                s.activation(lnst[:, 6:7], lnst[:, 5:6], AF.Exp, scale=-0.5)
                                s.drain().then_inc(sS.h, 1)
                            em.s(_)
                            sS.n += 1

                            def _(v, i=i, w=sS.n):
                                v.wait_ge(sS.h, w)
                                if i >= 1:
                                    v.wait_ge(hT.h, bhT + 8 * i)
                                v.tensor_mul(lnst[:, 7:8], lnst[:, 1:2], lnst[:, 6:7])
                                v.drain()
                                v.tensor_scalar(xnt[i % 2][:, :], xst[i % 2][:, :],
                                                lnst[:, 6:7], lnst[:, 7:8],
                                                op0=ALU.mult, op1=ALU.add)
                                v.drain().then_inc(vV.h, 2)
                            em.v(_)
                            vV.n += 2

                            def _(t, i=i, w=vV.n):
                                t.wait_ge(vV.h, w)
                                for dc in range(8):
                                    j = i * 8 + dc
                                    t.wait_ge(hE.h, bhE + max(0, j - 1))
                                    t.transpose(psb[j % 2][:, 0:128],
                                                xnt[i % 2][:, dc * 128:(dc + 1) * 128],
                                                idb[:, :]).then_inc(hT.h, 1)
                            em.t(_)
                            hT.n += 8

                            def _(s, i=i):
                                for dc in range(8):
                                    j = i * 8 + dc
                                    s.wait_ge(hT.h, bhT + j + 1)
                                    s.copy(u0[:, dc * T + i * 128:dc * T + (i + 1) * 128],
                                           psb[j % 2][:, 0:128])
                                    s.drain().then_inc(hE.h, 1)
                            em.s(_)
                            hE.n += 8

                        bkT = hT.n
                        bkE = hE.n

                        def _(t):
                            t.wait_ge(hE.h, bkE)
                            t.wait_ge(dW0.h, bW0 + 128)
                            for u in range(32):
                                fc, ts = u // 4, u % 4
                                t.wait_ge(hE.h, bkE + max(0, u - 1))
                                acc_group(t, ps[u % 2][:, :],
                                          [(Wb[0][:, dc * C + fc * 128:dc * C + (fc + 1) * 128],
                                            u0[:, dc * T + ts * 512:dc * T + (ts + 1) * 512])
                                           for dc in range(8)])
                        em.t(_)
                        hT.n += 32

                        def _(s):
                            for u in range(32):
                                fc, ts = u // 4, u % 4
                                s.wait_ge(hT.h, bkT + u + 1)
                                s.copy(k_sb[:, fc * T + ts * 512:fc * T + (ts + 1) * 512],
                                       ps[u % 2][:, :])
                                s.drain().then_inc(hE.h, 1)
                        em.s(_)
                        hE.n += 32

                        def _(v, w=hE.n):
                            v.wait_ge(hE.h, w)
                            for fc in range(8):
                                v.reduce_sum(k_l_f[:, fc * 64:(fc + 1) * 64],
                                             k_sb[:, fc * T:(fc + 1) * T]
                                             .rearrange("p (l g) -> p l g", g=32), axis=AX.X)
                            v.drain()
                            v.tensor_copy(k_l_b[:, :], k_l_f[:, :])
                            v.drain().then_inc(vV.h, 1)
                        em.v(_)
                        vV.n += 1
                        klb_ready = vV.n

                        bqv = vV.n
                        bdQ = [dQ[0].n, dQ[1].n]
                        qb = {}

                        def emit_qmm(fc):
                            qb[fc] = (hT.n, hE.n)
                            bqE = hE.n

                            def _(t, fc=fc, bqE=bqE):
                                if fc == 0:
                                    t.wait_ge(dW1.h, bW1 + 128)
                                for ts in range(4):
                                    if fc == 0 or ts >= 2:
                                        t.wait_ge(hE.h, bqE + max(0, ts - 1))
                                    else:
                                        t.wait_ge(hE.h, bqE - 9 + ts)
                                    acc_group(t, ps[ts % 2][:, :],
                                              [(Wb[1][:, dc * C + fc * 128:dc * C + (fc + 1) * 128],
                                                u0[:, dc * T + ts * 512:dc * T + (ts + 1) * 512])
                                               for dc in range(8)])
                            em.t(_)
                            hT.n += 4

                        emit_qmm(0)
                        for fc in range(8):
                            tbuf = tbufs[fc % 2]
                            bqT, bqE = qb[fc]

                            def _(s, fc=fc, bqT=bqT, tbuf=tbuf):
                                if fc >= 2:
                                    s.wait_ge(vV.h, bqv + 2 * (fc - 2) + 1)
                                for ts in range(4):
                                    s.wait_ge(hT.h, bqT + ts + 1)
                                    s.copy(tbuf[:, ts * 512:(ts + 1) * 512], ps[ts % 2][:, :])
                                    s.drain().then_inc(hE.h, 1)
                            em.s(_)
                            hE.n += 4

                            def _(v, fc=fc, w=hE.n):
                                v.wait_ge(hE.h, w)
                                v.reduce_sum(q_l_f[:, fc * 64:(fc + 1) * 64],
                                             tbuf[:, :].rearrange("p (l g) -> p l g", g=32),
                                             axis=AX.X)
                                v.drain()
                                v.tensor_copy(q_l_b[:, fc * 64:(fc + 1) * 64],
                                              q_l_f[:, fc * 64:(fc + 1) * 64])
                                v.drain().then_inc(vV.h, 1)
                            em.v(_)
                            vV.n += 1

                            beT = hT.n
                            beE = hE.n

                            def _(t, fc=fc, beE=beE, w=hE.n):
                                t.wait_ge(vV.h, klb_ready)
                                t.wait_ge(hE.h, w)
                                for hh in range(2):
                                    p = ((2 * fc + hh) % 2) * 64
                                    for ts in range(4):
                                        u = hh * 4 + ts
                                        t.wait_ge(hE.h, beE + max(0, u - 1))
                                        t.matmul(ps[2 + (u % 2)][p:p + 64, :],
                                                 k_l_b[p:p + 64, fc * 64:(fc + 1) * 64],
                                                 tbuf[p:p + 64, ts * 512:(ts + 1) * 512],
                                                 start=True, stop=True,
                                                 tile_position=(p, p)).then_inc(hT.h, 1)
                            em.t(_)
                            hT.n += 8

                            def _(s, fc=fc, beT=beT, wd_=bdQ[fc % 2] + 16 * (fc // 2)):
                                s.wait_ge(dQ[fc % 2].h, wd_)
                                for hh in range(2):
                                    p = ((2 * fc + hh) % 2) * 64
                                    for ts in range(4):
                                        u = hh * 4 + ts
                                        s.wait_ge(hT.h, beT + u + 1)
                                        s.activation(pairs[fc % 2][p:p + 64, ts * 512:(ts + 1) * 512],
                                                     ps[2 + (u % 2)][p:p + 64, :],
                                                     AF.Exp, scale=1.0 / 32.0)
                                        s.drain().then_inc(hE.h, 1)
                            em.s(_)
                            hE.n += 8

                            if fc + 1 < 8:
                                emit_qmm(fc + 1)

                            def _(t, fc=fc, w=hE.n, wv=vV.n):
                                t.wait_ge(hE.h, w)
                                t.wait_ge(vV.h, wv)
                                for hh in range(2):
                                    h = 2 * fc + hh
                                    p = (h % 2) * 64
                                    for tc in range(16):
                                        t.matmul(ps[4][:, hh * 16 + tc:hh * 16 + tc + 1],
                                                 pairs[fc % 2][p:p + 64, tc * 128:(tc + 1) * 128],
                                                 onescol_b[p:p + 64, 0:1],
                                                 start=True, stop=True).then_inc(hT.h, 1)
                            em.t(_)
                            hT.n += 32

                            def _(v, fc=fc, w=hT.n):
                                v.wait_ge(hT.h, w)
                                for hh in range(2):
                                    h = 2 * fc + hh
                                    v.reciprocal(recipL[:, h * 16:(h + 1) * 16],
                                                 ps[4][:, hh * 16:hh * 16 + 16])
                                v.drain().then_inc(vV.h, 1)
                            em.v(_)
                            vV.n += 1

                            def _(q, fc=fc, w=vV.n):
                                q.wait_ge(vV.h, w)
                                q.dma_start(out=s_eL[fc * 128:(fc + 1) * 128, :],
                                            in_=pairs[fc % 2][:, :]).then_inc(dQ[fc % 2].h, 16)
                            em.q(_)
                            dQ[fc % 2].n += 16

                        bvT = hT.n
                        bvE = hE.n

                        def _(t):
                            t.wait_ge(dW0.h, bW0 + 256)
                            for u in range(32):
                                tc, n2 = u // 2, u % 2
                                t.wait_ge(hE.h, bvE + max(0, u - 1))
                                acc_group(t, ps[u % 2][:, :],
                                          [(u0[:, dc * T + tc * 128:dc * T + (tc + 1) * 128],
                                            Wb[0][:, dc * C + n2 * 512:dc * C + (n2 + 1) * 512])
                                           for dc in range(8)])
                        em.t(_)
                        hT.n += 32

                        def _(s):
                            for u in range(32):
                                tc, n2 = u // 2, u % 2
                                s.wait_ge(hT.h, bvT + u + 1)
                                s.copy(v_sb[:, tc * C + n2 * 512:tc * C + (n2 + 1) * 512],
                                       ps[u % 2][:, :])
                                s.drain().then_inc(hE.h, 1)
                        em.s(_)
                        hE.n += 32
                        end_fence(em)

                    if b == 0:
                        dbg_live.update(u0=u0[:, :], k_sb=k_sb[:, :], v_sb=v_sb[:, :],
                                        klf=k_l_f[:, :], qlf=q_l_f[:, :], pair=pairs[1][:, :],
                                        recipL=recipL[:, :])
                        if stop_block("a1b0"):
                            return nc, dbg_aps

                with contextlib.ExitStack() as a2:
                    E2 = a2.enter_context
                    ebuf = E2(nc.sbuf_tensor(f"a_eN_{b}", [128, 2048], BF16))
                    iS = E2(nc.sbuf_tensor(f"i_S_{b}", [64, 1024], F32))
                    iST = E2(nc.sbuf_tensor(f"i_ST_{b}", [64, 1024], F32))
                    iV = E2(nc.sbuf_tensor(f"i_V_{b}", [64, 1024], F32))
                    iVT = E2(nc.sbuf_tensor(f"i_VT_{b}", [64, 1024], F32))
                    iKVT = E2(nc.sbuf_tensor(f"i_KVT_{b}", [64, 1024], F32))
                    iT1 = E2(nc.sbuf_tensor(f"i_T1_{b}", [64, 1024], F32))
                    iT2 = E2(nc.sbuf_tensor(f"i_T2_{b}", [64, 1024], F32))
                    csum = E2(nc.sbuf_tensor(f"i_cs_{b}", [64, 16], F32))
                    rd = E2(nc.sbuf_tensor(f"i_rd_{b}", [16, 4], F32))
                    rdT = E2(nc.sbuf_tensor(f"i_rdT_{b}", [1, 16], F32))
                    denb = E2(nc.sbuf_tensor(f"i_db_{b}", [64, 16], F32))

                    with nc.Block() as blk:
                        em = Emit(blk)

                        def _(t):
                            for h in range(NH):
                                p = (h % 2) * 64
                                g2, j = h % 2, h // 2
                                t.matmul(ps[g2][0:64, j * 64:(j + 1) * 64],
                                         q_l_f[p:p + 64, (h // 2) * 64:(h // 2 + 1) * 64],
                                         k_l_f[p:p + 64, (h // 2) * 64:(h // 2 + 1) * 64],
                                         start=True, stop=True,
                                         tile_position=(p, 0)).then_inc(hT.h, 1)
                        em.t(_)
                        hT.n += 16

                        def _(s, w=hT.n):
                            s.wait_ge(hT.h, w)
                            import os as _os
                            if _os.environ.get("NOSTRIDE"):
                                s.activation(iS[:, 0:512], ps[0][0:64, :],
                                             AF.Exp, scale=1.0 / 1024.0)
                                s.drain()
                                s.activation(iS[:, 512:1024], ps[1][0:64, :],
                                             AF.Exp, scale=1.0 / 1024.0)
                            else:
                                iSv = iS[:, :].rearrange("p (a q m) -> p a q m", q=2, m=64)
                                s.activation(iSv[:, :, 0, :],
                                             ps[0][0:64, :].rearrange("p (a m) -> p a m", m=64),
                                             AF.Exp, scale=1.0 / 1024.0)
                                s.drain()
                                s.activation(iSv[:, :, 1, :],
                                             ps[1][0:64, :].rearrange("p (a m) -> p a m", m=64),
                                             AF.Exp, scale=1.0 / 1024.0)
                            s.drain().then_inc(sS.h, 1)
                        em.s(_)
                        sS.n += 1

                        def _(v, w=sS.n, we=dE.n):
                            v.wait_ge(sS.h, w)
                            v.wait_ge(dE.h, we)
                            v.reduce_sum(csum[:, 0:16],
                                         iS[:, :].rearrange("p (j m) -> p j m", m=64),
                                         axis=AX.X)
                            v.drain()
                            v.reciprocal(csum[:, 0:16], csum[:, 0:16])
                            v.drain()
                            for h in range(NH):
                                v.tensor_scalar(iS[:, h * 64:(h + 1) * 64],
                                                iS[:, h * 64:(h + 1) * 64],
                                                csum[:, h:h + 1], None, op0=ALU.mult)
                            v.drain().then_inc(vV.h, 1)
                        em.v(_)
                        vV.n += 1

                        def _(t, w=vV.n):
                            t.wait_ge(vV.h, w)
                            for h in range(NH):
                                g2, j = 2 + h // 8, h % 8
                                t.transpose(ps[g2][0:64, j * 64:(j + 1) * 64],
                                            iS[:, h * 64:(h + 1) * 64],
                                            idf[0:64, 0:64]).then_inc(hT.h, 1)
                        em.t(_)
                        hT.n += 16

                        def _(v, w=hT.n):
                            v.wait_ge(hT.h, w)
                            v.tensor_copy(iST[:, 0:512], ps[2][0:64, :])
                            v.tensor_copy(iST[:, 512:1024], ps[3][0:64, :])
                            v.drain()
                            v.reduce_sum(csum[:, 0:16],
                                         iST[:, :].rearrange("p (j m) -> p j m", m=64),
                                         axis=AX.X)
                            v.drain().then_inc(vV.h, 1)
                        em.v(_)
                        vV.n += 1

                        def _(t, w=vV.n):
                            t.wait_ge(vV.h, w)
                            t.transpose(ps[2][0:16, 0:64], csum[:, 0:16], idf[0:64, 0:64]
                                        ).then_inc(hT.h, 1)
                        em.t(_)
                        hT.n += 1

                        def _(v, w=hT.n):
                            v.wait_ge(hT.h, w)
                            v.reduce_max(rd[0:16, 0:1], ps[2][0:16, 0:64], axis=AX.X)
                            v.drain()
                            v.reciprocal(rd[0:16, 0:1], rd[0:16, 0:1])
                            v.drain().then_inc(vV.h, 1)
                        em.v(_)
                        vV.n += 1

                        def _(t, w=vV.n):
                            t.wait_ge(vV.h, w)
                            t.transpose(ps[3][0:1, 0:16], rd[0:16, 0:1], idf[0:16, 0:16]
                                        ).then_inc(hT.h, 1)
                        em.t(_)
                        hT.n += 1

                        def _(v, w=hT.n):
                            v.wait_ge(hT.h, w)
                            v.tensor_copy(rdT[0:1, 0:16], ps[3][0:1, 0:16])
                            v.drain().then_inc(vV.h, 1)
                        em.v(_)
                        vV.n += 1

                        def _(t, w=vV.n):
                            t.wait_ge(vV.h, w)
                            t.matmul(ps[2][0:64, 0:16], onesrow_f64[0:1, 0:64],
                                     rdT[0:1, 0:16], start=True, stop=True
                                     ).then_inc(hT.h, 1)
                        em.t(_)
                        hT.n += 1

                        def _(v, w=hT.n):
                            v.wait_ge(hT.h, w)
                            v.tensor_copy(denb[:, 0:16], ps[2][0:64, 0:16])
                            v.drain()
                            for h in range(NH):
                                v.tensor_scalar(iV[:, h * 64:(h + 1) * 64],
                                                iST[:, h * 64:(h + 1) * 64],
                                                denb[:, h:h + 1], None, op0=ALU.mult)
                                v.tensor_scalar(iVT[:, h * 64:(h + 1) * 64],
                                                iS[:, h * 64:(h + 1) * 64],
                                                denb[:, h:h + 1], None, op0=ALU.mult)
                            v.drain().then_inc(vV.h, 1)
                        em.v(_)
                        vV.n += 1

                        def istep_mm(t, dst0, lhs_t, rhs_t):
                            for h in range(NH):
                                g2, j = dst0 + h // 8, h % 8
                                t.matmul(ps[g2][0:64, j * 64:(j + 1) * 64],
                                         lhs_t[:, h * 64:(h + 1) * 64],
                                         rhs_t[:, h * 64:(h + 1) * 64],
                                         start=True, stop=True).then_inc(hT.h, 1)

                        def istep_tr(t, dst0, src_t):
                            for h in range(NH):
                                g2, j = dst0 + h // 8, h % 8
                                t.transpose(ps[g2][0:64, j * 64:(j + 1) * 64],
                                            src_t[:, h * 64:(h + 1) * 64],
                                            idf[0:64, 0:64]).then_inc(hT.h, 1)

                        for itn in range(_N_ITER):
                            def _(t, w=vV.n):
                                t.wait_ge(vV.h, w)
                                istep_mm(t, 2, iST, iV)
                                istep_mm(t, 4, iV, iST)
                            em.t(_)
                            hT.n += 32

                            def _(v, w=hT.n):
                                v.wait_ge(hT.h, w)
                                v.scalar_tensor_tensor(iT1[:, 0:512], ps[2][0:64, :], -1.0,
                                                       eye7w[0:64, :], op0=ALU.mult, op1=ALU.add)
                                v.scalar_tensor_tensor(iT1[:, 512:1024], ps[3][0:64, :], -1.0,
                                                       eye7w[0:64, :], op0=ALU.mult, op1=ALU.add)
                                v.tensor_copy(iKVT[:, 0:512], ps[4][0:64, :])
                                v.tensor_copy(iKVT[:, 512:1024], ps[5][0:64, :])
                                v.drain().then_inc(vV.h, 1)
                            em.v(_)
                            vV.n += 1

                            def _(t, w=vV.n):
                                t.wait_ge(vV.h, w)
                                istep_mm(t, 2, iKVT, iT1)
                            em.t(_)
                            hT.n += 16

                            def _(v, w=hT.n):
                                v.wait_ge(hT.h, w)
                                v.scalar_tensor_tensor(iT2[:, 0:512], ps[2][0:64, :], -1.0,
                                                       eye15w[0:64, :], op0=ALU.mult, op1=ALU.add)
                                v.scalar_tensor_tensor(iT2[:, 512:1024], ps[3][0:64, :], -1.0,
                                                       eye15w[0:64, :], op0=ALU.mult, op1=ALU.add)
                                v.drain().then_inc(vV.h, 1)
                            em.v(_)
                            vV.n += 1

                            def _(t, w=vV.n):
                                t.wait_ge(vV.h, w)
                                istep_mm(t, 4, iKVT, iT2)
                            em.t(_)
                            hT.n += 16

                            def _(v, w=hT.n):
                                v.wait_ge(hT.h, w)
                                v.scalar_tensor_tensor(iT1[:, 0:512], ps[4][0:64, :], -1.0,
                                                       eye13w[0:64, :], op0=ALU.mult, op1=ALU.add)
                                v.scalar_tensor_tensor(iT1[:, 512:1024], ps[5][0:64, :], -1.0,
                                                       eye13w[0:64, :], op0=ALU.mult, op1=ALU.add)
                                v.drain().then_inc(vV.h, 1)
                            em.v(_)
                            vV.n += 1

                            def _(t, w=vV.n):
                                t.wait_ge(vV.h, w)
                                istep_mm(t, 2, iVT, iT1)
                            em.t(_)
                            hT.n += 16

                            def _(v, w=hT.n):
                                v.wait_ge(hT.h, w)
                                v.tensor_scalar(iV[:, 0:512], ps[2][0:64, :], 0.25, None,
                                                op0=ALU.mult)
                                v.tensor_scalar(iV[:, 512:1024], ps[3][0:64, :], 0.25, None,
                                                op0=ALU.mult)
                                v.drain().then_inc(vV.h, 1)
                            em.v(_)
                            vV.n += 1

                            def _(t, w=vV.n):
                                t.wait_ge(vV.h, w)
                                istep_tr(t, 4, iV)
                            em.t(_)
                            hT.n += 16

                            def _(v, w=hT.n):
                                v.wait_ge(hT.h, w)
                                v.tensor_copy(iVT[:, 0:512], ps[4][0:64, :])
                                v.tensor_copy(iVT[:, 512:1024], ps[5][0:64, :])
                                v.drain().then_inc(vV.h, 1)
                            em.v(_)
                            vV.n += 1

                        for h in range(_N_EXPN):
                            p, cb = (h % 2) * 64, h // 2
                            half = (h % 2) * 1024
                            ubank = 3 + h // 8
                            bnT = hT.n
                            bnE = hE.n

                            pb = 0 if h % 2 == 0 else 4

                            def _(t, p=p, cb=cb, pb=pb, wv=vV.n, we=hE.n):
                                t.wait_ge(vV.h, wv)
                                t.wait_ge(hE.h, we)
                                for tc in range(16):
                                    ins = t.matmul(
                                        ps[pb + tc // 8][:, (tc % 8) * 64:(tc % 8 + 1) * 64],
                                        k_sb[p:p + 64, cb * T + tc * 128:cb * T + (tc + 1) * 128],
                                        q_l_b[p:p + 64, cb * 64:(cb + 1) * 64],
                                        start=True, stop=True, tile_position=(p, 0))
                                    if tc % 8 == 7:
                                        ins.then_inc(hT.h, 1)
                            em.t(_)
                            hT.n += 2

                            def _(s, bnT=bnT, half=half, pb=pb):
                                for g2 in range(2):
                                    s.wait_ge(hT.h, bnT + g2 + 1)
                                    s.activation(ebuf[:, half + g2 * 512:half + (g2 + 1) * 512],
                                                 ps[pb + g2][:, :],
                                                 AF.Exp, scale=1.0 / 32.0)
                                    s.drain().then_inc(hE.h, 1)
                            em.s(_)
                            hE.n += 2

                            def _(t, h=h, half=half, ubank=ubank, w=hE.n, wv=vV.n):
                                t.wait_ge(hE.h, w)
                                t.wait_ge(vV.h, wv)
                                acc_group(t, ps[2][0:64, h:h + 1],
                                          [(ebuf[:, half + tc * 64:half + (tc + 1) * 64],
                                            onescol_b[:, 0:1]) for tc in range(16)])
                                acc_group(t, ps[ubank][0:64, (h % 8) * 64:(h % 8 + 1) * 64],
                                          [(ebuf[:, half + tc * 64:half + (tc + 1) * 64],
                                            v_sb[:, tc * C + h * 64:tc * C + (h + 1) * 64])
                                           for tc in range(16)])
                            em.t(_)
                            hT.n += 2

                            def _(v, h=h, ubank=ubank, w=hT.n):
                                v.wait_ge(hT.h, w)
                                v.reciprocal(rN[:, h:h + 1], ps[2][0:64, h:h + 1])
                                v.drain()
                                v.tensor_scalar(U_all[:, h * 64:(h + 1) * 64],
                                                ps[ubank][0:64, (h % 8) * 64:(h % 8 + 1) * 64],
                                                rN[:, h:h + 1], None, op0=ALU.mult)
                                v.drain().then_inc(vV.h, 1)
                            em.v(_)
                            vV.n += 1

                        def _(t, w=vV.n):
                            t.wait_ge(vV.h, w)
                            for h in range(NH if _DO_PNV else 0):
                                p, cb = (h % 2) * 64, h // 2
                                t.matmul(ps[5][p:p + 64, cb * 64:(cb + 1) * 64],
                                         iVT[:, h * 64:(h + 1) * 64],
                                         U_all[:, h * 64:(h + 1) * 64],
                                         start=True, stop=True,
                                         tile_position=(0, p)).then_inc(hT.h, 1)
                        em.t(_)
                        hT.n += 16

                        def _(v, w=hT.n):
                            v.wait_ge(hT.h, w)
                            if _DO_PNV:
                                v.tensor_copy(PNv_b[:, :], ps[5][:, :])
                            v.drain().then_inc(vV.h, 1)
                        em.v(_)
                        vV.n += 1
                        end_fence(em)

                    if b == 0:
                        dbg_live.update(S_all=iS[:, :], V_all=iV[:, :], U_all=U_all[:, :],
                                        PNv=PNv_b[:, :])
                        if stop_block("a2b0"):
                            return nc, dbg_aps

                with contextlib.ExitStack() as a3:
                    E3 = a3.enter_context
                    y_sb = E3(nc.sbuf_tensor(f"a_y_{b}", [128, 16 * C], BF16))
                    prb = E3(nc.sbuf_tensor(f"a_pr0_{b}", [128, T], BF16))
                    ytT = E3(nc.sbuf_tensor(f"a_yt_{b}", [128, C], BF16))
                    xst = [E3(nc.sbuf_tensor(f"a_r{b}{i}", [128, C], F32)) for i in range(2)]
                    x2t = [E3(nc.sbuf_tensor(f"a_2{b}{i}", [128, C], F32)) for i in range(2)]

                    with nc.Block() as blk:
                        em = Emit(blk)
                        brl = dP.n
                        bye = hE.n

                        def _(q):
                            for fc in range(8):
                                if fc >= 1:
                                    q.wait_ge(hE.h, bye + 2 * fc)
                                q.dma_start(out=prb[:, :],
                                            in_=s_eL[fc * 128:(fc + 1) * 128, :]
                                            ).then_inc(dP.h, 16)
                        em.q(_)
                        dP.n += 8 * 16

                        byT = hT.n

                        def _(t):
                            u = 0
                            for fc in range(8):
                                t.wait_ge(dP.h, brl + 16 * (fc + 1))
                                for hh in range(2):
                                    h = 2 * fc + hh
                                    p, cb = (h % 2) * 64, h // 2
                                    t.wait_ge(hE.h, bye + max(0, u - 1))
                                    for tc in range(16):
                                        ins = t.matmul(
                                            ps[(u % 2) * 2 + tc // 8][:, (tc % 8) * 64:(tc % 8 + 1) * 64],
                                            prb[p:p + 64, tc * 128:(tc + 1) * 128],
                                            PNv_b[p:p + 64, cb * 64:(cb + 1) * 64],
                                            start=True, stop=True,
                                            tile_position=(p, 0))
                                        if tc == 15:
                                            ins.then_inc(hT.h, 1)
                                    u += 1
                        em.t(_)
                        hT.n += 16

                        def _(v):
                            u = 0
                            for fc in range(8):
                                for hh in range(2):
                                    h = 2 * fc + hh
                                    v.wait_ge(hT.h, byT + u + 1)
                                    for tc in range(16):
                                        v.tensor_scalar(
                                            y_sb[:, tc * C + h * 64:tc * C + (h + 1) * 64],
                                            ps[(u % 2) * 2 + tc // 8][:, (tc % 8) * 64:(tc % 8 + 1) * 64],
                                            recipL[:, h * 16 + tc:h * 16 + tc + 1],
                                            None, op0=ALU.mult)
                                    v.drain().then_inc(hE.h, 1)
                                    u += 1
                        em.v(_)
                        hE.n += 16
                        yq_done = hE.n

                        bX = [dX[0].n, dX[1].n]
                        boX = [oX[0].n, oX[1].n]
                        bcons = vV.n

                        def _(q, bX=bX):
                            for tc in range(2):
                                q.dma_start(out=xst[tc % 2][:, :],
                                            in_=ap["x"][t0g + tc * 128:t0g + (tc + 1) * 128, :]
                                            ).then_inc(dX[tc % 2].h, 16)
                        em.q(_)
                        dX[0].n += 16
                        dX[1].n += 16

                        for tc in range(16):
                            btr = hT.n
                            bev = sS.n

                            def _(t, tc=tc, bev=bev):
                                t.wait_ge(hE.h, yq_done)
                                for dc in range(8):
                                    t.wait_ge(sS.h, bev + max(0, dc - 1))
                                    t.transpose(psb[dc % 2][:, 0:128],
                                                y_sb[:, tc * C + dc * 128:tc * C + (dc + 1) * 128],
                                                idb[:, :]).then_inc(hT.h, 1)
                            em.t(_)
                            hT.n += 8

                            def _(s, btr=btr):
                                for dc in range(8):
                                    s.wait_ge(hT.h, btr + dc + 1)
                                    s.copy(ytT[:, dc * 128:(dc + 1) * 128],
                                           psb[dc % 2][:, 0:128])
                                    s.drain().then_inc(sS.h, 1)
                            em.s(_)
                            sS.n += 8

                            bpj = hT.n

                            def _(t, tc=tc, w=sS.n, ww=dW1.n):
                                t.wait_ge(sS.h, w)
                                t.wait_ge(dW1.h, ww)
                                if tc >= 2:
                                    t.wait_ge(vV.h, bcons + 3 * (tc - 2) + 1)
                                for n2 in range(2):
                                    acc_group(t, ps[4 + n2][:, :],
                                              [(ytT[:, dc * 128:(dc + 1) * 128],
                                                Wb[1][:, dc * C + n2 * 512:dc * C + (n2 + 1) * 512])
                                               for dc in range(8)])
                            em.t(_)
                            hT.n += 2

                            def _(v, tc=tc, bpj=bpj):
                                v.wait_ge(hT.h, bpj + 2)
                                v.wait_ge(dW1.h, dW1.n)
                                v.wait_ge(dX[tc % 2].h, bX[tc % 2] + 16 * (tc // 2 + 1))
                                if tc >= 2:
                                    v.wait_ge(oX[tc % 2].h, boX[tc % 2] + 16 * (tc // 2))
                                for n2 in range(2):
                                    v.tensor_add(x2t[tc % 2][:, n2 * 512:(n2 + 1) * 512],
                                                 ps[4 + n2][:, :],
                                                 xst[tc % 2][:, n2 * 512:(n2 + 1) * 512])
                                v.drain()
                                v.reduce_sum(lnst[:, 0:1], x2t[tc % 2][:, :], axis=AX.X)
                                v.drain()
                                v.tensor_scalar(lnst[:, 1:2], lnst[:, 0:1], -1.0 / C, None,
                                                op0=ALU.mult)
                                v.drain().then_inc(vV.h, 1)
                            em.v(_)
                            vV.n += 1

                            def _(s, tc=tc, w=vV.n):
                                s.wait_ge(vV.h, w)
                                s.activation(ytT[:, :], x2t[tc % 2][:, :], AF.Square,
                                             scale=1.0 / 32.0, accum_out=lnst[:, 3:4])
                                s.drain().then_inc(sS.h, 1)
                            em.s(_)
                            sS.n += 1

                            def _(v, w=sS.n):
                                v.wait_ge(sS.h, w)
                                v.tensor_scalar(lnst[:, 2:3], lnst[:, 1:2], -1.0, None,
                                                op0=ALU.mult)
                                v.drain()
                                v.scalar_tensor_tensor(lnst[:, 4:5], lnst[:, 2:3], lnst[:, 1:2],
                                                       lnst[:, 3:4], op0=ALU.mult, op1=ALU.add)
                                v.drain().then_inc(vV.h, 1)
                            em.v(_)
                            vV.n += 1

                            def _(s, w=vV.n):
                                s.wait_ge(vV.h, w)
                                s.activation(lnst[:, 5:6], lnst[:, 4:5], AF.Ln, bias=epsA[:, 0:1])
                                s.drain()
                                s.activation(lnst[:, 6:7], lnst[:, 5:6], AF.Exp, scale=-0.5)
                                s.drain()
                                s.activation(lnst[:, 5:6], lnst[:, 4:5], AF.Ln, scale=1.0 + EPS,
                                             bias=epsB[:, 0:1])
                                s.drain()
                                s.activation(lnst[:, 7:8], lnst[:, 5:6], AF.Exp, scale=-0.5)
                                s.drain().then_inc(sS.h, 1)
                            em.s(_)
                            sS.n += 1

                            def _(v, tc=tc, b=b, w=sS.n):
                                s4 = 4 * (b * 16 + tc)
                                v.wait_ge(sS.h, w)
                                v.tensor_scalar(stats[:, s4:s4 + 1], lnst[:, 6:7], 1.0, None,
                                                op0=ALU.add)
                                v.tensor_mul(stats[:, s4 + 1:s4 + 2], lnst[:, 1:2], lnst[:, 6:7])
                                v.tensor_copy(stats[:, s4 + 2:s4 + 3], lnst[:, 7:8])
                                v.tensor_mul(stats[:, s4 + 3:s4 + 4], lnst[:, 1:2], lnst[:, 7:8])
                                v.drain().then_inc(vV.h, 1)
                            em.v(_)
                            vV.n += 1

                            def _(q, tc=tc, w=vV.n):
                                q.wait_ge(vV.h, w)
                                q.dma_start(out=s_x2[t0g + tc * 128:t0g + (tc + 1) * 128, :],
                                            in_=x2t[tc % 2][:, :]).then_inc(oX[tc % 2].h, 16)
                                if tc + 2 < 16:
                                    q.wait_ge(vV.h, bcons + 3 * tc + 1)
                                    q.dma_start(out=xst[tc % 2][:, :],
                                                in_=ap["x"][t0g + (tc + 2) * 128:t0g + (tc + 3) * 128, :]
                                                ).then_inc(dX[tc % 2].h, 16)
                            em.q(_)
                            oX[tc % 2].n += 16
                            if tc + 2 < 16:
                                dX[tc % 2].n += 16
                        end_fence(em)

                    if b == 0:
                        dbg_live.update(y_sb=y_sb[:, :], stats=stats[:, :])
                        if stop_block("a3b0"):
                            return nc, dbg_aps

            if stop_block("attn"):
                return nc, dbg_aps

        with contextlib.ExitStack() as est:
            Ee = est.enter_context
            eyA = Ee(nc.sbuf_tensor("e_I", [128, 8 * C], BF16))
            mA = Ee(nc.sbuf_tensor("e_A", [128, 8 * C], BF16))
            mA2 = Ee(nc.sbuf_tensor("e_A2", [128, 8 * C], BF16))
            mA3 = Ee(nc.sbuf_tensor("e_A3", [128, 8 * C], BF16))
            mB = Ee(nc.sbuf_tensor("e_B", [128, 8 * C], BF16))
            mT = Ee(nc.sbuf_tensor("e_T", [128, 8 * C], BF16))
            mE1 = Ee(nc.sbuf_tensor("e_E1", [128, 8 * C], BF16))
            mE2 = Ee(nc.sbuf_tensor("e_E2", [128, 8 * C], BF16))
            mXT = Ee(nc.sbuf_tensor("e_XT", [128, 8 * C], BF16))
            cidf = Ee(nc.sbuf_tensor("e_cf", [128, 2048], F32))
            cidb = Ee(nc.sbuf_tensor("e_cb", [128, 13 * 128], BF16))

            with nc.Block() as blk:
                em = Emit(blk)

                def mm1024g(t, pair0, prs_fn, bE, bS):
                    u = 0
                    for mc in range(8):
                        for n2 in range(2):
                            if u == 0:
                                t.wait_ge(hE.h, bE)
                                t.wait_ge(sS.h, bS)
                            elif u >= 2:
                                if u - 2 < 8:
                                    t.wait_ge(hE.h, bE + u - 1)
                                else:
                                    t.wait_ge(sS.h, bS + u - 9)
                            acc_group(t, ps[pair0 + (u % 2)][:, :], prs_fn(mc, n2))
                            u += 1

                def mm1024(t, pair0, lhsT_sb, rhs_sb, bE, bS):
                    def prs(mc, n2):
                        return [(lhsT_sb[:, dc * C + mc * 128:dc * C + (mc + 1) * 128],
                                 rhs_sb[:, dc * C + n2 * 512:dc * C + (n2 + 1) * 512])
                                for dc in range(8)]
                    mm1024g(t, pair0, prs, bE, bS)

                def evac1024(v, dst_sb, pair0, bT, scale):
                    for u in range(8):
                        mc, n2 = u // 2, u % 2
                        v.wait_ge(hT.h, bT + u + 1)
                        v.tensor_scalar(dst_sb[:, mc * C + n2 * 512:mc * C + (n2 + 1) * 512],
                                        ps[pair0 + (u % 2)][:, :], scale, None,
                                        op0=ALU.mult)
                        v.drain().then_inc(hE.h, 1)

                def evac1024_s(s, dst_sb, pair0, bT, scale):
                    for u in range(8, 16):
                        mc, n2 = u // 2, u % 2
                        s.wait_ge(hT.h, bT + u + 1)
                        s.activation(dst_sb[:, mc * C + n2 * 512:mc * C + (n2 + 1) * 512],
                                     ps[pair0 + (u % 2)][:, :], AF.Copy, scale=scale)
                        s.drain().then_inc(sS.h, 1)

                def cid_b(k):
                    return cidb[:, k * 128:(k + 1) * 128]

                def chunk(x_sb, mc, n2):
                    return x_sb[:, mc * C + n2 * 512:mc * C + (n2 + 1) * 512]

                def _(g):
                    g.dma_start(out=eyA[:, :], in_=ap["eyeA"]).then_inc(dG.h, 16)
                    for hh in range(2):
                        g.dma_start(out=cidf[:, hh * 1024:(hh + 1) * 1024],
                                    in_=ap["cid0" if hh == 0 else "cid1"]
                                    ).then_inc(dG.h, 16)
                em.g(_)
                dG.n += 48

                def _(v, w=dG.n):
                    v.wait_ge(dG.h, w)
                    v.tensor_copy(cidb[:, :], cidf[:, 0:13 * 128])
                    v.drain().then_inc(vV.h, 1)
                em.v(_)
                vV.n += 1
                cid_ready = vV.n

                for gi in range(3):
                    def _(g, gi=gi, wt=hT.n, wd=dG.n):
                        g.wait_ge(dG.h, wd)
                        g.wait_ge(hT.h, wt)
                        for dc in range(8):
                            g.dma_start(out=mA[:, dc * C:(dc + 1) * C],
                                        in_=ap["gens"][gi * C + dc * 128:gi * C + (dc + 1) * 128, :]
                                        ).then_inc(dG.h, 16)
                    em.g(_)
                    dG.n += 128

                    b1, b2, b3 = hT.n, hE.n, sS.n

                    def _(t, w=dG.n, b2=b2, b3=b3):
                        t.wait_ge(dG.h, w)
                        t.wait_ge(vV.h, cid_ready)
                        mm1024(t, 0, mA, mA, b2, b3)
                    em.t(_)
                    hT.n += 16

                    def _(v, b1=b1):
                        evac1024(v, mA2, 0, b1, -1.0)
                    em.v(_)

                    def _(s, b1=b1):
                        evac1024_s(s, mA2, 0, b1, -1.0)
                    em.s(_)
                    hE.n += 8
                    sS.n += 8

                    b1, b2, b3 = hT.n, hE.n, sS.n

                    def _(t, b2=b2, b3=b3):
                        mm1024(t, 2, mA2, mA, b2, b3)
                    em.t(_)
                    hT.n += 16

                    def _(v, b1=b1):
                        evac1024(v, mA3, 2, b1, 1.0)
                    em.v(_)

                    def _(s, b1=b1):
                        evac1024_s(s, mA3, 2, b1, 1.0)
                    em.s(_)
                    hE.n += 8
                    sS.n += 8

                    b1, b2, b3 = hT.n, hE.n, sS.n

                    def _(t, b2=b2, b3=b3):
                        def prs(mc, n2):
                            return [(cid_b(12), chunk(mA3, mc, n2)),
                                    (cid_b(11), chunk(mA2, mc, n2)),
                                    (cid_b(10), chunk(mA, mc, n2)),
                                    (cid_b(9), chunk(eyA, mc, n2))]
                        mm1024g(t, 0, prs, b2, b3)
                    em.t(_)
                    hT.n += 16

                    def _(v, b1=b1):
                        evac1024(v, mB, 0, b1, 1.0)
                    em.v(_)

                    def _(s, b1=b1):
                        evac1024_s(s, mB, 0, b1, 1.0)
                    em.s(_)
                    hE.n += 8
                    sS.n += 8

                    for stepi, (kI, kA, kA2) in enumerate([(6, 7, 8), (3, 4, 5), (0, 1, 2)]):
                        rhs_t = mB if stepi == 0 else (mT if stepi == 1 else mB)
                        dst_t = mT if stepi == 0 else (mB if stepi == 1 else mT)
                        b1, b2, b3 = hT.n, hE.n, sS.n

                        def _(t, rhs_t=rhs_t, kI=kI, kA=kA, kA2=kA2, b2=b2, b3=b3):
                            def prs(mc, n2):
                                lst = [(mA3[:, dc * C + mc * 128:dc * C + (mc + 1) * 128],
                                        rhs_t[:, dc * C + n2 * 512:dc * C + (n2 + 1) * 512])
                                       for dc in range(8)]
                                lst.append((cid_b(kA2), chunk(mA2, mc, n2)))
                                lst.append((cid_b(kA), chunk(mA, mc, n2)))
                                lst.append((cid_b(kI), chunk(eyA, mc, n2)))
                                return lst
                            mm1024g(t, 0, prs, b2, b3)
                        em.t(_)
                        hT.n += 16

                        def _(s, dst_t=dst_t, b1=b1):
                            evac1024_s(s, dst_t, 0, b1, -1.0)
                        em.s(_)

                        def _(v, dst_t=dst_t, b1=b1):
                            evac1024(v, dst_t, 0, b1, -1.0)
                        em.v(_)
                        hE.n += 8
                        sS.n += 8

                    if gi < 2:
                        dst_e = mE1 if gi == 0 else mE2

                        def _(v, dst_e=dst_e, w2=hE.n, w3=sS.n):
                            v.wait_ge(hE.h, w2)
                            v.wait_ge(sS.h, w3)
                            for mc in range(8):
                                sl = slice(mc * C, (mc + 1) * C)
                                v.tensor_copy(dst_e[:, sl], mT[:, sl])
                            v.drain().then_inc(vV.h, 1)
                        em.v(_)
                        vV.n += 1
                    else:
                        btr, bte = hT.n, sS.n

                        def _(t, w=vV.n, w2=hE.n, bte=bte):
                            t.wait_ge(vV.h, w)
                            t.wait_ge(hE.h, w2)
                            for u in range(64):
                                mc, dc = u // 8, u % 8
                                t.wait_ge(sS.h, bte + max(0, u - 1))
                                t.transpose(psb[u % 2][:, 0:128],
                                            mE1[:, mc * C + dc * 128:mc * C + (dc + 1) * 128],
                                            idb[:, :]).then_inc(hT.h, 1)
                        em.t(_)
                        hT.n += 64

                        def _(s, btr=btr):
                            for u in range(64):
                                mc, dc = u // 8, u % 8
                                s.wait_ge(hT.h, btr + u + 1)
                                s.copy(mXT[:, dc * C + mc * 128:dc * C + (mc + 1) * 128],
                                       psb[u % 2][:, 0:128])
                                s.drain().then_inc(sS.h, 1)
                        em.s(_)
                        sS.n += 64

                        b1, b2, b3 = hT.n, hE.n, sS.n

                        def _(t, w=sS.n, b2=b2, b3=b3):
                            t.wait_ge(sS.h, w)
                            mm1024(t, 0, mXT, mE2, b2, b3)
                        em.t(_)
                        hT.n += 16

                        def _(s, b1=b1):
                            evac1024_s(s, mB, 0, b1, 1.0)
                        em.s(_)

                        def _(v, b1=b1):
                            evac1024(v, mB, 0, b1, 1.0)
                        em.v(_)
                        hE.n += 8
                        sS.n += 8

                        btr, bte = hT.n, sS.n

                        def _(t, w2=hE.n, bte=bte):
                            t.wait_ge(hE.h, w2)
                            for u in range(64):
                                mc, dc = u // 8, u % 8
                                t.wait_ge(sS.h, bte + max(0, u - 1))
                                t.transpose(psb[u % 2][:, 0:128],
                                            mB[:, mc * C + dc * 128:mc * C + (dc + 1) * 128],
                                            idb[:, :]).then_inc(hT.h, 1)
                        em.t(_)
                        hT.n += 64

                        def _(s, btr=btr):
                            for u in range(64):
                                mc, dc = u // 8, u % 8
                                s.wait_ge(hT.h, btr + u + 1)
                                s.copy(mXT[:, dc * C + mc * 128:dc * C + (mc + 1) * 128],
                                       psb[u % 2][:, 0:128])
                                s.drain().then_inc(sS.h, 1)
                        em.s(_)
                        sS.n += 64

                        b1, b2, b3 = hT.n, hE.n, sS.n

                        def _(t, w=sS.n, b2=b2, b3=b3):
                            t.wait_ge(sS.h, w)
                            mm1024(t, 0, mXT, mT, b2, b3)
                        em.t(_)
                        hT.n += 16

                        def _(s, b1=b1):
                            evac1024_s(s, R_bf, 0, b1, 1.0)
                        em.s(_)

                        def _(v, b1=b1):
                            evac1024(v, R_bf, 0, b1, 1.0)
                        em.v(_)
                        hE.n += 8
                        sS.n += 8

                end_fence(em)

            dbg_live.update(R=R_bf[:, :])
            if stop_block("expm"):
                return nc, dbg_aps

        with contextlib.ExitStack() as fst:
            Ef = fst.enter_context
            abw = Ef(nc.sbuf_tensor("f_aw", [128, 8 * C], BF16))
            gcw1b = Ef(nc.sbuf_tensor("c_g1", [128, C], BF16))
            abelbrow = Ef(nc.sbuf_tensor("c_abr", [1, C], BF16))
            gatebrow = Ef(nc.sbuf_tensor("c_gbr", [1, 8], BF16))
            gatew_t = Ef(nc.sbuf_tensor("c_gwt", [128, 64], BF16))
            xn2s = Ef(nc.sbuf_tensor("f_xn", [128, 8 * SLAB], BF16))
            x2st = [Ef(nc.sbuf_tensor(f"f_x{i}", [128, C], F32)) for i in range(2)]
            xcb = Ef(nc.sbuf_tensor("f_xc", [128, C], BF16))
            abt = [Ef(nc.sbuf_tensor(f"f_ab{i}", [128, C], BF16)) for i in range(2)]
            est_f = [Ef(nc.sbuf_tensor(f"f_ef{i}", [128, 2048], F32)) for i in range(2)]
            est_b = [Ef(nc.sbuf_tensor(f"f_eb{i}", [128, 2048], BF16)) for i in range(2)]
            prod = Ef(nc.sbuf_tensor("f_pr", [128, C], F32))
            glog = Ef(nc.sbuf_tensor("f_gl", [128, 8], F32))
            gw2 = Ef(nc.sbuf_tensor("f_g2", [128, 8], F32))
            gmsk = Ef(nc.sbuf_tensor("f_gm", [128, 8], F32))
            gw1s = Ef(nc.sbuf_tensor("f_g1s", [128, 64], F32))
            gw2ms = Ef(nc.sbuf_tensor("f_g2s", [128, 64], F32))
            gex = Ef(nc.sbuf_tensor("f_gx", [128, 48], F32))

            with nc.Block() as blk:
                em = Emit(blk)
                x2sp = [oX[0].n, oX[1].n]
                bXf = [dX[0].n, dX[1].n]
                bax = [oX[0].n, oX[1].n]

                def _(g):
                    for t_sb, srcn in [(gcw1b, "gcw1b"), (abelbrow, "abelbrow"),
                                       (gatebrow, "gatebrow"), (gatew_t, "gatew_t")]:
                        g.dma_start(out=t_sb[:, :], in_=ap[srcn]).then_inc(dG.h, 16)
                    for dc in range(8):
                        g.dma_start(out=abw[:, dc * C:(dc + 1) * C],
                                    in_=ap["abel_w"][dc * 128:(dc + 1) * 128, :]
                                    ).then_inc(dG.h, 16)
                em.g(_)
                dG.n += 128 + 64
                abw_ready = dG.n

                ew2v = ap["ew2"].rearrange("(a b) c -> a (b c)", b=2)
                s_ew2v = s_ew2b.rearrange("(a b) c -> a (b c)", b=2)
                bpd = [pd[0].n, pd[1].n]
                bpL = [pL.n, pLv.n]

                def _(g, bpd=bpd, bpL=bpL):
                    for i in range(130):
                        p2 = i % 2
                        if i >= 2:
                            j = i - 2
                            jp = j % 2
                            g.wait_ge((pL if jp == 0 else pLv).h,
                                      bpL[jp] + j // 2 + 1)
                            if j < 64:
                                g.dma_start(out=s_ew1b[j * 128:(j + 1) * 128, :],
                                            in_=est_b[jp][:, :]).then_inc(pd[jp].h, 16)
                            else:
                                jj = j - 64
                                g.dma_start(out=s_ew2v[jj * 128:(jj + 1) * 128, :],
                                            in_=est_b[jp][:, :]).then_inc(pd[jp].h, 16)
                        if i < 64:
                            g.dma_start(out=est_f[p2][:, :],
                                        in_=ap["ew1"][i * 128:(i + 1) * 128, :]
                                        ).then_inc(pd[p2].h, 16)
                        elif i < 128:
                            ii = i - 64
                            g.dma_start(out=est_f[p2][:, :],
                                        in_=ew2v[ii * 128:(ii + 1) * 128, :]
                                        ).then_inc(pd[p2].h, 16)
                em.g(_)
                for i in range(130):
                    p2 = i % 2
                    if i >= 2:
                        pd[(i - 2) % 2].n += 16
                    if i < 128:
                        pd[p2].n += 16

                def emit_casts(j0, j1, bpd=bpd):
                    evens = [j for j in range(j0, j1) if j % 2 == 0]
                    odds = [j for j in range(j0, j1) if j % 2 == 1]

                    def _(s, js=tuple(evens)):
                        for j in js:
                            s.wait_ge(pd[0].h, bpd[0] + 32 * (j // 2) + 16)
                            s.copy(est_b[0][:, :], est_f[0][:, :])
                            s.drain().then_inc(pL.h, 1)
                    em.s(_)
                    pL.n += len(evens)

                    def _(v, js=tuple(odds)):
                        for j in js:
                            v.wait_ge(pd[1].h, bpd[1] + 32 * (j // 2) + 16)
                            v.tensor_copy(est_b[1][:, :], est_f[1][:, :])
                            v.drain().then_inc(pLv.h, 1)
                    em.v(_)
                    pLv.n += len(odds)

                ps2free = {-1: 0}
                for si in range(NSLAB):
                    for j in range(8):
                        bi = si * 8 + j
                        tg = bi * 128

                        def _(q, bi=bi, tg=tg, wv=vV.n, x2sp=x2sp):
                            q.wait_ge(oX[0].h, x2sp[0])
                            q.wait_ge(oX[1].h, x2sp[1])
                            if bi >= 2:
                                q.wait_ge(vV.h, wv)
                            q.dma_start(out=x2st[bi % 2][:, :],
                                        in_=s_x2[tg:tg + 128, :]).then_inc(dX[bi % 2].h, 16)
                        em.q(_)
                        dX[bi % 2].n += 16

                        def _(v, bi=bi, w=dX[bi % 2].n):
                            v.wait_ge(dX[bi % 2].h, w)
                            s4 = 4 * bi
                            v.tensor_scalar(xcb[:, :], x2st[bi % 2][:, :],
                                            stats[:, s4 + 2:s4 + 3], stats[:, s4 + 3:s4 + 4],
                                            op0=ALU.mult, op1=ALU.add)
                            v.drain().then_inc(vV.h, 1)
                        em.v(_)
                        vV.n += 1

                        btr, bev = hT.n, vV.n

                        def _(t, w=vV.n, bev=bev):
                            t.wait_ge(vV.h, w)
                            for dc in range(8):
                                t.wait_ge(vV.h, bev + max(0, dc - 1))
                                t.transpose(psb[dc % 2][:, 0:128],
                                            xcb[:, dc * 128:(dc + 1) * 128],
                                            idb[:, :]).then_inc(hT.h, 1)
                        em.t(_)
                        hT.n += 8

                        def _(v, j=j, btr=btr):
                            for dc in range(8):
                                v.wait_ge(hT.h, btr + dc + 1)
                                v.tensor_scalar(xn2s[:, dc * SLAB + j * 128:dc * SLAB + (j + 1) * 128],
                                                psb[dc % 2][:, 0:128],
                                                gamma_t[:, dc:dc + 1], beta_t[:, dc:dc + 1],
                                                op0=ALU.mult, op1=ALU.add)
                                v.drain().then_inc(vV.h, 1)
                        em.v(_)
                        vV.n += 8

                    for j in range(8):
                        bi = si * 8 + j
                        bab = hT.n

                        def _(t, j=j, ws=vV.n, whe=hE.n, wvv=ps2free[si * 8 + j - 1]):
                            t.wait_ge(vV.h, ws)
                            t.wait_ge(dG.h, abw_ready)
                            t.wait_ge(hE.h, whe)
                            t.wait_ge(vV.h, wvv)
                            for n2 in range(2):
                                prs = [(xn2s[:, dc * SLAB + j * 128:dc * SLAB + (j + 1) * 128],
                                        abw[:, dc * C + n2 * 512:dc * C + (n2 + 1) * 512])
                                       for dc in range(8)]
                                prs.append((onesrow_b[0:1, 0:128],
                                            abelbrow[0:1, n2 * 512:(n2 + 1) * 512]))
                                acc_group(t, ps[n2][:, :], prs)
                            gprs = [(xn2s[:, dc * SLAB + j * 128:dc * SLAB + (j + 1) * 128],
                                     gatew_t[:, dc * 8:(dc + 1) * 8]) for dc in range(8)]
                            gprs.append((onesrow_b[0:1, 0:128], gatebrow[0:1, 0:8]))
                            acc_group(t, ps[2][:, 0:8], gprs)
                        em.t(_)
                        hT.n += 3

                        def _(s, j=j, bab=bab, wo=oX[j % 2].n):
                            s.wait_ge(oX[j % 2].h, wo)
                            for n2 in range(2):
                                s.wait_ge(hT.h, bab + n2 + 1)
                                s.activation(abt[j % 2][:, n2 * 512:(n2 + 1) * 512],
                                             ps[n2][:, :], AF.Gelu)
                                s.drain().then_inc(hE.h, 1)
                        em.s(_)
                        hE.n += 2

                        ps2free[bi] = vV.n + 1

                        def _(v, j=j, bi=bi, bab=bab, w=hE.n):
                            v.wait_ge(hT.h, bab + 3)
                            v.tensor_copy(glog[:, 0:8], ps[2][:, 0:8])
                            v.drain().then_inc(vV.h, 1)
                            v.wait_ge(hE.h, w)
                            v.tensor_mul(prod[:, :], abt[j % 2][:, :], gcw1b[:, :])
                            v.drain()
                            v.reduce_sum(gdot1[:, bi:bi + 1], prod[:, :], axis=AX.X)
                            v.drain().then_inc(vV.h, 1)
                            v.reduce_max(gex[:, j:j + 1], glog[:, 0:8], axis=AX.X)
                            v.drain()
                            v.tensor_scalar(gw1s[:, j * 8:(j + 1) * 8], glog[:, 0:8],
                                            gex[:, j:j + 1], None, op0=ALU.is_ge)
                            v.drain()
                            v.scalar_tensor_tensor(gw2[:, 0:8], gw1s[:, j * 8:(j + 1) * 8],
                                                   -1e30, glog[:, 0:8],
                                                   op0=ALU.mult, op1=ALU.add)
                            v.drain()
                            v.reduce_max(gex[:, 8 + j:9 + j], gw2[:, 0:8], axis=AX.X)
                            v.drain()
                            v.tensor_scalar(gw2ms[:, j * 8:(j + 1) * 8], gw2[:, 0:8],
                                            gex[:, 8 + j:9 + j], None, op0=ALU.is_ge)
                            v.drain().then_inc(vV.h, 1)
                        em.v(_)
                        vV.n += 3

                        emit_casts(bi * 4, bi * 4 + 4)

                        def _(q, bi=bi, j=j, w=vV.n):
                            q.wait_ge(vV.h, w - 1)
                            q.dma_start(out=s_abel[bi * 128:(bi + 1) * 128, :],
                                        in_=abt[j % 2][:, :]).then_inc(oX[j % 2].h, 16)
                        em.q(_)
                        oX[j % 2].n += 16

                    def _(v, w=vV.n):
                        v.wait_ge(vV.h, w)
                        v.tensor_tensor(gex[:, 16:24], gex[:, 8:16], gex[:, 0:8],
                                        op=ALU.subtract)
                        v.drain().then_inc(vV.h, 1)
                    em.v(_)
                    vV.n += 1

                    def _(s, w=vV.n):
                        s.wait_ge(vV.h, w)
                        s.activation(gex[:, 24:32], gex[:, 16:24], AF.Exp)
                        s.drain().then_inc(sS.h, 1)
                    em.s(_)
                    sS.n += 1

                    def _(v, si=si, w=sS.n):
                        v.wait_ge(sS.h, w)
                        v.tensor_scalar(gex[:, 32:40], gex[:, 24:32], 1.0, None, op0=ALU.add)
                        v.drain()
                        v.reciprocal(gex[:, 32:40], gex[:, 32:40])
                        v.drain()
                        v.tensor_scalar(gex[:, 40:48], gex[:, 32:40], -1.0, 1.0,
                                        op0=ALU.mult, op1=ALU.add)
                        v.drain()
                        for j in range(8):
                            bi = si * 8 + j
                            v.tensor_scalar(gmsk[:, 0:8], gw2ms[:, j * 8:(j + 1) * 8],
                                            gex[:, 40 + j:41 + j], None, op0=ALU.mult)
                            v.drain()
                            v.scalar_tensor_tensor(wd_sb[:, bi * 8:(bi + 1) * 8],
                                                   gw1s[:, j * 8:(j + 1) * 8],
                                                   gex[:, 32 + j:33 + j], gmsk[:, 0:8],
                                                   op0=ALU.mult, op1=ALU.add)
                            v.drain()
                        v.drain().then_inc(vV.h, 1)
                    em.v(_)
                    vV.n += 1

                end_fence(em)

            dbg_live.update(wd=wd_sb[:, :], gd1=gdot1[:, :], xn2s=xn2s[:, :])
            if stop_block("fa"):
                return nc, dbg_aps

        with contextlib.ExitStack() as fbt:
            Eb = fbt.enter_context
            ew1s = Eb(nc.sbuf_tensor("b_w1", [128, 8 * HD], BF16))
            eb2row = Eb(nc.sbuf_tensor("c_eb2", [1, C], BF16))
            ew2s = Eb(nc.sbuf_tensor("b_w2", [128, 16 * C], BF16))
            xn2s = Eb(nc.sbuf_tensor("b_xn", [128, 8 * SLAB], BF16))
            h_sb = Eb(nc.sbuf_tensor("b_h", [128, 16 * SLAB], BF16))
            acc = Eb(nc.sbuf_tensor("b_acc", [128, 8 * C], F32))
            x2st = [Eb(nc.sbuf_tensor(f"b_x{i}", [128, C], F32)) for i in range(2)]
            abr = [Eb(nc.sbuf_tensor(f"b_ar{i}", [128, C], BF16)) for i in range(2)]
            xcb = Eb(nc.sbuf_tensor("b_xc", [128, C], BF16))
            accT = Eb(nc.sbuf_tensor("b_aT", [128, C], BF16))
            accB = Eb(nc.sbuf_tensor("b_aB", [128, C], BF16))
            otile = [Eb(nc.sbuf_tensor(f"b_o{i}", [128, C], F32)) for i in range(2)]
            prod = Eb(nc.sbuf_tensor("b_pr", [128, 512], F32))

            with nc.Block() as blk:
                em = Emit(blk)
                spx = [oX[0].n, oX[1].n]

                def emit_expert_loads(e, gate_hT):
                    def _(q, e=e, gate_hT=gate_hT):
                        q.wait_ge(hT.h, gate_hT)
                        for dc in range(8):
                            q.dma_start(out=ew1s[:, dc * HD:(dc + 1) * HD],
                                        in_=s_ew1b[e * C + dc * 128:e * C + (dc + 1) * 128, :]
                                        ).then_inc(eW[0].h, 16)
                        for hc in range(16):
                            q.dma_start(out=ew2s[:, hc * C:(hc + 1) * C],
                                        in_=s_ew2b[e * HD + hc * 128:e * HD + (hc + 1) * 128, :]
                                        ).then_inc(eW[1].h, 16)
                    em.q(_)
                    eW[0].n += 128
                    eW[1].n += 256

                def _(g):
                    g.dma_start(out=eb2row[:, :], in_=ap["eb2row"][0:1, 0:C]
                                ).then_inc(dG.h, 16)
                em.g(_)
                dG.n += 16

                emit_expert_loads(0, 0)

                for si in range(NSLAB):
                    for j in range(8):
                        bi = si * 8 + j
                        tg = bi * 128

                        def _(q, bi=bi, tg=tg, wv=vV.n, spx=spx):
                            q.wait_ge(oX[0].h, spx[0])
                            q.wait_ge(oX[1].h, spx[1])
                            if bi >= 2:
                                q.wait_ge(vV.h, wv)
                            q.dma_start(out=x2st[bi % 2][:, :],
                                        in_=s_x2[tg:tg + 128, :]).then_inc(dX[bi % 2].h, 16)
                        em.q(_)
                        dX[bi % 2].n += 16

                        def _(v, bi=bi, w=dX[bi % 2].n):
                            v.wait_ge(dX[bi % 2].h, w)
                            s4 = 4 * bi
                            v.tensor_scalar(xcb[:, :], x2st[bi % 2][:, :],
                                            stats[:, s4 + 2:s4 + 3], stats[:, s4 + 3:s4 + 4],
                                            op0=ALU.mult, op1=ALU.add)
                            v.drain().then_inc(vV.h, 1)
                        em.v(_)
                        vV.n += 1

                        btr, bev = hT.n, vV.n

                        def _(t, w=vV.n, bev=bev):
                            t.wait_ge(vV.h, w)
                            for dc in range(8):
                                t.wait_ge(vV.h, bev + max(0, dc - 1))
                                t.transpose(psb[dc % 2][:, 0:128],
                                            xcb[:, dc * 128:(dc + 1) * 128],
                                            idb[:, :]).then_inc(hT.h, 1)
                        em.t(_)
                        hT.n += 8

                        def _(v, j=j, btr=btr):
                            for dc in range(8):
                                v.wait_ge(hT.h, btr + dc + 1)
                                v.tensor_scalar(xn2s[:, dc * SLAB + j * 128:dc * SLAB + (j + 1) * 128],
                                                psb[dc % 2][:, 0:128],
                                                gamma_t[:, dc:dc + 1], beta_t[:, dc:dc + 1],
                                                op0=ALU.mult, op1=ALU.add)
                                v.drain().then_inc(vV.h, 1)
                        em.v(_)
                        vV.n += 8

                    xn2_ready = vV.n

                    for e in range(NE):
                        bhT, bhE = hT.n, hE.n

                        def _(t, w1=eW[0].n, bhE=bhE, xn2_ready=xn2_ready, wv=vV.n):
                            t.wait_ge(vV.h, max(xn2_ready, wv))
                            t.wait_ge(eW[0].h, w1)
                            for hc in range(16):
                                pb = 0 if hc % 2 == 0 else 2
                                t.wait_ge(hE.h, bhE + max(0, hc - 1))
                                for tb2 in range(2):
                                    for dc in range(8):
                                        ins = t.matmul(
                                            ps[pb + tb2][:, :],
                                            ew1s[:, dc * HD + hc * 128:dc * HD + (hc + 1) * 128],
                                            xn2s[:, dc * SLAB + tb2 * 512:dc * SLAB + (tb2 + 1) * 512],
                                            start=(dc == 0), stop=(dc == 7))
                                        if dc == 7 and tb2 == 1:
                                            ins.then_inc(hT.h, 1)
                        em.t(_)
                        hT.n += 16
                        h_mm_done = hT.n

                        def _(s, e=e, bhT=bhT):
                            for hc in range(16):
                                pb = 0 if hc % 2 == 0 else 2
                                s.wait_ge(hT.h, bhT + hc + 1)
                                for tb2 in range(2):
                                    s.activation(h_sb[:, hc * SLAB + tb2 * 512:hc * SLAB + (tb2 + 1) * 512],
                                                 ps[pb + tb2][:, :], AF.Gelu,
                                                 bias=eb1t[:, e * 16 + hc:e * 16 + hc + 1])
                                s.drain().then_inc(hE.h, 1)
                        em.s(_)
                        hE.n += 16
                        h_done = hE.n

                        nxt = (e + 1, si) if e < NE - 1 else (0, si + 1)
                        if nxt[1] < NSLAB:
                            def _(q, ne=nxt[0], gate=h_mm_done):
                                q.wait_ge(hT.h, gate)
                                for dc in range(8):
                                    q.dma_start(out=ew1s[:, dc * HD:(dc + 1) * HD],
                                                in_=s_ew1b[ne * C + dc * 128:ne * C + (dc + 1) * 128, :]
                                                ).then_inc(eW[0].h, 16)
                            em.q(_)
                            eW[0].n += 128

                        beT, beE = hT.n, vV.n

                        def _(t, e=e, h_done=h_done, beE=beE, w2=eW[1].n, wg=dG.n):
                            t.wait_ge(hE.h, h_done)
                            t.wait_ge(eW[1].h, w2)
                            t.wait_ge(dG.h, wg)
                            for tc in range(8):
                                pb = 4 if tc % 2 == 0 else 2
                                t.wait_ge(vV.h, beE + max(0, tc - 1))
                                for n2 in range(2):
                                    for hc in range(16):
                                        t.matmul(
                                            ps[pb + n2][:, :],
                                            h_sb[:, hc * SLAB + tc * 128:hc * SLAB + (tc + 1) * 128],
                                            ew2s[:, hc * C + n2 * 512:hc * C + (n2 + 1) * 512],
                                            start=(hc == 0), stop=False)
                                    ins = t.matmul(ps[pb + n2][:, :], onesrow_b[0:1, 0:128],
                                                   eb2row[0:1, n2 * 512:(n2 + 1) * 512],
                                                   start=False, stop=True)
                                    if n2 == 1:
                                        ins.then_inc(hT.h, 1)
                        em.t(_)
                        hT.n += 8
                        eo_mm_done = hT.n

                        def _(v, e=e, si=si, beT=beT):
                            for tc in range(8):
                                pb = 4 if tc % 2 == 0 else 2
                                v.wait_ge(hT.h, beT + tc + 1)
                                for n2 in range(2):
                                    asl = acc[:, tc * C + n2 * 512:tc * C + (n2 + 1) * 512]
                                    wsl = wd_sb[:, (si * 8 + tc) * 8 + e:(si * 8 + tc) * 8 + e + 1]
                                    if e == 0:
                                        v.tensor_scalar(asl, ps[pb + n2][:, :], wsl, None,
                                                        op0=ALU.mult)
                                    else:
                                        v.scalar_tensor_tensor(asl, ps[pb + n2][:, :], wsl,
                                                               asl, op0=ALU.mult, op1=ALU.add)
                                v.drain().then_inc(vV.h, 1)
                        em.v(_)
                        vV.n += 8

                        if nxt[1] < NSLAB:
                            def _(g, ne=nxt[0], gate=eo_mm_done):
                                g.wait_ge(hT.h, gate)
                                g.dma_start(out=eb2row[:, :],
                                            in_=ap["eb2row"][0:1, ne * C:(ne + 1) * C]
                                            ).then_inc(dG.h, 16)
                            em.g(_)
                            dG.n += 16

                            def _(q, ne=nxt[0], gate=eo_mm_done):
                                q.wait_ge(hT.h, gate)
                                for hc in range(16):
                                    q.dma_start(out=ew2s[:, hc * C:(hc + 1) * C],
                                                in_=s_ew2b[ne * HD + hc * 128:ne * HD + (hc + 1) * 128, :]
                                                ).then_inc(eW[1].h, 16)
                            em.q(_)
                            eW[1].n += 256

                    brm = vV.n

                    for tc in range(8):
                        bi = si * 8 + tc
                        tg = bi * 128
                        rp = 4 if tc % 2 == 0 else 2
                        btr, bev = hT.n, sS.n

                        def _(q, tc=tc, tg=tg):
                            q.wait_ge(vV.h, brm if tc < 2 else brm + 3 * (tc - 2) + 3)
                            q.dma_start(out=x2st[tc % 2][:, :],
                                        in_=s_x2[tg:tg + 128, :]).then_inc(dX[tc % 2].h, 16)
                            q.dma_start(out=abr[tc % 2][:, :],
                                        in_=s_abel[tg:tg + 128, :]).then_inc(dX[tc % 2].h, 16)
                        em.q(_)
                        dX[tc % 2].n += 32

                        def _(v, tc=tc, w=vV.n if tc == 0 else brm, wt=hT.n):
                            v.wait_ge(vV.h, w)
                            v.wait_ge(hT.h, wt)
                            v.tensor_copy(accB[:, :], acc[:, tc * C:(tc + 1) * C])
                            v.drain().then_inc(vV.h, 1)
                        em.v(_)
                        vV.n += 1

                        def _(t, tc=tc, w=vV.n, bev=bev):
                            t.wait_ge(vV.h, w)
                            for dc in range(8):
                                t.wait_ge(sS.h, bev + max(0, dc - 1))
                                t.transpose(psb[dc % 2][:, 0:128],
                                            accB[:, dc * 128:(dc + 1) * 128],
                                            idb[:, :]).then_inc(hT.h, 1)
                        em.t(_)
                        hT.n += 8

                        def _(s, btr=btr):
                            for dc in range(8):
                                s.wait_ge(hT.h, btr + dc + 1)
                                s.copy(accT[:, dc * 128:(dc + 1) * 128],
                                       psb[dc % 2][:, 0:128])
                                s.drain().then_inc(sS.h, 1)
                        em.s(_)
                        sS.n += 8

                        brt = hT.n

                        def _(t, tc=tc, rp=rp, w=sS.n):
                            t.wait_ge(sS.h, w)
                            if tc >= 2:
                                t.wait_ge(vV.h, brm + 3 * (tc - 2) + 3)
                            for n2 in range(2):
                                acc_group(t, ps[rp + n2][:, :],
                                          [(accT[:, dc * 128:(dc + 1) * 128],
                                            R_bf[:, dc * C + n2 * 512:dc * C + (n2 + 1) * 512])
                                           for dc in range(8)])
                        em.t(_)
                        hT.n += 2

                        def _(v, bi=bi, rp=rp, brt=brt, w=dX[tc % 2].n, tc=tc):
                            v.wait_ge(hT.h, brt + 2)
                            v.wait_ge(dX[tc % 2].h, w)
                            v.tensor_tensor(prod[:, :], ps[rp][:, :], gcw2b[:, 0:512],
                                            op=ALU.mult)
                            v.drain()
                            v.reduce_sum(lnst[:, 0:1], prod[:, :], axis=AX.X)
                            v.drain()
                            v.tensor_tensor(prod[:, :], ps[rp + 1][:, :], gcw2b[:, 512:1024],
                                            op=ALU.mult)
                            v.drain()
                            v.reduce_sum(lnst[:, 4:5], prod[:, :], axis=AX.X)
                            v.drain()
                            v.tensor_tensor(lnst[:, 1:2], lnst[:, 0:1], lnst[:, 4:5],
                                            op=ALU.add)
                            v.drain()
                            v.tensor_tensor(lnst[:, 1:2], lnst[:, 1:2], gdot1[:, bi:bi + 1],
                                            op=ALU.add)
                            v.drain().then_inc(vV.h, 1)
                        em.v(_)
                        vV.n += 1

                        def _(s, w=vV.n):
                            s.wait_ge(vV.h, w)
                            s.activation(lnst[:, 2:3], lnst[:, 1:2], AF.Sigmoid,
                                         bias=gcbb[:, 0:1])
                            s.drain().then_inc(sS.h, 1)
                        em.s(_)
                        sS.n += 1

                        def _(v, bi=bi, tc=tc, rp=rp, w=sS.n, wo=oX[tc % 2].n):
                            v.wait_ge(sS.h, w)
                            v.wait_ge(oX[tc % 2].h, wo)
                            v.tensor_scalar(lnst[:, 3:4], lnst[:, 2:3], -1.0, 1.0,
                                            op0=ALU.mult, op1=ALU.add)
                            s4 = 4 * bi
                            v.tensor_scalar(otile[tc % 2][:, :], x2st[tc % 2][:, :],
                                            stats[:, s4:s4 + 1], stats[:, s4 + 1:s4 + 2],
                                            op0=ALU.mult, op1=ALU.add)
                            v.drain()
                            v.scalar_tensor_tensor(otile[tc % 2][:, :], abr[tc % 2][:, :],
                                                   lnst[:, 3:4], otile[tc % 2][:, :],
                                                   op0=ALU.mult, op1=ALU.add)
                            v.drain()
                            for n2 in range(2):
                                v.scalar_tensor_tensor(otile[tc % 2][:, n2 * 512:(n2 + 1) * 512],
                                                       ps[rp + n2][:, :], lnst[:, 2:3],
                                                       otile[tc % 2][:, n2 * 512:(n2 + 1) * 512],
                                                       op0=ALU.mult, op1=ALU.add)
                            v.drain().then_inc(vV.h, 1)
                        em.v(_)
                        vV.n += 1

                        def _(q, tc=tc, tg=tg, w=vV.n):
                            q.wait_ge(vV.h, w)
                            q.dma_start(out=out_ap[tg:tg + 128, :],
                                        in_=otile[tc % 2][:, :]).then_inc(oX[tc % 2].h, 16)
                        em.q(_)
                        oX[tc % 2].n += 16

                end_fence(em)

            dbg_live.update(acc=acc[:, :], hsb=h_sb[:, :])
            stop_block(None)

    return nc, dbg_aps


def _cid_const():
    import math
    cid = np.zeros((128, 2048), np.float32)
    for k in range(13):
        s = 1.0 / math.factorial(k)
        if k <= 8:
            s = -s
        cid[:, k * 128:(k + 1) * 128] = s * np.eye(128, dtype=np.float32)
    return cid


def _consts():
    import ml_dtypes
    i64 = np.eye(64, dtype=np.float32)
    eyeA = np.zeros((128, 8 * C), np.float32)
    for dc in range(8):
        eyeA[:, dc * C + dc * 128:dc * C + (dc + 1) * 128] = np.eye(128)
    bc2 = np.zeros((2, 128), np.float32)
    bc2[0, 0:64] = 1.0
    bc2[1, 64:128] = 1.0
    d = {
        "idb": np.eye(128, dtype=np.float32).astype(ml_dtypes.bfloat16),
        "idf": np.eye(128, dtype=np.float32),
        "onescol_b": np.ones((128, 1), np.float32).astype(ml_dtypes.bfloat16),
        "onescol_f": np.ones((128, 1), np.float32),
        "onesrow_b": np.ones((1, 128), np.float32).astype(ml_dtypes.bfloat16),
        "onesrow_f": np.ones((1, 64), np.float32),
        "eye7w": np.tile(7.0 * i64, (2, 8)).astype(np.float32),
        "eye15w": np.tile(15.0 * i64, (2, 8)).astype(np.float32),
        "eye13w": np.tile(13.0 * i64, (2, 8)).astype(np.float32),
        "eyeA": eyeA.astype(ml_dtypes.bfloat16),
        "cid0": _cid_const()[:, 0:1024].copy(),
        "cid1": _cid_const()[:, 1024:2048].copy(),
        "bc2": bc2,
        "epsA": np.full((128, 1), 1e-5, np.float32),
        "epsB": np.full((128, 1), 1e-10, np.float32),
    }
    return d


def kernel(L0_attn_x, L0_attn_W_attn, L0_attn_W_proj, L0_ffn_ln_gamma,
           L0_ffn_ln_beta, L0_ffn_abel_w, L0_ffn_abel_b, L0_ffn_moe_gate_w,
           L0_ffn_moe_gate_b, L0_ffn_ew1, L0_ffn_eb1, L0_ffn_ew2, L0_ffn_eb2,
           L0_ffn_gens, L0_ffn_gc_w, L0_ffn_gc_b):
    x = np.ascontiguousarray(np.asarray(L0_attn_x, np.float32))
    consts = _consts()
    g = np.asarray(L0_ffn_ln_gamma, np.float32)
    be = np.asarray(L0_ffn_ln_beta, np.float32)
    gamma_t = np.ascontiguousarray(g.reshape(8, 128).T)
    beta_t = np.ascontiguousarray(be.reshape(8, 128).T)
    eb1 = np.asarray(L0_ffn_eb1, np.float32)
    eb1t = np.ascontiguousarray(
        eb1.reshape(8, 16, 128).transpose(2, 0, 1).reshape(128, 128))
    eb2row = np.ascontiguousarray(np.asarray(L0_ffn_eb2, np.float32).reshape(1, NE * C))
    abelbrow = np.ascontiguousarray(np.asarray(L0_ffn_abel_b, np.float32).reshape(1, C))
    gatebrow = np.ascontiguousarray(
        np.asarray(L0_ffn_moe_gate_b, np.float32).reshape(1, 8))
    gw = np.asarray(L0_ffn_moe_gate_w, np.float32)
    gatew_t = np.ascontiguousarray(
        gw.reshape(8, 128, 8).transpose(1, 0, 2).reshape(128, 64))
    gcw = np.asarray(L0_ffn_gc_w, np.float32)
    gcw1b = np.ascontiguousarray(np.broadcast_to(gcw[:C, 0][None, :], (128, C)))
    gcw2b = np.ascontiguousarray(np.broadcast_to(gcw[C:, 0][None, :], (128, C)))
    gcbb = np.full((128, 1), float(np.asarray(L0_ffn_gc_b).reshape(())), np.float32)

    shared = dict(consts)
    shared.update(
        W_attn=np.ascontiguousarray(np.asarray(L0_attn_W_attn, np.float32)),
        W_proj=np.ascontiguousarray(np.asarray(L0_attn_W_proj, np.float32)),
        abel_w=np.ascontiguousarray(np.asarray(L0_ffn_abel_w, np.float32)),
        ew1=np.ascontiguousarray(np.asarray(L0_ffn_ew1, np.float32).reshape(NE * C, HD)),
        ew2=np.ascontiguousarray(np.asarray(L0_ffn_ew2, np.float32).reshape(NE * HD, C)),
        gens=np.ascontiguousarray(np.asarray(L0_ffn_gens, np.float32).reshape(3 * C, C)),
        gamma_t=gamma_t, beta_t=beta_t, eb1t=eb1t, eb2row=eb2row,
        abelbrow=abelbrow, gatebrow=gatebrow, gatew_t=gatew_t,
        gcw1b=gcw1b, gcw2b=gcw2b, gcbb=gcbb,
    )
    nc, _dbg = build()
    in_maps = []
    for c in range(8):
        m = dict(shared)
        m["x"] = np.ascontiguousarray(x[2 * c:2 * c + 2].reshape(B_PER * T, C))
        in_maps.append(m)
    res = run_bass_kernel_spmd(nc, in_maps, core_ids=list(range(8)))
    outs = [r["out"].reshape(B_PER, T, C) for r in res.results]
    return np.concatenate(outs, axis=0)


# version 15
# speedup vs baseline: 1.0443x; 1.0443x over previous
import contextlib
import math
import numpy as np
import concourse.bass as bass
import concourse.mybir as mybir
from concourse.bass_utils import run_bass_kernel_spmd

F32 = mybir.dt.float32
BF16 = mybir.dt.bfloat16
AF = mybir.ActivationFunctionType
ALU = mybir.AluOpType
AX = mybir.AxisListType

B_PER, T, C, NH, HS, NL, NE, HD, SLAB = 2, 2048, 1024, 16, 64, 64, 8, 2048, 1024
NSLAB = (B_PER * T) // SLAB
EPS = 1e-5

_STOP_AFTER = None
_DBG_REQ = []
_N_ITER = 6
_N_EXPN = NH
_DO_PNV = True
_DO_DEN = True


class Ctr:
    __slots__ = ("h", "n")

    def __init__(self, h):
        self.h = h
        self.n = 0


class Emit:
    def __init__(self, blk):
        self.blk = blk

    def t(self, f):
        self.blk.tensor(f)

    def v(self, f):
        self.blk.vector(f)

    def s(self, f):
        self.blk.scalar(f)

    def q(self, f):
        self.blk.sync(f)

    def g(self, f):
        self.blk.gpsimd(f)


def build():
    nc = bass.Bass("TRN2", target_bir_lowering=False)
    ap = {}

    def din(name, shape, dt):
        ap[name] = nc.dram_tensor(name, shape, dt, kind="ExternalInput").ap()

    din("x", [B_PER * T, C], F32)
    din("W_attn", [C, 3 * C], F32)
    din("W_proj", [C, C], F32)
    din("abel_w", [C, C], F32)
    din("ew1", [NE * C, HD], F32)
    din("ew2", [NE * HD, C], F32)
    din("gens", [3 * C, C], F32)
    din("idb", [128, 128], BF16)
    din("idf", [128, 128], F32)
    din("onescol_b", [128, 1], BF16)
    din("onescol_f", [128, 1], F32)
    din("onesrow_b", [1, 128], BF16)
    din("onesrow_f", [1, 64], F32)
    din("eye7w", [128, 512], F32)
    din("eye15w", [128, 512], F32)
    din("eye13w", [128, 512], F32)
    din("eyeA", [128, 8 * C], BF16)
    din("cid0", [128, 1024], F32)
    din("cid1", [128, 1024], F32)
    din("bc2", [2, 128], F32)
    din("gcw1b", [128, C], F32)
    din("gcw2b", [128, C], F32)
    din("gcbb", [128, 1], F32)
    din("epsA", [128, 1], F32)
    din("epsB", [128, 1], F32)
    din("gamma_t", [128, 8], F32)
    din("beta_t", [128, 8], F32)
    din("eb1t", [128, 128], F32)
    din("eb2row", [1, NE * C], F32)
    din("abelbrow", [1, C], F32)
    din("gatebrow", [1, 8], F32)
    din("gatew_t", [128, 64], F32)

    out_ap = nc.dram_tensor("out", [B_PER * T, C], F32, kind="ExternalOutput").ap()
    s_x2 = nc.dram_tensor("s_x2", [B_PER * T, C], F32, kind="Internal").ap()
    s_eL = nc.dram_tensor("s_eL", [8 * 128, T], BF16, kind="Internal").ap()
    s_abel = nc.dram_tensor("s_abel", [B_PER * T, C], BF16, kind="Internal").ap()
    s_wab = nc.dram_tensor("s_wab", [C, 4 * C], BF16, kind="Internal").ap()
    s_ew1b = nc.dram_tensor("s_ew1b", [NE * C, HD], BF16, kind="Internal").ap()
    s_ew2b = nc.dram_tensor("s_ew2b", [NE * HD, C], BF16, kind="Internal").ap()

    dbg_aps = {}
    dbg_live = {}
    _dbg_shapes = {
        "u0": ([128, 8 * T], BF16), "k_sb": ([128, 8 * T], BF16),
        "v_sb": ([128, 16 * C], BF16), "klf": ([128, 512], F32),
        "qlf": ([128, 512], F32), "pair": ([128, T], BF16),
        "recipL": ([128, 256], F32), "S_all": ([64, 1024], F32),
        "V_all": ([64, 1024], F32), "U_all": ([64, 1024], F32),
        "PNv": ([128, 512], BF16), "y_sb": ([128, 16 * C], BF16),
        "stats": ([128, 128], F32), "R": ([128, 8 * C], BF16),
        "wd": ([128, 256], F32), "gd1": ([128, 32], F32),
        "acc": ([128, 8 * C], F32), "xn2s": ([128, 8 * SLAB], BF16),
        "hsb": ([128, 16 * SLAB], BF16),
    }
    for nm in _DBG_REQ:
        shp, dt = _dbg_shapes[nm]
        dbg_aps[nm] = nc.dram_tensor("dbg_" + nm, shp, dt, kind="ExternalOutput").ap()

    st = contextlib.ExitStack()
    with st:
        E = st.enter_context
        idb = E(nc.sbuf_tensor("c_idb", [128, 128], BF16))
        idf = E(nc.sbuf_tensor("c_idf", [128, 128], F32))
        onescol_b = E(nc.sbuf_tensor("c_ocb", [128, 1], BF16))
        onescol_f = E(nc.sbuf_tensor("c_ocf", [128, 1], F32))
        onesrow_b = E(nc.sbuf_tensor("c_orb", [1, 128], BF16))
        gcw2b = E(nc.sbuf_tensor("c_g2", [128, C], BF16))
        gcbb = E(nc.sbuf_tensor("c_gcb", [128, 1], F32))
        epsA = E(nc.sbuf_tensor("c_eA", [128, 1], F32))
        epsB = E(nc.sbuf_tensor("c_eB", [128, 1], F32))
        gamma_t = E(nc.sbuf_tensor("c_gam", [128, 8], F32))
        beta_t = E(nc.sbuf_tensor("c_bet", [128, 8], F32))
        eb1t = E(nc.sbuf_tensor("c_eb1", [128, 128], F32))
        R_bf = E(nc.sbuf_tensor("p_R", [128, 8 * C], BF16))
        stats = E(nc.sbuf_tensor("p_st", [128, 128], F32))
        wd_sb = E(nc.sbuf_tensor("p_wd", [128, 32 * 8], F32))
        gdot1 = E(nc.sbuf_tensor("p_gd1", [128, 32], F32))
        lnst = E(nc.sbuf_tensor("p_ln", [128, 8], F32))
        estf = [E(nc.sbuf_tensor(f"c_ef{i}", [128, 1024], F32)) for i in range(2)]
        estb = [E(nc.sbuf_tensor(f"c_esb{i}", [128, 1024], BF16)) for i in range(2)]
        ps = [E(nc.psum_tensor(f"ps{i}", [128, 512], F32)) for i in range(6)]
        psb = [E(nc.psum_tensor(f"psb{i}", [128, 1024], BF16)) for i in range(2)]

        dS = Ctr(E(nc.semaphore("dS")))
        dG = Ctr(E(nc.semaphore("dG")))
        dE = Ctr(E(nc.semaphore("dE")))
        dP = Ctr(E(nc.semaphore("dP")))
        dQ = [Ctr(E(nc.semaphore("dQ0"))), Ctr(E(nc.semaphore("dQ1")))]
        dW0 = Ctr(E(nc.semaphore("dW0")))
        dW1 = Ctr(E(nc.semaphore("dW1")))
        dX = [Ctr(E(nc.semaphore("dX0"))), Ctr(E(nc.semaphore("dX1")))]
        oX = [Ctr(E(nc.semaphore("oX0"))), Ctr(E(nc.semaphore("oX1")))]
        pc = [Ctr(E(nc.semaphore("pc0"))), Ctr(E(nc.semaphore("pc1")))]
        pL = Ctr(E(nc.semaphore("pL")))
        pLv = Ctr(E(nc.semaphore("pLv")))
        pd = [Ctr(E(nc.semaphore("pd0"))), Ctr(E(nc.semaphore("pd1")))]
        eW = [Ctr(E(nc.semaphore("eW0"))), Ctr(E(nc.semaphore("eW1")))]
        vV = Ctr(E(nc.semaphore("vV")))
        sS = Ctr(E(nc.semaphore("sS")))
        hT = Ctr(E(nc.semaphore("hT")))
        hE = Ctr(E(nc.semaphore("hE")))
        oO = Ctr(E(nc.semaphore("oO")))

        def acc_group(t, out_sl, pairs, tp=None):
            n = len(pairs)
            for i, (lh, rh) in enumerate(pairs):
                ins = t.matmul(out_sl, lh, rh, start=(i == 0), stop=(i == n - 1),
                               tile_position=tp)
                if i == n - 1:
                    ins.then_inc(hT.h, 1)

        def end_fence(em):
            vals = [(c.h, c.n) for c in (dS, dG, dE, dP, dQ[0], dQ[1],
                                         dW0, dW1, dX[0], dX[1],
                                         oX[0], oX[1], pc[0], pc[1], eW[0], eW[1],
                                         pd[0], pd[1], pL, pLv, oO)]

            def _(q):
                for h_, n_ in vals:
                    q.wait_ge(h_, n_)
            em.q(_)

        def stop_block(phase):
            if _STOP_AFTER != phase:
                return False
            with nc.Block() as blk:
                em = Emit(blk)

                def _(q):
                    q.wait_ge(hT.h, hT.n)
                    q.wait_ge(vV.h, vV.n)
                    q.wait_ge(sS.h, sS.n)
                    q.wait_ge(hE.h, hE.n)
                    q.wait_ge(dG.h, dG.n)
                    q.wait_ge(dS.h, dS.n)
                    q.wait_ge(dE.h, dE.n)
                    q.wait_ge(dP.h, dP.n)
                    q.wait_ge(dQ[0].h, dQ[0].n)
                    q.wait_ge(dQ[1].h, dQ[1].n)
                    q.wait_ge(dW0.h, dW0.n)
                    q.wait_ge(dW1.h, dW1.n)
                    for c_ in dX + oX + pc + eW:
                        q.wait_ge(c_.h, c_.n)
                    q.wait_ge(oO.h, oO.n)
                    for nm in _DBG_REQ:
                        src = dbg_live.get(nm)
                        if src is None:
                            continue
                        q.dma_start(out=dbg_aps[nm], in_=src).then_inc(oO.h, 16)
                        oO.n += 16
                    q.wait_ge(oO.h, oO.n)
                em.q(_)
            return True

        ew1v = ap["ew1"].rearrange("a (b c) -> (a b) c", c=1024)
        s1v = s_ew1b.rearrange("a (b c) -> (a b) c", c=1024)
        NEST = 256
        est_st = {"i": 0, "c": 0}
        bpdE = [pd[0].n, pd[1].n]
        bpLE = pL.n

        def emit_stream(em, n):
            i0, i1 = est_st["i"], min(est_st["i"] + n, NEST + 2)
            if i1 <= i0:
                return

            def _(g, i0=i0, i1=i1):
                for i in range(i0, i1):
                    if i >= 2:
                        j = i - 2
                        g.wait_ge(pL.h, bpLE + j + 1)
                        dst = (s1v[j * 128:(j + 1) * 128, :] if j < 128 else
                               s_ew2b[(j - 128) * 128:(j - 127) * 128, :])
                        g.dma_start(out=dst, in_=estb[j % 2][:, :]
                                    ).then_inc(pd[j % 2].h, 16)
                    if i < NEST:
                        srcs = (ew1v[i * 128:(i + 1) * 128, :] if i < 128 else
                                ap["ew2"][(i - 128) * 128:(i - 127) * 128, :])
                        g.dma_start(out=estf[i % 2][:, :], in_=srcs
                                    ).then_inc(pd[i % 2].h, 16)
            em.g(_)
            for i in range(i0, i1):
                if i >= 2:
                    pd[(i - 2) % 2].n += 16
                if i < NEST:
                    pd[i % 2].n += 16
            est_st["i"] = i1

        def emit_bg_casts(em, n):
            j0 = est_st["c"]
            j1 = min(j0 + n, est_st["i"], NEST)
            if j1 <= j0:
                return

            def _(v, j0=j0, j1=j1):
                for j in range(j0, j1):
                    v.wait_ge(pd[j % 2].h, bpdE[j % 2] + 32 * (j // 2) + 16)
                    v.tensor_copy(estb[j % 2][:, :], estf[j % 2][:, :])
                    v.drain().then_inc(pL.h, 1)
            em.v(_)
            pL.n += (j1 - j0)
            est_st["c"] = j1

        def emit_bg(em, n=2):
            emit_stream(em, n)
            emit_bg_casts(em, n)

        with contextlib.ExitStack() as b0st:
            E0 = b0st.enter_context
            wst = [E0(nc.sbuf_tensor(f"w0_f{i}", [128, 4 * C], F32)) for i in range(2)]
            wsb = [E0(nc.sbuf_tensor(f"w0_b{i}", [128, 4 * C], BF16)) for i in range(2)]
            with nc.Block() as blk:
                em = Emit(blk)
                bv0 = vV.n

                def _(q):
                    for t_sb, srcn in [(idb, "idb"), (idf, "idf"), (onescol_b, "onescol_b"),
                                       (onescol_f, "onescol_f"), (onesrow_b, "onesrow_b"),
                                       (gcbb, "gcbb"), (epsA, "epsA"),
                                       (epsB, "epsB"), (gamma_t, "gamma_t"),
                                       (beta_t, "beta_t"), (eb1t, "eb1t")]:
                        q.dma_start(out=t_sb[:, :], in_=ap[srcn]).then_inc(dS.h, 16)
                        dS.n += 16
                    for i in range(10):
                        p2 = i % 2
                        if i >= 2:
                            j = i - 2
                            q.wait_ge(vV.h, bv0 + j + 1)
                            q.dma_start(out=s_wab[j * 128:(j + 1) * 128, :],
                                        in_=wsb[p2][:, :]).then_inc(pc[p2].h, 16)
                        if i < 8:
                            q.dma_start(out=wst[p2][:, 0:3 * C],
                                        in_=ap["W_attn"][i * 128:(i + 1) * 128, :]
                                        ).then_inc(pc[p2].h, 16)
                            q.dma_start(out=wst[p2][:, 3 * C:4 * C],
                                        in_=ap["W_proj"][i * 128:(i + 1) * 128, :]
                                        ).then_inc(pc[p2].h, 16)
                em.q(_)
                for i in range(10):
                    p2 = i % 2
                    if i >= 2:
                        pc[p2].n += 16
                    if i < 8:
                        pc[p2].n += 32

                for i in range(8):
                    p2 = i % 2

                    def _(v, p2=p2, w=48 * (i // 2) + 32):
                        v.wait_ge(pc[p2].h, w)
                        v.tensor_copy(wsb[p2][:, :], wst[p2][:, :])
                        v.drain().then_inc(vV.h, 1)
                    em.v(_)
                    vV.n += 1

                def _(g):
                    for t_sb, srcn in [(gcw2b, "gcw2b")]:
                        g.dma_start(out=t_sb[:, :], in_=ap[srcn]).then_inc(dG.h, 16)
                        dG.n += 16
                em.g(_)

                def _(t, w1=dS.n, w2=dG.n):
                    t.wait_ge(dS.h, w1)
                    t.wait_ge(dG.h, w2)
                em.t(_)
                end_fence(em)

        with contextlib.ExitStack() as ast:
            Ea = ast.enter_context
            Wb = [Ea(nc.sbuf_tensor(f"a_Wb{i}", [128, 8 * C], BF16)) for i in range(2)]
            eye7w = Ea(nc.sbuf_tensor("c_e7", [128, 512], F32))
            eye15w = Ea(nc.sbuf_tensor("c_e15", [128, 512], F32))
            eye13w = Ea(nc.sbuf_tensor("c_e13", [128, 512], F32))
            onesrow_f64 = Ea(nc.sbuf_tensor("c_orf", [1, 64], F32))
            k_sb = Ea(nc.sbuf_tensor("a_k", [128, 8 * T], BF16))
            v_sb = Ea(nc.sbuf_tensor("a_v", [128, 16 * C], BF16))
            q_l_f = Ea(nc.sbuf_tensor("a_qlf", [128, 512], F32))
            k_l_f = Ea(nc.sbuf_tensor("a_klf", [128, 512], F32))
            q_l_b = Ea(nc.sbuf_tensor("a_qlb", [128, 512], BF16))
            k_l_b = Ea(nc.sbuf_tensor("a_klb", [128, 512], BF16))
            U_all = Ea(nc.sbuf_tensor("a_U", [64, 1024], F32))
            PNv_b = Ea(nc.sbuf_tensor("a_PNv", [128, 512], BF16))
            recipL = Ea(nc.sbuf_tensor("a_rL", [128, 256], F32))
            rN = Ea(nc.sbuf_tensor("a_rN", [64, 16], F32))

            for b in range(B_PER):
                t0g = b * T

                with contextlib.ExitStack() as a1:
                    E1 = a1.enter_context
                    u0 = E1(nc.sbuf_tensor(f"a_u0_{b}", [128, 8 * T], BF16))
                    pairs = [E1(nc.sbuf_tensor(f"a_pair_{b}{i}", [128, T], BF16))
                             for i in range(2)]
                    tbufs = [E1(nc.sbuf_tensor(f"a_tb_{b}{i}", [128, T], BF16))
                             for i in range(2)]
                    xst = [E1(nc.sbuf_tensor(f"a_x{b}{i}", [128, C], F32)) for i in range(2)]
                    xnt_t = E1(nc.sbuf_tensor(f"a_n{b}", [128, C], BF16))
                    xnt = [xnt_t, xnt_t]

                    with nc.Block() as blk:
                        em = Emit(blk)
                        bx = dS.n
                        bv0 = vV.n

                        bX = [dX[0].n, dX[1].n]

                        def _(q, b=b, bX=bX):
                            if b == 0:
                                for t_sb, srcn in [(eye7w, "eye7w"), (eye15w, "eye15w"),
                                                   (eye13w, "eye13w"), (onesrow_f64, "onesrow_f")]:
                                    q.dma_start(out=t_sb[:, :], in_=ap[srcn]
                                                ).then_inc(dE.h, 16)
                            for i in range(16):
                                if i >= 2:
                                    q.wait_ge(vV.h, bv0 + 4 * (i - 1))
                                q.dma_start(out=xst[i % 2][:, :],
                                            in_=ap["x"][t0g + i * 128:t0g + (i + 1) * 128, :]
                                            ).then_inc(dX[i % 2].h, 16)
                        em.q(_)
                        if b == 0:
                            dE.n += 64
                        dX[0].n += 128
                        dX[1].n += 128

                        bW0 = dW0.n
                        bW1 = dW1.n
                        btM0 = hT.n

                        def _(g):
                            for dc in range(8):
                                g.dma_start(out=Wb[0][:, dc * C:(dc + 1) * C],
                                            in_=s_wab[dc * 128:(dc + 1) * 128, C:2 * C]
                                            ).then_inc(dW0.h, 16)
                            for dc in range(8):
                                g.dma_start(out=Wb[1][:, dc * C:(dc + 1) * C],
                                            in_=s_wab[dc * 128:(dc + 1) * 128, 0:C]
                                            ).then_inc(dW1.h, 16)
                            g.wait_ge(hT.h, btM0 + 128 + 32)
                            for dc in range(8):
                                g.dma_start(out=Wb[0][:, dc * C:(dc + 1) * C],
                                            in_=s_wab[dc * 128:(dc + 1) * 128, 2 * C:3 * C]
                                            ).then_inc(dW0.h, 16)
                            g.wait_ge(hT.h, btM0 + 160 + 7 * 44 + 4)
                            for dc in range(8):
                                g.dma_start(out=Wb[1][:, dc * C:(dc + 1) * C],
                                            in_=s_wab[dc * 128:(dc + 1) * 128, 3 * C:4 * C]
                                            ).then_inc(dW1.h, 16)
                        em.g(_)
                        dW0.n += 256
                        dW1.n += 256

                        bhT = hT.n
                        bhE = hE.n
                        for i in range(16):
                            def _(v, i=i):
                                v.wait_ge(dX[i % 2].h, bX[i % 2] + 16 * (i // 2 + 1))
                                v.reduce_sum(lnst[:, 0:1], xst[i % 2][:, :], axis=AX.X)
                                v.drain()
                                v.tensor_scalar(lnst[:, 1:2], lnst[:, 0:1], -1.0 / C, None,
                                                op0=ALU.mult)
                                v.drain().then_inc(vV.h, 1)
                            em.v(_)
                            vV.n += 1

                            def _(s, i=i):
                                s.wait_ge(dX[i % 2].h, bX[i % 2] + 16 * (i // 2 + 1))
                                s.activation(xnt[i % 2][:, :], xst[i % 2][:, :], AF.Square,
                                             scale=1.0 / 32.0, accum_out=lnst[:, 3:4])
                                s.drain().then_inc(sS.h, 1)
                            em.s(_)
                            sS.n += 1

                            def _(v, w=sS.n):
                                v.wait_ge(sS.h, w)
                                v.tensor_scalar(lnst[:, 2:3], lnst[:, 1:2], -1.0, None,
                                                op0=ALU.mult)
                                v.drain()
                                v.scalar_tensor_tensor(lnst[:, 4:5], lnst[:, 2:3], lnst[:, 1:2],
                                                       lnst[:, 3:4], op0=ALU.mult, op1=ALU.add)
                                v.drain().then_inc(vV.h, 1)
                            em.v(_)
                            vV.n += 1

                            def _(s, w=vV.n):
                                s.wait_ge(vV.h, w)
                                s.activation(lnst[:, 5:6], lnst[:, 4:5], AF.Ln, bias=epsA[:, 0:1])
                                s.drain()
                                s.activation(lnst[:, 6:7], lnst[:, 5:6], AF.Exp, scale=-0.5)
                                s.drain().then_inc(sS.h, 1)
                            em.s(_)
                            sS.n += 1

                            def _(v, i=i, w=sS.n):
                                v.wait_ge(sS.h, w)
                                if i >= 1:
                                    v.wait_ge(hT.h, bhT + 8 * i)
                                v.tensor_mul(lnst[:, 7:8], lnst[:, 1:2], lnst[:, 6:7])
                                v.drain()
                                v.tensor_scalar(xnt[i % 2][:, :], xst[i % 2][:, :],
                                                lnst[:, 6:7], lnst[:, 7:8],
                                                op0=ALU.mult, op1=ALU.add)
                                v.drain().then_inc(vV.h, 2)
                            em.v(_)
                            vV.n += 2

                            def _(t, i=i, w=vV.n):
                                t.wait_ge(vV.h, w)
                                for dc in range(8):
                                    j = i * 8 + dc
                                    t.wait_ge(hE.h, bhE + max(0, j - 1))
                                    t.transpose(psb[j % 2][:, 0:128],
                                                xnt[i % 2][:, dc * 128:(dc + 1) * 128],
                                                idb[:, :]).then_inc(hT.h, 1)
                            em.t(_)
                            hT.n += 8

                            def _(s, i=i):
                                for dc in range(8):
                                    j = i * 8 + dc
                                    s.wait_ge(hT.h, bhT + j + 1)
                                    s.copy(u0[:, dc * T + i * 128:dc * T + (i + 1) * 128],
                                           psb[j % 2][:, 0:128])
                                    s.drain().then_inc(hE.h, 1)
                            em.s(_)
                            hE.n += 8

                        bkT = hT.n
                        bkE = hE.n

                        def _(t):
                            t.wait_ge(hE.h, bkE)
                            t.wait_ge(dW0.h, bW0 + 128)
                            for u in range(32):
                                fc, ts = u // 4, u % 4
                                t.wait_ge(hE.h, bkE + max(0, u - 1))
                                acc_group(t, ps[u % 2][:, :],
                                          [(Wb[0][:, dc * C + fc * 128:dc * C + (fc + 1) * 128],
                                            u0[:, dc * T + ts * 512:dc * T + (ts + 1) * 512])
                                           for dc in range(8)])
                        em.t(_)
                        hT.n += 32

                        def _(s):
                            for u in range(32):
                                fc, ts = u // 4, u % 4
                                s.wait_ge(hT.h, bkT + u + 1)
                                s.copy(k_sb[:, fc * T + ts * 512:fc * T + (ts + 1) * 512],
                                       ps[u % 2][:, :])
                                s.drain().then_inc(hE.h, 1)
                        em.s(_)
                        hE.n += 32

                        def _(v, w=hE.n):
                            v.wait_ge(hE.h, w)
                            for fc in range(8):
                                v.reduce_sum(k_l_f[:, fc * 64:(fc + 1) * 64],
                                             k_sb[:, fc * T:(fc + 1) * T]
                                             .rearrange("p (l g) -> p l g", g=32), axis=AX.X)
                            v.drain()
                            v.tensor_copy(k_l_b[:, :], k_l_f[:, :])
                            v.drain().then_inc(vV.h, 1)
                        em.v(_)
                        vV.n += 1
                        klb_ready = vV.n

                        bqv = vV.n
                        bdQ = [dQ[0].n, dQ[1].n]
                        qb = {}

                        def emit_qmm(fc):
                            qb[fc] = (hT.n, hE.n)
                            bqE = hE.n

                            def _(t, fc=fc, bqE=bqE):
                                if fc == 0:
                                    t.wait_ge(dW1.h, bW1 + 128)
                                for ts in range(4):
                                    if fc == 0 or ts >= 2:
                                        t.wait_ge(hE.h, bqE + max(0, ts - 1))
                                    else:
                                        t.wait_ge(hE.h, bqE - 9 + ts)
                                    acc_group(t, ps[ts % 2][:, :],
                                              [(Wb[1][:, dc * C + fc * 128:dc * C + (fc + 1) * 128],
                                                u0[:, dc * T + ts * 512:dc * T + (ts + 1) * 512])
                                               for dc in range(8)])
                            em.t(_)
                            hT.n += 4

                        emit_qmm(0)
                        for fc in range(8):
                            tbuf = tbufs[fc % 2]
                            bqT, bqE = qb[fc]

                            def _(s, fc=fc, bqT=bqT, tbuf=tbuf):
                                if fc >= 2:
                                    s.wait_ge(vV.h, bqv + 2 * (fc - 2) + 1)
                                for ts in range(4):
                                    s.wait_ge(hT.h, bqT + ts + 1)
                                    s.copy(tbuf[:, ts * 512:(ts + 1) * 512], ps[ts % 2][:, :])
                                    s.drain().then_inc(hE.h, 1)
                            em.s(_)
                            hE.n += 4

                            def _(v, fc=fc, w=hE.n):
                                v.wait_ge(hE.h, w)
                                v.reduce_sum(q_l_f[:, fc * 64:(fc + 1) * 64],
                                             tbuf[:, :].rearrange("p (l g) -> p l g", g=32),
                                             axis=AX.X)
                                v.drain()
                                v.tensor_copy(q_l_b[:, fc * 64:(fc + 1) * 64],
                                              q_l_f[:, fc * 64:(fc + 1) * 64])
                                v.drain().then_inc(vV.h, 1)
                            em.v(_)
                            vV.n += 1

                            beT = hT.n
                            beE = hE.n

                            def _(t, fc=fc, beE=beE, w=hE.n):
                                t.wait_ge(vV.h, klb_ready)
                                t.wait_ge(hE.h, w)
                                for hh in range(2):
                                    p = ((2 * fc + hh) % 2) * 64
                                    for ts in range(4):
                                        u = hh * 4 + ts
                                        t.wait_ge(hE.h, beE + max(0, u - 1))
                                        t.matmul(ps[2 + (u % 2)][p:p + 64, :],
                                                 k_l_b[p:p + 64, fc * 64:(fc + 1) * 64],
                                                 tbuf[p:p + 64, ts * 512:(ts + 1) * 512],
                                                 start=True, stop=True,
                                                 tile_position=(p, p)).then_inc(hT.h, 1)
                            em.t(_)
                            hT.n += 8

                            def _(s, fc=fc, beT=beT, wd_=bdQ[fc % 2] + 16 * (fc // 2)):
                                s.wait_ge(dQ[fc % 2].h, wd_)
                                for hh in range(2):
                                    p = ((2 * fc + hh) % 2) * 64
                                    for ts in range(4):
                                        u = hh * 4 + ts
                                        s.wait_ge(hT.h, beT + u + 1)
                                        s.activation(pairs[fc % 2][p:p + 64, ts * 512:(ts + 1) * 512],
                                                     ps[2 + (u % 2)][p:p + 64, :],
                                                     AF.Exp, scale=1.0 / 32.0)
                                        s.drain().then_inc(hE.h, 1)
                            em.s(_)
                            hE.n += 8

                            if fc + 1 < 8:
                                emit_qmm(fc + 1)

                            def _(t, fc=fc, w=hE.n, wv=vV.n):
                                t.wait_ge(hE.h, w)
                                t.wait_ge(vV.h, wv)
                                for hh in range(2):
                                    h = 2 * fc + hh
                                    p = (h % 2) * 64
                                    for tc in range(16):
                                        t.matmul(ps[4][:, hh * 16 + tc:hh * 16 + tc + 1],
                                                 pairs[fc % 2][p:p + 64, tc * 128:(tc + 1) * 128],
                                                 onescol_b[p:p + 64, 0:1],
                                                 start=True, stop=True).then_inc(hT.h, 1)
                            em.t(_)
                            hT.n += 32

                            def _(v, fc=fc, w=hT.n):
                                v.wait_ge(hT.h, w)
                                for hh in range(2):
                                    h = 2 * fc + hh
                                    v.reciprocal(recipL[:, h * 16:(h + 1) * 16],
                                                 ps[4][:, hh * 16:hh * 16 + 16])
                                v.drain().then_inc(vV.h, 1)
                            em.v(_)
                            vV.n += 1

                            def _(q, fc=fc, w=vV.n):
                                q.wait_ge(vV.h, w)
                                q.dma_start(out=s_eL[fc * 128:(fc + 1) * 128, :],
                                            in_=pairs[fc % 2][:, :]).then_inc(dQ[fc % 2].h, 16)
                            em.q(_)
                            dQ[fc % 2].n += 16

                        bvT = hT.n
                        bvE = hE.n

                        def _(t):
                            t.wait_ge(dW0.h, bW0 + 256)
                            for u in range(32):
                                tc, n2 = u // 2, u % 2
                                t.wait_ge(hE.h, bvE + max(0, u - 1))
                                acc_group(t, ps[u % 2][:, :],
                                          [(u0[:, dc * T + tc * 128:dc * T + (tc + 1) * 128],
                                            Wb[0][:, dc * C + n2 * 512:dc * C + (n2 + 1) * 512])
                                           for dc in range(8)])
                        em.t(_)
                        hT.n += 32

                        def _(s):
                            for u in range(32):
                                tc, n2 = u // 2, u % 2
                                s.wait_ge(hT.h, bvT + u + 1)
                                s.copy(v_sb[:, tc * C + n2 * 512:tc * C + (n2 + 1) * 512],
                                       ps[u % 2][:, :])
                                s.drain().then_inc(hE.h, 1)
                        em.s(_)
                        hE.n += 32
                        end_fence(em)

                    if b == 0:
                        dbg_live.update(u0=u0[:, :], k_sb=k_sb[:, :], v_sb=v_sb[:, :],
                                        klf=k_l_f[:, :], qlf=q_l_f[:, :], pair=pairs[1][:, :],
                                        recipL=recipL[:, :])
                        if stop_block("a1b0"):
                            return nc, dbg_aps

                with contextlib.ExitStack() as a2:
                    E2 = a2.enter_context
                    ebuf = E2(nc.sbuf_tensor(f"a_eN_{b}", [128, 2048], BF16))
                    iS = E2(nc.sbuf_tensor(f"i_S_{b}", [64, 1024], F32))
                    iST = E2(nc.sbuf_tensor(f"i_ST_{b}", [64, 1024], F32))
                    iV = E2(nc.sbuf_tensor(f"i_V_{b}", [64, 1024], F32))
                    iVT = E2(nc.sbuf_tensor(f"i_VT_{b}", [64, 1024], F32))
                    iKVT = E2(nc.sbuf_tensor(f"i_KVT_{b}", [64, 1024], F32))
                    iT1 = E2(nc.sbuf_tensor(f"i_T1_{b}", [64, 1024], F32))
                    iT2 = E2(nc.sbuf_tensor(f"i_T2_{b}", [64, 1024], F32))
                    csum = E2(nc.sbuf_tensor(f"i_cs_{b}", [64, 16], F32))
                    rd = E2(nc.sbuf_tensor(f"i_rd_{b}", [16, 4], F32))
                    rdT = E2(nc.sbuf_tensor(f"i_rdT_{b}", [1, 16], F32))
                    denb = E2(nc.sbuf_tensor(f"i_db_{b}", [64, 16], F32))

                    with nc.Block() as blk:
                        em = Emit(blk)

                        def _(t):
                            for h in range(NH):
                                p = (h % 2) * 64
                                g2, j = h % 2, h // 2
                                t.matmul(ps[g2][0:64, j * 64:(j + 1) * 64],
                                         q_l_f[p:p + 64, (h // 2) * 64:(h // 2 + 1) * 64],
                                         k_l_f[p:p + 64, (h // 2) * 64:(h // 2 + 1) * 64],
                                         start=True, stop=True,
                                         tile_position=(p, 0)).then_inc(hT.h, 1)
                        em.t(_)
                        hT.n += 16

                        def _(s, w=hT.n):
                            s.wait_ge(hT.h, w)
                            import os as _os
                            if _os.environ.get("NOSTRIDE"):
                                s.activation(iS[:, 0:512], ps[0][0:64, :],
                                             AF.Exp, scale=1.0 / 1024.0)
                                s.drain()
                                s.activation(iS[:, 512:1024], ps[1][0:64, :],
                                             AF.Exp, scale=1.0 / 1024.0)
                            else:
                                iSv = iS[:, :].rearrange("p (a q m) -> p a q m", q=2, m=64)
                                s.activation(iSv[:, :, 0, :],
                                             ps[0][0:64, :].rearrange("p (a m) -> p a m", m=64),
                                             AF.Exp, scale=1.0 / 1024.0)
                                s.drain()
                                s.activation(iSv[:, :, 1, :],
                                             ps[1][0:64, :].rearrange("p (a m) -> p a m", m=64),
                                             AF.Exp, scale=1.0 / 1024.0)
                            s.drain().then_inc(sS.h, 1)
                        em.s(_)
                        sS.n += 1

                        def _(v, w=sS.n, we=dE.n):
                            v.wait_ge(sS.h, w)
                            v.wait_ge(dE.h, we)
                            v.reduce_sum(csum[:, 0:16],
                                         iS[:, :].rearrange("p (j m) -> p j m", m=64),
                                         axis=AX.X)
                            v.drain()
                            v.reciprocal(csum[:, 0:16], csum[:, 0:16])
                            v.drain()
                            for h in range(NH):
                                v.tensor_scalar(iS[:, h * 64:(h + 1) * 64],
                                                iS[:, h * 64:(h + 1) * 64],
                                                csum[:, h:h + 1], None, op0=ALU.mult)
                            v.drain().then_inc(vV.h, 1)
                        em.v(_)
                        vV.n += 1

                        def _(t, w=vV.n):
                            t.wait_ge(vV.h, w)
                            for h in range(NH):
                                g2, j = 2 + h // 8, h % 8
                                t.transpose(ps[g2][0:64, j * 64:(j + 1) * 64],
                                            iS[:, h * 64:(h + 1) * 64],
                                            idf[0:64, 0:64]).then_inc(hT.h, 1)
                        em.t(_)
                        hT.n += 16

                        def _(v, w=hT.n):
                            v.wait_ge(hT.h, w)
                            v.tensor_copy(iST[:, 0:512], ps[2][0:64, :])
                            v.tensor_copy(iST[:, 512:1024], ps[3][0:64, :])
                            v.drain()
                            v.reduce_sum(csum[:, 0:16],
                                         iST[:, :].rearrange("p (j m) -> p j m", m=64),
                                         axis=AX.X)
                            v.drain().then_inc(vV.h, 1)
                        em.v(_)
                        vV.n += 1

                        def _(t, w=vV.n):
                            t.wait_ge(vV.h, w)
                            t.transpose(ps[2][0:16, 0:64], csum[:, 0:16], idf[0:64, 0:64]
                                        ).then_inc(hT.h, 1)
                        em.t(_)
                        hT.n += 1

                        def _(v, w=hT.n):
                            v.wait_ge(hT.h, w)
                            v.reduce_max(rd[0:16, 0:1], ps[2][0:16, 0:64], axis=AX.X)
                            v.drain()
                            v.reciprocal(rd[0:16, 0:1], rd[0:16, 0:1])
                            v.drain().then_inc(vV.h, 1)
                        em.v(_)
                        vV.n += 1

                        def _(t, w=vV.n):
                            t.wait_ge(vV.h, w)
                            t.transpose(ps[3][0:1, 0:16], rd[0:16, 0:1], idf[0:16, 0:16]
                                        ).then_inc(hT.h, 1)
                        em.t(_)
                        hT.n += 1

                        def _(v, w=hT.n):
                            v.wait_ge(hT.h, w)
                            v.tensor_copy(rdT[0:1, 0:16], ps[3][0:1, 0:16])
                            v.drain().then_inc(vV.h, 1)
                        em.v(_)
                        vV.n += 1

                        def _(t, w=vV.n):
                            t.wait_ge(vV.h, w)
                            t.matmul(ps[2][0:64, 0:16], onesrow_f64[0:1, 0:64],
                                     rdT[0:1, 0:16], start=True, stop=True
                                     ).then_inc(hT.h, 1)
                        em.t(_)
                        hT.n += 1

                        def _(v, w=hT.n):
                            v.wait_ge(hT.h, w)
                            v.tensor_copy(denb[:, 0:16], ps[2][0:64, 0:16])
                            v.drain()
                            for h in range(NH):
                                v.tensor_scalar(iV[:, h * 64:(h + 1) * 64],
                                                iST[:, h * 64:(h + 1) * 64],
                                                denb[:, h:h + 1], None, op0=ALU.mult)
                                v.tensor_scalar(iVT[:, h * 64:(h + 1) * 64],
                                                iS[:, h * 64:(h + 1) * 64],
                                                denb[:, h:h + 1], None, op0=ALU.mult)
                            v.drain().then_inc(vV.h, 1)
                        em.v(_)
                        vV.n += 1

                        def istep_mm(t, dst0, lhs_t, rhs_t):
                            for h in range(NH):
                                g2, j = dst0 + h // 8, h % 8
                                t.matmul(ps[g2][0:64, j * 64:(j + 1) * 64],
                                         lhs_t[:, h * 64:(h + 1) * 64],
                                         rhs_t[:, h * 64:(h + 1) * 64],
                                         start=True, stop=True).then_inc(hT.h, 1)

                        def istep_tr(t, dst0, src_t):
                            for h in range(NH):
                                g2, j = dst0 + h // 8, h % 8
                                t.transpose(ps[g2][0:64, j * 64:(j + 1) * 64],
                                            src_t[:, h * 64:(h + 1) * 64],
                                            idf[0:64, 0:64]).then_inc(hT.h, 1)

                        for itn in range(_N_ITER):
                            def _(t, w=vV.n):
                                t.wait_ge(vV.h, w)
                                istep_mm(t, 2, iST, iV)
                                istep_mm(t, 4, iV, iST)
                            em.t(_)
                            hT.n += 32

                            def _(v, w=hT.n):
                                v.wait_ge(hT.h, w)
                                v.scalar_tensor_tensor(iT1[:, 0:512], ps[2][0:64, :], -1.0,
                                                       eye7w[0:64, :], op0=ALU.mult, op1=ALU.add)
                                v.scalar_tensor_tensor(iT1[:, 512:1024], ps[3][0:64, :], -1.0,
                                                       eye7w[0:64, :], op0=ALU.mult, op1=ALU.add)
                                v.tensor_copy(iKVT[:, 0:512], ps[4][0:64, :])
                                v.tensor_copy(iKVT[:, 512:1024], ps[5][0:64, :])
                                v.drain().then_inc(vV.h, 1)
                            em.v(_)
                            vV.n += 1

                            def _(t, w=vV.n):
                                t.wait_ge(vV.h, w)
                                istep_mm(t, 2, iKVT, iT1)
                            em.t(_)
                            hT.n += 16

                            def _(v, w=hT.n):
                                v.wait_ge(hT.h, w)
                                v.scalar_tensor_tensor(iT2[:, 0:512], ps[2][0:64, :], -1.0,
                                                       eye15w[0:64, :], op0=ALU.mult, op1=ALU.add)
                                v.scalar_tensor_tensor(iT2[:, 512:1024], ps[3][0:64, :], -1.0,
                                                       eye15w[0:64, :], op0=ALU.mult, op1=ALU.add)
                                v.drain().then_inc(vV.h, 1)
                            em.v(_)
                            vV.n += 1

                            def _(t, w=vV.n):
                                t.wait_ge(vV.h, w)
                                istep_mm(t, 4, iKVT, iT2)
                            em.t(_)
                            hT.n += 16

                            def _(v, w=hT.n):
                                v.wait_ge(hT.h, w)
                                v.scalar_tensor_tensor(iT1[:, 0:512], ps[4][0:64, :], -1.0,
                                                       eye13w[0:64, :], op0=ALU.mult, op1=ALU.add)
                                v.scalar_tensor_tensor(iT1[:, 512:1024], ps[5][0:64, :], -1.0,
                                                       eye13w[0:64, :], op0=ALU.mult, op1=ALU.add)
                                v.drain().then_inc(vV.h, 1)
                            em.v(_)
                            vV.n += 1

                            def _(t, w=vV.n):
                                t.wait_ge(vV.h, w)
                                istep_mm(t, 2, iVT, iT1)
                            em.t(_)
                            hT.n += 16

                            def _(v, w=hT.n):
                                v.wait_ge(hT.h, w)
                                v.tensor_scalar(iV[:, 0:512], ps[2][0:64, :], 0.25, None,
                                                op0=ALU.mult)
                                v.tensor_scalar(iV[:, 512:1024], ps[3][0:64, :], 0.25, None,
                                                op0=ALU.mult)
                                v.drain().then_inc(vV.h, 1)
                            em.v(_)
                            vV.n += 1

                            def _(t, w=vV.n):
                                t.wait_ge(vV.h, w)
                                istep_tr(t, 4, iV)
                            em.t(_)
                            hT.n += 16

                            def _(v, w=hT.n):
                                v.wait_ge(hT.h, w)
                                v.tensor_copy(iVT[:, 0:512], ps[4][0:64, :])
                                v.tensor_copy(iVT[:, 512:1024], ps[5][0:64, :])
                                v.drain().then_inc(vV.h, 1)
                            em.v(_)
                            vV.n += 1

                        for h in range(_N_EXPN):
                            emit_bg(em, 2)
                            p, cb = (h % 2) * 64, h // 2
                            half = (h % 2) * 1024
                            ubank = 3 + h // 8
                            bnT = hT.n
                            bnE = hE.n

                            pb = 0 if h % 2 == 0 else 4

                            def _(t, p=p, cb=cb, pb=pb, wv=vV.n, we=hE.n):
                                t.wait_ge(vV.h, wv)
                                t.wait_ge(hE.h, we)
                                for tc in range(16):
                                    ins = t.matmul(
                                        ps[pb + tc // 8][:, (tc % 8) * 64:(tc % 8 + 1) * 64],
                                        k_sb[p:p + 64, cb * T + tc * 128:cb * T + (tc + 1) * 128],
                                        q_l_b[p:p + 64, cb * 64:(cb + 1) * 64],
                                        start=True, stop=True, tile_position=(p, 0))
                                    if tc % 8 == 7:
                                        ins.then_inc(hT.h, 1)
                            em.t(_)
                            hT.n += 2

                            def _(s, bnT=bnT, half=half, pb=pb):
                                for g2 in range(2):
                                    s.wait_ge(hT.h, bnT + g2 + 1)
                                    s.activation(ebuf[:, half + g2 * 512:half + (g2 + 1) * 512],
                                                 ps[pb + g2][:, :],
                                                 AF.Exp, scale=1.0 / 32.0)
                                    s.drain().then_inc(hE.h, 1)
                            em.s(_)
                            hE.n += 2

                            def _(t, h=h, half=half, ubank=ubank, w=hE.n, wv=vV.n):
                                t.wait_ge(hE.h, w)
                                t.wait_ge(vV.h, wv)
                                acc_group(t, ps[2][0:64, h:h + 1],
                                          [(ebuf[:, half + tc * 64:half + (tc + 1) * 64],
                                            onescol_b[:, 0:1]) for tc in range(16)])
                                acc_group(t, ps[ubank][0:64, (h % 8) * 64:(h % 8 + 1) * 64],
                                          [(ebuf[:, half + tc * 64:half + (tc + 1) * 64],
                                            v_sb[:, tc * C + h * 64:tc * C + (h + 1) * 64])
                                           for tc in range(16)])
                            em.t(_)
                            hT.n += 2

                            def _(v, h=h, ubank=ubank, w=hT.n):
                                v.wait_ge(hT.h, w)
                                v.reciprocal(rN[:, h:h + 1], ps[2][0:64, h:h + 1])
                                v.drain()
                                v.tensor_scalar(U_all[:, h * 64:(h + 1) * 64],
                                                ps[ubank][0:64, (h % 8) * 64:(h % 8 + 1) * 64],
                                                rN[:, h:h + 1], None, op0=ALU.mult)
                                v.drain().then_inc(vV.h, 1)
                            em.v(_)
                            vV.n += 1

                        def _(t, w=vV.n):
                            t.wait_ge(vV.h, w)
                            for h in range(NH if _DO_PNV else 0):
                                p, cb = (h % 2) * 64, h // 2
                                t.matmul(ps[5][p:p + 64, cb * 64:(cb + 1) * 64],
                                         iVT[:, h * 64:(h + 1) * 64],
                                         U_all[:, h * 64:(h + 1) * 64],
                                         start=True, stop=True,
                                         tile_position=(0, p)).then_inc(hT.h, 1)
                        em.t(_)
                        hT.n += 16

                        def _(v, w=hT.n):
                            v.wait_ge(hT.h, w)
                            if _DO_PNV:
                                v.tensor_copy(PNv_b[:, :], ps[5][:, :])
                            v.drain().then_inc(vV.h, 1)
                        em.v(_)
                        vV.n += 1
                        end_fence(em)

                    if b == 0:
                        dbg_live.update(S_all=iS[:, :], V_all=iV[:, :], U_all=U_all[:, :],
                                        PNv=PNv_b[:, :])
                        if stop_block("a2b0"):
                            return nc, dbg_aps

                with contextlib.ExitStack() as a3:
                    E3 = a3.enter_context
                    y_sb = E3(nc.sbuf_tensor(f"a_y_{b}", [128, 16 * C], BF16))
                    prb = E3(nc.sbuf_tensor(f"a_pr0_{b}", [128, T], BF16))
                    ytT = E3(nc.sbuf_tensor(f"a_yt_{b}", [128, C], BF16))
                    xst = [E3(nc.sbuf_tensor(f"a_r{b}{i}", [128, C], F32)) for i in range(2)]
                    x2t = [E3(nc.sbuf_tensor(f"a_2{b}{i}", [128, C], F32)) for i in range(2)]

                    with nc.Block() as blk:
                        em = Emit(blk)
                        brl = dP.n
                        bye = hE.n

                        def _(q):
                            for fc in range(8):
                                if fc >= 1:
                                    q.wait_ge(hE.h, bye + 2 * fc)
                                q.dma_start(out=prb[:, :],
                                            in_=s_eL[fc * 128:(fc + 1) * 128, :]
                                            ).then_inc(dP.h, 16)
                        em.q(_)
                        dP.n += 8 * 16

                        byT = hT.n

                        def _(t):
                            u = 0
                            for fc in range(8):
                                t.wait_ge(dP.h, brl + 16 * (fc + 1))
                                for hh in range(2):
                                    h = 2 * fc + hh
                                    p, cb = (h % 2) * 64, h // 2
                                    t.wait_ge(hE.h, bye + max(0, u - 1))
                                    for tc in range(16):
                                        ins = t.matmul(
                                            ps[(u % 2) * 2 + tc // 8][:, (tc % 8) * 64:(tc % 8 + 1) * 64],
                                            prb[p:p + 64, tc * 128:(tc + 1) * 128],
                                            PNv_b[p:p + 64, cb * 64:(cb + 1) * 64],
                                            start=True, stop=True,
                                            tile_position=(p, 0))
                                        if tc == 15:
                                            ins.then_inc(hT.h, 1)
                                    u += 1
                        em.t(_)
                        hT.n += 16

                        def _(v):
                            u = 0
                            for fc in range(8):
                                for hh in range(2):
                                    h = 2 * fc + hh
                                    v.wait_ge(hT.h, byT + u + 1)
                                    for tc in range(16):
                                        v.tensor_scalar(
                                            y_sb[:, tc * C + h * 64:tc * C + (h + 1) * 64],
                                            ps[(u % 2) * 2 + tc // 8][:, (tc % 8) * 64:(tc % 8 + 1) * 64],
                                            recipL[:, h * 16 + tc:h * 16 + tc + 1],
                                            None, op0=ALU.mult)
                                    v.drain().then_inc(hE.h, 1)
                                    u += 1
                        em.v(_)
                        hE.n += 16
                        yq_done = hE.n

                        bX = [dX[0].n, dX[1].n]
                        boX = [oX[0].n, oX[1].n]
                        bcons = vV.n

                        def _(q, bX=bX):
                            for tc in range(2):
                                q.dma_start(out=xst[tc % 2][:, :],
                                            in_=ap["x"][t0g + tc * 128:t0g + (tc + 1) * 128, :]
                                            ).then_inc(dX[tc % 2].h, 16)
                        em.q(_)
                        dX[0].n += 16
                        dX[1].n += 16

                        for tc in range(16):
                            btr = hT.n
                            bev = sS.n

                            def _(t, tc=tc, bev=bev):
                                t.wait_ge(hE.h, yq_done)
                                for dc in range(8):
                                    t.wait_ge(sS.h, bev + max(0, dc - 1))
                                    t.transpose(psb[dc % 2][:, 0:128],
                                                y_sb[:, tc * C + dc * 128:tc * C + (dc + 1) * 128],
                                                idb[:, :]).then_inc(hT.h, 1)
                            em.t(_)
                            hT.n += 8

                            def _(s, btr=btr):
                                for dc in range(8):
                                    s.wait_ge(hT.h, btr + dc + 1)
                                    s.copy(ytT[:, dc * 128:(dc + 1) * 128],
                                           psb[dc % 2][:, 0:128])
                                    s.drain().then_inc(sS.h, 1)
                            em.s(_)
                            sS.n += 8

                            bpj = hT.n

                            def _(t, tc=tc, w=sS.n, ww=dW1.n):
                                t.wait_ge(sS.h, w)
                                t.wait_ge(dW1.h, ww)
                                if tc >= 2:
                                    t.wait_ge(vV.h, bcons + 3 * (tc - 2) + 1)
                                for n2 in range(2):
                                    acc_group(t, ps[4 + n2][:, :],
                                              [(ytT[:, dc * 128:(dc + 1) * 128],
                                                Wb[1][:, dc * C + n2 * 512:dc * C + (n2 + 1) * 512])
                                               for dc in range(8)])
                            em.t(_)
                            hT.n += 2

                            def _(v, tc=tc, bpj=bpj):
                                v.wait_ge(hT.h, bpj + 2)
                                v.wait_ge(dW1.h, dW1.n)
                                v.wait_ge(dX[tc % 2].h, bX[tc % 2] + 16 * (tc // 2 + 1))
                                if tc >= 2:
                                    v.wait_ge(oX[tc % 2].h, boX[tc % 2] + 16 * (tc // 2))
                                for n2 in range(2):
                                    v.tensor_add(x2t[tc % 2][:, n2 * 512:(n2 + 1) * 512],
                                                 ps[4 + n2][:, :],
                                                 xst[tc % 2][:, n2 * 512:(n2 + 1) * 512])
                                v.drain()
                                v.reduce_sum(lnst[:, 0:1], x2t[tc % 2][:, :], axis=AX.X)
                                v.drain()
                                v.tensor_scalar(lnst[:, 1:2], lnst[:, 0:1], -1.0 / C, None,
                                                op0=ALU.mult)
                                v.drain().then_inc(vV.h, 1)
                            em.v(_)
                            vV.n += 1

                            def _(s, tc=tc, w=vV.n):
                                s.wait_ge(vV.h, w)
                                s.activation(ytT[:, :], x2t[tc % 2][:, :], AF.Square,
                                             scale=1.0 / 32.0, accum_out=lnst[:, 3:4])
                                s.drain().then_inc(sS.h, 1)
                            em.s(_)
                            sS.n += 1

                            def _(v, w=sS.n):
                                v.wait_ge(sS.h, w)
                                v.tensor_scalar(lnst[:, 2:3], lnst[:, 1:2], -1.0, None,
                                                op0=ALU.mult)
                                v.drain()
                                v.scalar_tensor_tensor(lnst[:, 4:5], lnst[:, 2:3], lnst[:, 1:2],
                                                       lnst[:, 3:4], op0=ALU.mult, op1=ALU.add)
                                v.drain().then_inc(vV.h, 1)
                            em.v(_)
                            vV.n += 1

                            def _(s, w=vV.n):
                                s.wait_ge(vV.h, w)
                                s.activation(lnst[:, 5:6], lnst[:, 4:5], AF.Ln, bias=epsA[:, 0:1])
                                s.drain()
                                s.activation(lnst[:, 6:7], lnst[:, 5:6], AF.Exp, scale=-0.5)
                                s.drain()
                                s.activation(lnst[:, 5:6], lnst[:, 4:5], AF.Ln, scale=1.0 + EPS,
                                             bias=epsB[:, 0:1])
                                s.drain()
                                s.activation(lnst[:, 7:8], lnst[:, 5:6], AF.Exp, scale=-0.5)
                                s.drain().then_inc(sS.h, 1)
                            em.s(_)
                            sS.n += 1

                            def _(v, tc=tc, b=b, w=sS.n):
                                s4 = 4 * (b * 16 + tc)
                                v.wait_ge(sS.h, w)
                                v.tensor_scalar(stats[:, s4:s4 + 1], lnst[:, 6:7], 1.0, None,
                                                op0=ALU.add)
                                v.tensor_mul(stats[:, s4 + 1:s4 + 2], lnst[:, 1:2], lnst[:, 6:7])
                                v.tensor_copy(stats[:, s4 + 2:s4 + 3], lnst[:, 7:8])
                                v.tensor_mul(stats[:, s4 + 3:s4 + 4], lnst[:, 1:2], lnst[:, 7:8])
                                v.drain().then_inc(vV.h, 1)
                            em.v(_)
                            vV.n += 1

                            def _(q, tc=tc, w=vV.n):
                                q.wait_ge(vV.h, w)
                                q.dma_start(out=s_x2[t0g + tc * 128:t0g + (tc + 1) * 128, :],
                                            in_=x2t[tc % 2][:, :]).then_inc(oX[tc % 2].h, 16)
                                if tc + 2 < 16:
                                    q.wait_ge(vV.h, bcons + 3 * tc + 1)
                                    q.dma_start(out=xst[tc % 2][:, :],
                                                in_=ap["x"][t0g + (tc + 2) * 128:t0g + (tc + 3) * 128, :]
                                                ).then_inc(dX[tc % 2].h, 16)
                            em.q(_)
                            oX[tc % 2].n += 16
                            if tc + 2 < 16:
                                dX[tc % 2].n += 16

                            emit_bg(em, 2)
                        end_fence(em)

                    if b == 0:
                        dbg_live.update(y_sb=y_sb[:, :], stats=stats[:, :])
                        if stop_block("a3b0"):
                            return nc, dbg_aps

            if stop_block("attn"):
                return nc, dbg_aps

        with contextlib.ExitStack() as est:
            Ee = est.enter_context
            eyA = Ee(nc.sbuf_tensor("e_I", [128, 8 * C], BF16))
            mA = Ee(nc.sbuf_tensor("e_A", [128, 8 * C], BF16))
            mA2 = Ee(nc.sbuf_tensor("e_A2", [128, 8 * C], BF16))
            mA3 = Ee(nc.sbuf_tensor("e_A3", [128, 8 * C], BF16))
            mB = Ee(nc.sbuf_tensor("e_B", [128, 8 * C], BF16))
            mT = Ee(nc.sbuf_tensor("e_T", [128, 8 * C], BF16))
            mE1 = Ee(nc.sbuf_tensor("e_E1", [128, 8 * C], BF16))
            mE2 = Ee(nc.sbuf_tensor("e_E2", [128, 8 * C], BF16))
            mXT = Ee(nc.sbuf_tensor("e_XT", [128, 8 * C], BF16))
            cidf = Ee(nc.sbuf_tensor("e_cf", [128, 2048], F32))
            cidb = Ee(nc.sbuf_tensor("e_cb", [128, 13 * 128], BF16))

            with nc.Block() as blk:
                em = Emit(blk)

                def mm1024g(t, pair0, prs_fn, bE, bS):
                    u = 0
                    for mc in range(8):
                        for n2 in range(2):
                            if u == 0:
                                t.wait_ge(hE.h, bE)
                                t.wait_ge(sS.h, bS)
                            elif u >= 2:
                                if u - 2 < 8:
                                    t.wait_ge(hE.h, bE + u - 1)
                                else:
                                    t.wait_ge(sS.h, bS + u - 9)
                            acc_group(t, ps[pair0 + (u % 2)][:, :], prs_fn(mc, n2))
                            u += 1

                def mm1024(t, pair0, lhsT_sb, rhs_sb, bE, bS):
                    def prs(mc, n2):
                        return [(lhsT_sb[:, dc * C + mc * 128:dc * C + (mc + 1) * 128],
                                 rhs_sb[:, dc * C + n2 * 512:dc * C + (n2 + 1) * 512])
                                for dc in range(8)]
                    mm1024g(t, pair0, prs, bE, bS)

                def evac1024(v, dst_sb, pair0, bT, scale):
                    for u in range(8):
                        mc, n2 = u // 2, u % 2
                        v.wait_ge(hT.h, bT + u + 1)
                        v.tensor_scalar(dst_sb[:, mc * C + n2 * 512:mc * C + (n2 + 1) * 512],
                                        ps[pair0 + (u % 2)][:, :], scale, None,
                                        op0=ALU.mult)
                        v.drain().then_inc(hE.h, 1)

                def evac1024_s(s, dst_sb, pair0, bT, scale):
                    for u in range(8, 16):
                        mc, n2 = u // 2, u % 2
                        s.wait_ge(hT.h, bT + u + 1)
                        s.activation(dst_sb[:, mc * C + n2 * 512:mc * C + (n2 + 1) * 512],
                                     ps[pair0 + (u % 2)][:, :], AF.Copy, scale=scale)
                        s.drain().then_inc(sS.h, 1)

                def cid_b(k):
                    return cidb[:, k * 128:(k + 1) * 128]

                def chunk(x_sb, mc, n2):
                    return x_sb[:, mc * C + n2 * 512:mc * C + (n2 + 1) * 512]

                def _(g):
                    g.dma_start(out=eyA[:, :], in_=ap["eyeA"]).then_inc(dG.h, 16)
                    for hh in range(2):
                        g.dma_start(out=cidf[:, hh * 1024:(hh + 1) * 1024],
                                    in_=ap["cid0" if hh == 0 else "cid1"]
                                    ).then_inc(dG.h, 16)
                em.g(_)
                dG.n += 48

                def _(v, w=dG.n):
                    v.wait_ge(dG.h, w)
                    v.tensor_copy(cidb[:, :], cidf[:, 0:13 * 128])
                    v.drain().then_inc(vV.h, 1)
                em.v(_)
                vV.n += 1
                cid_ready = vV.n

                for gi in range(3):
                    def _(g, gi=gi, wt=hT.n, wd=dG.n):
                        g.wait_ge(dG.h, wd)
                        g.wait_ge(hT.h, wt)
                        for dc in range(8):
                            g.dma_start(out=mA[:, dc * C:(dc + 1) * C],
                                        in_=ap["gens"][gi * C + dc * 128:gi * C + (dc + 1) * 128, :]
                                        ).then_inc(dG.h, 16)
                    em.g(_)
                    dG.n += 128

                    b1, b2, b3 = hT.n, hE.n, sS.n

                    def _(t, w=dG.n, b2=b2, b3=b3):
                        t.wait_ge(dG.h, w)
                        t.wait_ge(vV.h, cid_ready)
                        mm1024(t, 0, mA, mA, b2, b3)
                    em.t(_)
                    hT.n += 16

                    def _(v, b1=b1):
                        evac1024(v, mA2, 0, b1, -1.0)
                    em.v(_)

                    def _(s, b1=b1):
                        evac1024_s(s, mA2, 0, b1, -1.0)
                    em.s(_)
                    hE.n += 8
                    sS.n += 8

                    emit_bg(em, 2)

                    b1, b2, b3 = hT.n, hE.n, sS.n

                    def _(t, b2=b2, b3=b3):
                        mm1024(t, 2, mA2, mA, b2, b3)
                    em.t(_)
                    hT.n += 16

                    def _(v, b1=b1):
                        evac1024(v, mA3, 2, b1, 1.0)
                    em.v(_)

                    def _(s, b1=b1):
                        evac1024_s(s, mA3, 2, b1, 1.0)
                    em.s(_)
                    hE.n += 8
                    sS.n += 8

                    emit_bg(em, 2)

                    b1, b2, b3 = hT.n, hE.n, sS.n

                    def _(t, b2=b2, b3=b3):
                        def prs(mc, n2):
                            return [(cid_b(12), chunk(mA3, mc, n2)),
                                    (cid_b(11), chunk(mA2, mc, n2)),
                                    (cid_b(10), chunk(mA, mc, n2)),
                                    (cid_b(9), chunk(eyA, mc, n2))]
                        mm1024g(t, 0, prs, b2, b3)
                    em.t(_)
                    hT.n += 16

                    def _(v, b1=b1):
                        evac1024(v, mB, 0, b1, 1.0)
                    em.v(_)

                    def _(s, b1=b1):
                        evac1024_s(s, mB, 0, b1, 1.0)
                    em.s(_)
                    hE.n += 8
                    sS.n += 8

                    emit_bg(em, 2)

                    for stepi, (kI, kA, kA2) in enumerate([(6, 7, 8), (3, 4, 5), (0, 1, 2)]):
                        rhs_t = mB if stepi == 0 else (mT if stepi == 1 else mB)
                        dst_t = mT if stepi == 0 else (mB if stepi == 1 else mT)
                        b1, b2, b3 = hT.n, hE.n, sS.n

                        def _(t, rhs_t=rhs_t, kI=kI, kA=kA, kA2=kA2, b2=b2, b3=b3):
                            def prs(mc, n2):
                                lst = [(mA3[:, dc * C + mc * 128:dc * C + (mc + 1) * 128],
                                        rhs_t[:, dc * C + n2 * 512:dc * C + (n2 + 1) * 512])
                                       for dc in range(8)]
                                lst.append((cid_b(kA2), chunk(mA2, mc, n2)))
                                lst.append((cid_b(kA), chunk(mA, mc, n2)))
                                lst.append((cid_b(kI), chunk(eyA, mc, n2)))
                                return lst
                            mm1024g(t, 0, prs, b2, b3)
                        em.t(_)
                        hT.n += 16

                        def _(s, dst_t=dst_t, b1=b1):
                            evac1024_s(s, dst_t, 0, b1, -1.0)
                        em.s(_)

                        def _(v, dst_t=dst_t, b1=b1):
                            evac1024(v, dst_t, 0, b1, -1.0)
                        em.v(_)
                        hE.n += 8
                        sS.n += 8

                        emit_bg(em, 2)

                    if gi < 2:
                        dst_e = mE1 if gi == 0 else mE2

                        def _(v, dst_e=dst_e, w2=hE.n, w3=sS.n):
                            v.wait_ge(hE.h, w2)
                            v.wait_ge(sS.h, w3)
                            for mc in range(8):
                                sl = slice(mc * C, (mc + 1) * C)
                                v.tensor_copy(dst_e[:, sl], mT[:, sl])
                            v.drain().then_inc(vV.h, 1)
                        em.v(_)
                        vV.n += 1
                    else:
                        btr, bte = hT.n, sS.n

                        def _(t, w=vV.n, w2=hE.n, bte=bte):
                            t.wait_ge(vV.h, w)
                            t.wait_ge(hE.h, w2)
                            for u in range(64):
                                mc, dc = u // 8, u % 8
                                t.wait_ge(sS.h, bte + max(0, u - 1))
                                t.transpose(psb[u % 2][:, 0:128],
                                            mE1[:, mc * C + dc * 128:mc * C + (dc + 1) * 128],
                                            idb[:, :]).then_inc(hT.h, 1)
                        em.t(_)
                        hT.n += 64

                        def _(s, btr=btr):
                            for u in range(64):
                                mc, dc = u // 8, u % 8
                                s.wait_ge(hT.h, btr + u + 1)
                                s.copy(mXT[:, dc * C + mc * 128:dc * C + (mc + 1) * 128],
                                       psb[u % 2][:, 0:128])
                                s.drain().then_inc(sS.h, 1)
                        em.s(_)
                        sS.n += 64

                        b1, b2, b3 = hT.n, hE.n, sS.n

                        def _(t, w=sS.n, b2=b2, b3=b3):
                            t.wait_ge(sS.h, w)
                            mm1024(t, 0, mXT, mE2, b2, b3)
                        em.t(_)
                        hT.n += 16

                        def _(s, b1=b1):
                            evac1024_s(s, mB, 0, b1, 1.0)
                        em.s(_)

                        def _(v, b1=b1):
                            evac1024(v, mB, 0, b1, 1.0)
                        em.v(_)
                        hE.n += 8
                        sS.n += 8

                        emit_bg(em, 2)

                        btr, bte = hT.n, sS.n

                        def _(t, w2=hE.n, bte=bte):
                            t.wait_ge(hE.h, w2)
                            for u in range(64):
                                mc, dc = u // 8, u % 8
                                t.wait_ge(sS.h, bte + max(0, u - 1))
                                t.transpose(psb[u % 2][:, 0:128],
                                            mB[:, mc * C + dc * 128:mc * C + (dc + 1) * 128],
                                            idb[:, :]).then_inc(hT.h, 1)
                        em.t(_)
                        hT.n += 64

                        def _(s, btr=btr):
                            for u in range(64):
                                mc, dc = u // 8, u % 8
                                s.wait_ge(hT.h, btr + u + 1)
                                s.copy(mXT[:, dc * C + mc * 128:dc * C + (mc + 1) * 128],
                                       psb[u % 2][:, 0:128])
                                s.drain().then_inc(sS.h, 1)
                        em.s(_)
                        sS.n += 64

                        b1, b2, b3 = hT.n, hE.n, sS.n

                        def _(t, w=sS.n, b2=b2, b3=b3):
                            t.wait_ge(sS.h, w)
                            mm1024(t, 0, mXT, mT, b2, b3)
                        em.t(_)
                        hT.n += 16

                        def _(s, b1=b1):
                            evac1024_s(s, R_bf, 0, b1, 1.0)
                        em.s(_)

                        def _(v, b1=b1):
                            evac1024(v, R_bf, 0, b1, 1.0)
                        em.v(_)
                        hE.n += 8
                        sS.n += 8

                        emit_bg(em, 2)

                end_fence(em)

            dbg_live.update(R=R_bf[:, :])
            if stop_block("expm"):
                return nc, dbg_aps

        with contextlib.ExitStack() as fst:
            Ef = fst.enter_context
            abw = Ef(nc.sbuf_tensor("f_aw", [128, 8 * C], BF16))
            gcw1b = Ef(nc.sbuf_tensor("c_g1", [128, C], BF16))
            abelbrow = Ef(nc.sbuf_tensor("c_abr", [1, C], BF16))
            gatebrow = Ef(nc.sbuf_tensor("c_gbr", [1, 8], BF16))
            gatew_t = Ef(nc.sbuf_tensor("c_gwt", [128, 64], BF16))
            xn2s = Ef(nc.sbuf_tensor("f_xn", [128, 8 * SLAB], BF16))
            x2st = [Ef(nc.sbuf_tensor(f"f_x{i}", [128, C], F32)) for i in range(2)]
            xcb = Ef(nc.sbuf_tensor("f_xc", [128, C], BF16))
            abt = [Ef(nc.sbuf_tensor(f"f_ab{i}", [128, C], BF16)) for i in range(2)]
            prod = Ef(nc.sbuf_tensor("f_pr", [128, C], F32))
            glog = Ef(nc.sbuf_tensor("f_gl", [128, 8], F32))
            gw2 = Ef(nc.sbuf_tensor("f_g2", [128, 8], F32))
            gmsk = Ef(nc.sbuf_tensor("f_gm", [128, 8], F32))
            gw1s = Ef(nc.sbuf_tensor("f_g1s", [128, 64], F32))
            gw2ms = Ef(nc.sbuf_tensor("f_g2s", [128, 64], F32))
            gex = Ef(nc.sbuf_tensor("f_gx", [128, 48], F32))

            with nc.Block() as blk:
                em = Emit(blk)
                x2sp = [oX[0].n, oX[1].n]
                bXf = [dX[0].n, dX[1].n]
                bax = [oX[0].n, oX[1].n]

                def _(g):
                    for t_sb, srcn in [(gcw1b, "gcw1b"), (abelbrow, "abelbrow"),
                                       (gatebrow, "gatebrow"), (gatew_t, "gatew_t")]:
                        g.dma_start(out=t_sb[:, :], in_=ap[srcn]).then_inc(dG.h, 16)
                    for dc in range(8):
                        g.dma_start(out=abw[:, dc * C:(dc + 1) * C],
                                    in_=ap["abel_w"][dc * 128:(dc + 1) * 128, :]
                                    ).then_inc(dG.h, 16)
                em.g(_)
                dG.n += 128 + 64
                abw_ready = dG.n

                ps2free = {-1: 0}
                for si in range(NSLAB):
                    for j in range(8):
                        bi = si * 8 + j
                        tg = bi * 128

                        def _(q, bi=bi, tg=tg, wv=vV.n, x2sp=x2sp):
                            q.wait_ge(oX[0].h, x2sp[0])
                            q.wait_ge(oX[1].h, x2sp[1])
                            if bi >= 2:
                                q.wait_ge(vV.h, wv)
                            q.dma_start(out=x2st[bi % 2][:, :],
                                        in_=s_x2[tg:tg + 128, :]).then_inc(dX[bi % 2].h, 16)
                        em.q(_)
                        dX[bi % 2].n += 16

                        def _(v, bi=bi, w=dX[bi % 2].n):
                            v.wait_ge(dX[bi % 2].h, w)
                            s4 = 4 * bi
                            v.tensor_scalar(xcb[:, :], x2st[bi % 2][:, :],
                                            stats[:, s4 + 2:s4 + 3], stats[:, s4 + 3:s4 + 4],
                                            op0=ALU.mult, op1=ALU.add)
                            v.drain().then_inc(vV.h, 1)
                        em.v(_)
                        vV.n += 1

                        btr, bev = hT.n, vV.n

                        def _(t, w=vV.n, bev=bev):
                            t.wait_ge(vV.h, w)
                            for dc in range(8):
                                t.wait_ge(vV.h, bev + max(0, dc - 1))
                                t.transpose(psb[dc % 2][:, 0:128],
                                            xcb[:, dc * 128:(dc + 1) * 128],
                                            idb[:, :]).then_inc(hT.h, 1)
                        em.t(_)
                        hT.n += 8

                        def _(v, j=j, btr=btr):
                            for dc in range(8):
                                v.wait_ge(hT.h, btr + dc + 1)
                                v.tensor_scalar(xn2s[:, dc * SLAB + j * 128:dc * SLAB + (j + 1) * 128],
                                                psb[dc % 2][:, 0:128],
                                                gamma_t[:, dc:dc + 1], beta_t[:, dc:dc + 1],
                                                op0=ALU.mult, op1=ALU.add)
                                v.drain().then_inc(vV.h, 1)
                        em.v(_)
                        vV.n += 8

                    for j in range(8):
                        bi = si * 8 + j
                        bab = hT.n

                        def _(t, j=j, ws=vV.n, whe=hE.n, wvv=ps2free[si * 8 + j - 1]):
                            t.wait_ge(vV.h, ws)
                            t.wait_ge(dG.h, abw_ready)
                            t.wait_ge(hE.h, whe)
                            t.wait_ge(vV.h, wvv)
                            for n2 in range(2):
                                prs = [(xn2s[:, dc * SLAB + j * 128:dc * SLAB + (j + 1) * 128],
                                        abw[:, dc * C + n2 * 512:dc * C + (n2 + 1) * 512])
                                       for dc in range(8)]
                                prs.append((onesrow_b[0:1, 0:128],
                                            abelbrow[0:1, n2 * 512:(n2 + 1) * 512]))
                                acc_group(t, ps[n2][:, :], prs)
                            gprs = [(xn2s[:, dc * SLAB + j * 128:dc * SLAB + (j + 1) * 128],
                                     gatew_t[:, dc * 8:(dc + 1) * 8]) for dc in range(8)]
                            gprs.append((onesrow_b[0:1, 0:128], gatebrow[0:1, 0:8]))
                            acc_group(t, ps[2][:, 0:8], gprs)
                        em.t(_)
                        hT.n += 3

                        def _(s, j=j, bab=bab, wo=oX[j % 2].n):
                            s.wait_ge(oX[j % 2].h, wo)
                            for n2 in range(2):
                                s.wait_ge(hT.h, bab + n2 + 1)
                                s.activation(abt[j % 2][:, n2 * 512:(n2 + 1) * 512],
                                             ps[n2][:, :], AF.Gelu)
                                s.drain().then_inc(hE.h, 1)
                        em.s(_)
                        hE.n += 2

                        ps2free[bi] = vV.n + 1

                        def _(v, j=j, bi=bi, bab=bab, w=hE.n):
                            v.wait_ge(hT.h, bab + 3)
                            v.tensor_copy(glog[:, 0:8], ps[2][:, 0:8])
                            v.drain().then_inc(vV.h, 1)
                            v.wait_ge(hE.h, w)
                            v.tensor_mul(prod[:, :], abt[j % 2][:, :], gcw1b[:, :])
                            v.drain()
                            v.reduce_sum(gdot1[:, bi:bi + 1], prod[:, :], axis=AX.X)
                            v.drain().then_inc(vV.h, 1)
                            v.reduce_max(gex[:, j:j + 1], glog[:, 0:8], axis=AX.X)
                            v.drain()
                            v.tensor_scalar(gw1s[:, j * 8:(j + 1) * 8], glog[:, 0:8],
                                            gex[:, j:j + 1], None, op0=ALU.is_ge)
                            v.drain()
                            v.scalar_tensor_tensor(gw2[:, 0:8], gw1s[:, j * 8:(j + 1) * 8],
                                                   -1e30, glog[:, 0:8],
                                                   op0=ALU.mult, op1=ALU.add)
                            v.drain()
                            v.reduce_max(gex[:, 8 + j:9 + j], gw2[:, 0:8], axis=AX.X)
                            v.drain()
                            v.tensor_scalar(gw2ms[:, j * 8:(j + 1) * 8], gw2[:, 0:8],
                                            gex[:, 8 + j:9 + j], None, op0=ALU.is_ge)
                            v.drain().then_inc(vV.h, 1)
                        em.v(_)
                        vV.n += 3

                        emit_bg(em, 2)

                        def _(q, bi=bi, j=j, w=vV.n):
                            q.wait_ge(vV.h, w - 1)
                            q.dma_start(out=s_abel[bi * 128:(bi + 1) * 128, :],
                                        in_=abt[j % 2][:, :]).then_inc(oX[j % 2].h, 16)
                        em.q(_)
                        oX[j % 2].n += 16

                    def _(v, w=vV.n):
                        v.wait_ge(vV.h, w)
                        v.tensor_tensor(gex[:, 16:24], gex[:, 8:16], gex[:, 0:8],
                                        op=ALU.subtract)
                        v.drain().then_inc(vV.h, 1)
                    em.v(_)
                    vV.n += 1

                    def _(s, w=vV.n):
                        s.wait_ge(vV.h, w)
                        s.activation(gex[:, 24:32], gex[:, 16:24], AF.Exp)
                        s.drain().then_inc(sS.h, 1)
                    em.s(_)
                    sS.n += 1

                    def _(v, si=si, w=sS.n):
                        v.wait_ge(sS.h, w)
                        v.tensor_scalar(gex[:, 32:40], gex[:, 24:32], 1.0, None, op0=ALU.add)
                        v.drain()
                        v.reciprocal(gex[:, 32:40], gex[:, 32:40])
                        v.drain()
                        v.tensor_scalar(gex[:, 40:48], gex[:, 32:40], -1.0, 1.0,
                                        op0=ALU.mult, op1=ALU.add)
                        v.drain()
                        for j in range(8):
                            bi = si * 8 + j
                            v.tensor_scalar(gmsk[:, 0:8], gw2ms[:, j * 8:(j + 1) * 8],
                                            gex[:, 40 + j:41 + j], None, op0=ALU.mult)
                            v.drain()
                            v.scalar_tensor_tensor(wd_sb[:, bi * 8:(bi + 1) * 8],
                                                   gw1s[:, j * 8:(j + 1) * 8],
                                                   gex[:, 32 + j:33 + j], gmsk[:, 0:8],
                                                   op0=ALU.mult, op1=ALU.add)
                            v.drain()
                        v.drain().then_inc(vV.h, 1)
                    em.v(_)
                    vV.n += 1

                while est_st["i"] < NEST + 2 or est_st["c"] < NEST:
                    emit_bg(em, 4)
                end_fence(em)

            dbg_live.update(wd=wd_sb[:, :], gd1=gdot1[:, :], xn2s=xn2s[:, :])
            if stop_block("fa"):
                return nc, dbg_aps

        with contextlib.ExitStack() as fbt:
            Eb = fbt.enter_context
            ew1s = Eb(nc.sbuf_tensor("b_w1", [128, 8 * HD], BF16))
            eb2row = Eb(nc.sbuf_tensor("c_eb2", [1, C], BF16))
            ew2s = Eb(nc.sbuf_tensor("b_w2", [128, 16 * C], BF16))
            xn2s = Eb(nc.sbuf_tensor("b_xn", [128, 8 * SLAB], BF16))
            h_sb = Eb(nc.sbuf_tensor("b_h", [128, 16 * SLAB], BF16))
            acc = Eb(nc.sbuf_tensor("b_acc", [128, 8 * C], F32))
            x2st = [Eb(nc.sbuf_tensor(f"b_x{i}", [128, C], F32)) for i in range(2)]
            abr = [Eb(nc.sbuf_tensor(f"b_ar{i}", [128, C], BF16)) for i in range(2)]
            xcb = Eb(nc.sbuf_tensor("b_xc", [128, C], BF16))
            accT = Eb(nc.sbuf_tensor("b_aT", [128, C], BF16))
            accB = Eb(nc.sbuf_tensor("b_aB", [128, C], BF16))
            otile = [Eb(nc.sbuf_tensor(f"b_o{i}", [128, C], F32)) for i in range(2)]
            prod = Eb(nc.sbuf_tensor("b_pr", [128, 512], F32))

            with nc.Block() as blk:
                em = Emit(blk)
                spx = [oX[0].n, oX[1].n]

                def emit_expert_loads(e, gate_hT):
                    def _(q, e=e, gate_hT=gate_hT):
                        q.wait_ge(hT.h, gate_hT)
                        for dc in range(8):
                            q.dma_start(out=ew1s[:, dc * HD:(dc + 1) * HD],
                                        in_=s_ew1b[e * C + dc * 128:e * C + (dc + 1) * 128, :]
                                        ).then_inc(eW[0].h, 16)
                        for hc in range(16):
                            q.dma_start(out=ew2s[:, hc * C:(hc + 1) * C],
                                        in_=s_ew2b[e * HD + hc * 128:e * HD + (hc + 1) * 128, :]
                                        ).then_inc(eW[1].h, 16)
                    em.q(_)
                    eW[0].n += 128
                    eW[1].n += 256

                def _(g):
                    g.dma_start(out=eb2row[:, :], in_=ap["eb2row"][0:1, 0:C]
                                ).then_inc(dG.h, 16)
                em.g(_)
                dG.n += 16

                emit_expert_loads(0, 0)

                for si in range(NSLAB):
                    for j in range(8):
                        bi = si * 8 + j
                        tg = bi * 128

                        def _(q, bi=bi, tg=tg, wv=vV.n, spx=spx):
                            q.wait_ge(oX[0].h, spx[0])
                            q.wait_ge(oX[1].h, spx[1])
                            if bi >= 2:
                                q.wait_ge(vV.h, wv)
                            q.dma_start(out=x2st[bi % 2][:, :],
                                        in_=s_x2[tg:tg + 128, :]).then_inc(dX[bi % 2].h, 16)
                        em.q(_)
                        dX[bi % 2].n += 16

                        def _(v, bi=bi, w=dX[bi % 2].n):
                            v.wait_ge(dX[bi % 2].h, w)
                            s4 = 4 * bi
                            v.tensor_scalar(xcb[:, :], x2st[bi % 2][:, :],
                                            stats[:, s4 + 2:s4 + 3], stats[:, s4 + 3:s4 + 4],
                                            op0=ALU.mult, op1=ALU.add)
                            v.drain().then_inc(vV.h, 1)
                        em.v(_)
                        vV.n += 1

                        btr, bev = hT.n, vV.n

                        def _(t, w=vV.n, bev=bev):
                            t.wait_ge(vV.h, w)
                            for dc in range(8):
                                t.wait_ge(vV.h, bev + max(0, dc - 1))
                                t.transpose(psb[dc % 2][:, 0:128],
                                            xcb[:, dc * 128:(dc + 1) * 128],
                                            idb[:, :]).then_inc(hT.h, 1)
                        em.t(_)
                        hT.n += 8

                        def _(v, j=j, btr=btr):
                            for dc in range(8):
                                v.wait_ge(hT.h, btr + dc + 1)
                                v.tensor_scalar(xn2s[:, dc * SLAB + j * 128:dc * SLAB + (j + 1) * 128],
                                                psb[dc % 2][:, 0:128],
                                                gamma_t[:, dc:dc + 1], beta_t[:, dc:dc + 1],
                                                op0=ALU.mult, op1=ALU.add)
                                v.drain().then_inc(vV.h, 1)
                        em.v(_)
                        vV.n += 8

                    xn2_ready = vV.n

                    for e in range(NE):
                        bhT, bhE = hT.n, hE.n

                        def _(t, w1=eW[0].n, bhE=bhE, xn2_ready=xn2_ready, wv=vV.n):
                            t.wait_ge(vV.h, max(xn2_ready, wv))
                            t.wait_ge(eW[0].h, w1)
                            for hc in range(16):
                                pb = 0 if hc % 2 == 0 else 2
                                t.wait_ge(hE.h, bhE + max(0, hc - 1))
                                for tb2 in range(2):
                                    for dc in range(8):
                                        ins = t.matmul(
                                            ps[pb + tb2][:, :],
                                            ew1s[:, dc * HD + hc * 128:dc * HD + (hc + 1) * 128],
                                            xn2s[:, dc * SLAB + tb2 * 512:dc * SLAB + (tb2 + 1) * 512],
                                            start=(dc == 0), stop=(dc == 7))
                                        if dc == 7 and tb2 == 1:
                                            ins.then_inc(hT.h, 1)
                        em.t(_)
                        hT.n += 16
                        h_mm_done = hT.n

                        def _(s, e=e, bhT=bhT):
                            for hc in range(16):
                                pb = 0 if hc % 2 == 0 else 2
                                s.wait_ge(hT.h, bhT + hc + 1)
                                for tb2 in range(2):
                                    s.activation(h_sb[:, hc * SLAB + tb2 * 512:hc * SLAB + (tb2 + 1) * 512],
                                                 ps[pb + tb2][:, :], AF.Gelu,
                                                 bias=eb1t[:, e * 16 + hc:e * 16 + hc + 1])
                                s.drain().then_inc(hE.h, 1)
                        em.s(_)
                        hE.n += 16
                        h_done = hE.n

                        nxt = (e + 1, si) if e < NE - 1 else (0, si + 1)
                        if nxt[1] < NSLAB:
                            def _(q, ne=nxt[0], gate=h_mm_done):
                                q.wait_ge(hT.h, gate)
                                for dc in range(8):
                                    q.dma_start(out=ew1s[:, dc * HD:(dc + 1) * HD],
                                                in_=s_ew1b[ne * C + dc * 128:ne * C + (dc + 1) * 128, :]
                                                ).then_inc(eW[0].h, 16)
                            em.q(_)
                            eW[0].n += 128

                        beT, beE = hT.n, vV.n

                        def _(t, e=e, h_done=h_done, beE=beE, w2=eW[1].n, wg=dG.n):
                            t.wait_ge(hE.h, h_done)
                            t.wait_ge(eW[1].h, w2)
                            t.wait_ge(dG.h, wg)
                            for tc in range(8):
                                pb = 4 if tc % 2 == 0 else 2
                                t.wait_ge(vV.h, beE + max(0, tc - 1))
                                for n2 in range(2):
                                    for hc in range(16):
                                        t.matmul(
                                            ps[pb + n2][:, :],
                                            h_sb[:, hc * SLAB + tc * 128:hc * SLAB + (tc + 1) * 128],
                                            ew2s[:, hc * C + n2 * 512:hc * C + (n2 + 1) * 512],
                                            start=(hc == 0), stop=False)
                                    ins = t.matmul(ps[pb + n2][:, :], onesrow_b[0:1, 0:128],
                                                   eb2row[0:1, n2 * 512:(n2 + 1) * 512],
                                                   start=False, stop=True)
                                    if n2 == 1:
                                        ins.then_inc(hT.h, 1)
                        em.t(_)
                        hT.n += 8
                        eo_mm_done = hT.n

                        def _(v, e=e, si=si, beT=beT):
                            for tc in range(8):
                                pb = 4 if tc % 2 == 0 else 2
                                v.wait_ge(hT.h, beT + tc + 1)
                                for n2 in range(2):
                                    asl = acc[:, tc * C + n2 * 512:tc * C + (n2 + 1) * 512]
                                    wsl = wd_sb[:, (si * 8 + tc) * 8 + e:(si * 8 + tc) * 8 + e + 1]
                                    if e == 0:
                                        v.tensor_scalar(asl, ps[pb + n2][:, :], wsl, None,
                                                        op0=ALU.mult)
                                    else:
                                        v.scalar_tensor_tensor(asl, ps[pb + n2][:, :], wsl,
                                                               asl, op0=ALU.mult, op1=ALU.add)
                                v.drain().then_inc(vV.h, 1)
                        em.v(_)
                        vV.n += 8

                        if nxt[1] < NSLAB:
                            def _(g, ne=nxt[0], gate=eo_mm_done):
                                g.wait_ge(hT.h, gate)
                                g.dma_start(out=eb2row[:, :],
                                            in_=ap["eb2row"][0:1, ne * C:(ne + 1) * C]
                                            ).then_inc(dG.h, 16)
                            em.g(_)
                            dG.n += 16

                            def _(q, ne=nxt[0], gate=eo_mm_done):
                                q.wait_ge(hT.h, gate)
                                for hc in range(16):
                                    q.dma_start(out=ew2s[:, hc * C:(hc + 1) * C],
                                                in_=s_ew2b[ne * HD + hc * 128:ne * HD + (hc + 1) * 128, :]
                                                ).then_inc(eW[1].h, 16)
                            em.q(_)
                            eW[1].n += 256

                    brm = vV.n

                    for tc in range(8):
                        bi = si * 8 + tc
                        tg = bi * 128
                        rp = 4 if tc % 2 == 0 else 2
                        btr, bev = hT.n, sS.n

                        def _(q, tc=tc, tg=tg):
                            q.wait_ge(vV.h, brm if tc < 2 else brm + 3 * (tc - 2) + 3)
                            q.dma_start(out=x2st[tc % 2][:, :],
                                        in_=s_x2[tg:tg + 128, :]).then_inc(dX[tc % 2].h, 16)
                            q.dma_start(out=abr[tc % 2][:, :],
                                        in_=s_abel[tg:tg + 128, :]).then_inc(dX[tc % 2].h, 16)
                        em.q(_)
                        dX[tc % 2].n += 32

                        def _(v, tc=tc, w=vV.n if tc == 0 else brm, wt=hT.n):
                            v.wait_ge(vV.h, w)
                            v.wait_ge(hT.h, wt)
                            v.tensor_copy(accB[:, :], acc[:, tc * C:(tc + 1) * C])
                            v.drain().then_inc(vV.h, 1)
                        em.v(_)
                        vV.n += 1

                        def _(t, tc=tc, w=vV.n, bev=bev):
                            t.wait_ge(vV.h, w)
                            for dc in range(8):
                                t.wait_ge(sS.h, bev + max(0, dc - 1))
                                t.transpose(psb[dc % 2][:, 0:128],
                                            accB[:, dc * 128:(dc + 1) * 128],
                                            idb[:, :]).then_inc(hT.h, 1)
                        em.t(_)
                        hT.n += 8

                        def _(s, btr=btr):
                            for dc in range(8):
                                s.wait_ge(hT.h, btr + dc + 1)
                                s.copy(accT[:, dc * 128:(dc + 1) * 128],
                                       psb[dc % 2][:, 0:128])
                                s.drain().then_inc(sS.h, 1)
                        em.s(_)
                        sS.n += 8

                        brt = hT.n

                        def _(t, tc=tc, rp=rp, w=sS.n):
                            t.wait_ge(sS.h, w)
                            if tc >= 2:
                                t.wait_ge(vV.h, brm + 3 * (tc - 2) + 3)
                            for n2 in range(2):
                                acc_group(t, ps[rp + n2][:, :],
                                          [(accT[:, dc * 128:(dc + 1) * 128],
                                            R_bf[:, dc * C + n2 * 512:dc * C + (n2 + 1) * 512])
                                           for dc in range(8)])
                        em.t(_)
                        hT.n += 2

                        def _(v, bi=bi, rp=rp, brt=brt, w=dX[tc % 2].n, tc=tc):
                            v.wait_ge(hT.h, brt + 2)
                            v.wait_ge(dX[tc % 2].h, w)
                            v.tensor_tensor(prod[:, :], ps[rp][:, :], gcw2b[:, 0:512],
                                            op=ALU.mult)
                            v.drain()
                            v.reduce_sum(lnst[:, 0:1], prod[:, :], axis=AX.X)
                            v.drain()
                            v.tensor_tensor(prod[:, :], ps[rp + 1][:, :], gcw2b[:, 512:1024],
                                            op=ALU.mult)
                            v.drain()
                            v.reduce_sum(lnst[:, 4:5], prod[:, :], axis=AX.X)
                            v.drain()
                            v.tensor_tensor(lnst[:, 1:2], lnst[:, 0:1], lnst[:, 4:5],
                                            op=ALU.add)
                            v.drain()
                            v.tensor_tensor(lnst[:, 1:2], lnst[:, 1:2], gdot1[:, bi:bi + 1],
                                            op=ALU.add)
                            v.drain().then_inc(vV.h, 1)
                        em.v(_)
                        vV.n += 1

                        def _(s, w=vV.n):
                            s.wait_ge(vV.h, w)
                            s.activation(lnst[:, 2:3], lnst[:, 1:2], AF.Sigmoid,
                                         bias=gcbb[:, 0:1])
                            s.drain().then_inc(sS.h, 1)
                        em.s(_)
                        sS.n += 1

                        def _(v, bi=bi, tc=tc, rp=rp, w=sS.n, wo=oX[tc % 2].n):
                            v.wait_ge(sS.h, w)
                            v.wait_ge(oX[tc % 2].h, wo)
                            v.tensor_scalar(lnst[:, 3:4], lnst[:, 2:3], -1.0, 1.0,
                                            op0=ALU.mult, op1=ALU.add)
                            s4 = 4 * bi
                            v.tensor_scalar(otile[tc % 2][:, :], x2st[tc % 2][:, :],
                                            stats[:, s4:s4 + 1], stats[:, s4 + 1:s4 + 2],
                                            op0=ALU.mult, op1=ALU.add)
                            v.drain()
                            v.scalar_tensor_tensor(otile[tc % 2][:, :], abr[tc % 2][:, :],
                                                   lnst[:, 3:4], otile[tc % 2][:, :],
                                                   op0=ALU.mult, op1=ALU.add)
                            v.drain()
                            for n2 in range(2):
                                v.scalar_tensor_tensor(otile[tc % 2][:, n2 * 512:(n2 + 1) * 512],
                                                       ps[rp + n2][:, :], lnst[:, 2:3],
                                                       otile[tc % 2][:, n2 * 512:(n2 + 1) * 512],
                                                       op0=ALU.mult, op1=ALU.add)
                            v.drain().then_inc(vV.h, 1)
                        em.v(_)
                        vV.n += 1

                        def _(q, tc=tc, tg=tg, w=vV.n):
                            q.wait_ge(vV.h, w)
                            q.dma_start(out=out_ap[tg:tg + 128, :],
                                        in_=otile[tc % 2][:, :]).then_inc(oX[tc % 2].h, 16)
                        em.q(_)
                        oX[tc % 2].n += 16

                end_fence(em)

            dbg_live.update(acc=acc[:, :], hsb=h_sb[:, :])
            stop_block(None)

    return nc, dbg_aps


def _cid_const():
    import math
    cid = np.zeros((128, 2048), np.float32)
    for k in range(13):
        s = 1.0 / math.factorial(k)
        if k <= 8:
            s = -s
        cid[:, k * 128:(k + 1) * 128] = s * np.eye(128, dtype=np.float32)
    return cid


def _consts():
    import ml_dtypes
    i64 = np.eye(64, dtype=np.float32)
    eyeA = np.zeros((128, 8 * C), np.float32)
    for dc in range(8):
        eyeA[:, dc * C + dc * 128:dc * C + (dc + 1) * 128] = np.eye(128)
    bc2 = np.zeros((2, 128), np.float32)
    bc2[0, 0:64] = 1.0
    bc2[1, 64:128] = 1.0
    d = {
        "idb": np.eye(128, dtype=np.float32).astype(ml_dtypes.bfloat16),
        "idf": np.eye(128, dtype=np.float32),
        "onescol_b": np.ones((128, 1), np.float32).astype(ml_dtypes.bfloat16),
        "onescol_f": np.ones((128, 1), np.float32),
        "onesrow_b": np.ones((1, 128), np.float32).astype(ml_dtypes.bfloat16),
        "onesrow_f": np.ones((1, 64), np.float32),
        "eye7w": np.tile(7.0 * i64, (2, 8)).astype(np.float32),
        "eye15w": np.tile(15.0 * i64, (2, 8)).astype(np.float32),
        "eye13w": np.tile(13.0 * i64, (2, 8)).astype(np.float32),
        "eyeA": eyeA.astype(ml_dtypes.bfloat16),
        "cid0": _cid_const()[:, 0:1024].copy(),
        "cid1": _cid_const()[:, 1024:2048].copy(),
        "bc2": bc2,
        "epsA": np.full((128, 1), 1e-5, np.float32),
        "epsB": np.full((128, 1), 1e-10, np.float32),
    }
    return d


def kernel(L0_attn_x, L0_attn_W_attn, L0_attn_W_proj, L0_ffn_ln_gamma,
           L0_ffn_ln_beta, L0_ffn_abel_w, L0_ffn_abel_b, L0_ffn_moe_gate_w,
           L0_ffn_moe_gate_b, L0_ffn_ew1, L0_ffn_eb1, L0_ffn_ew2, L0_ffn_eb2,
           L0_ffn_gens, L0_ffn_gc_w, L0_ffn_gc_b):
    x = np.ascontiguousarray(np.asarray(L0_attn_x, np.float32))
    consts = _consts()
    g = np.asarray(L0_ffn_ln_gamma, np.float32)
    be = np.asarray(L0_ffn_ln_beta, np.float32)
    gamma_t = np.ascontiguousarray(g.reshape(8, 128).T)
    beta_t = np.ascontiguousarray(be.reshape(8, 128).T)
    eb1 = np.asarray(L0_ffn_eb1, np.float32)
    eb1t = np.ascontiguousarray(
        eb1.reshape(8, 16, 128).transpose(2, 0, 1).reshape(128, 128))
    eb2row = np.ascontiguousarray(np.asarray(L0_ffn_eb2, np.float32).reshape(1, NE * C))
    abelbrow = np.ascontiguousarray(np.asarray(L0_ffn_abel_b, np.float32).reshape(1, C))
    gatebrow = np.ascontiguousarray(
        np.asarray(L0_ffn_moe_gate_b, np.float32).reshape(1, 8))
    gw = np.asarray(L0_ffn_moe_gate_w, np.float32)
    gatew_t = np.ascontiguousarray(
        gw.reshape(8, 128, 8).transpose(1, 0, 2).reshape(128, 64))
    gcw = np.asarray(L0_ffn_gc_w, np.float32)
    gcw1b = np.ascontiguousarray(np.broadcast_to(gcw[:C, 0][None, :], (128, C)))
    gcw2b = np.ascontiguousarray(np.broadcast_to(gcw[C:, 0][None, :], (128, C)))
    gcbb = np.full((128, 1), float(np.asarray(L0_ffn_gc_b).reshape(())), np.float32)

    shared = dict(consts)
    shared.update(
        W_attn=np.ascontiguousarray(np.asarray(L0_attn_W_attn, np.float32)),
        W_proj=np.ascontiguousarray(np.asarray(L0_attn_W_proj, np.float32)),
        abel_w=np.ascontiguousarray(np.asarray(L0_ffn_abel_w, np.float32)),
        ew1=np.ascontiguousarray(np.asarray(L0_ffn_ew1, np.float32).reshape(NE * C, HD)),
        ew2=np.ascontiguousarray(np.asarray(L0_ffn_ew2, np.float32).reshape(NE * HD, C)),
        gens=np.ascontiguousarray(np.asarray(L0_ffn_gens, np.float32).reshape(3 * C, C)),
        gamma_t=gamma_t, beta_t=beta_t, eb1t=eb1t, eb2row=eb2row,
        abelbrow=abelbrow, gatebrow=gatebrow, gatew_t=gatew_t,
        gcw1b=gcw1b, gcw2b=gcw2b, gcbb=gcbb,
    )
    nc, _dbg = build()
    in_maps = []
    for c in range(8):
        m = dict(shared)
        m["x"] = np.ascontiguousarray(x[2 * c:2 * c + 2].reshape(B_PER * T, C))
        in_maps.append(m)
    res = run_bass_kernel_spmd(nc, in_maps, core_ids=list(range(8)))
    outs = [r["out"].reshape(B_PER, T, C) for r in res.results]
    return np.concatenate(outs, axis=0)


# version 16
# speedup vs baseline: 1.1232x; 1.0756x over previous
import contextlib
import math
import numpy as np
import concourse.bass as bass
import concourse.mybir as mybir
from concourse.bass_utils import run_bass_kernel_spmd

F32 = mybir.dt.float32
BF16 = mybir.dt.bfloat16
AF = mybir.ActivationFunctionType
ALU = mybir.AluOpType
AX = mybir.AxisListType

B_PER, T, C, NH, HS, NL, NE, HD, SLAB = 2, 2048, 1024, 16, 64, 64, 8, 2048, 1024
NSLAB = (B_PER * T) // SLAB
EPS = 1e-5

_STOP_AFTER = None
_DBG_REQ = []
_N_ITER = 6
_N_EXPN = NH
_DO_PNV = True
_DO_DEN = True


class Ctr:
    __slots__ = ("h", "n")

    def __init__(self, h):
        self.h = h
        self.n = 0


class Emit:
    def __init__(self, blk):
        self.blk = blk

    def t(self, f):
        self.blk.tensor(f)

    def v(self, f):
        self.blk.vector(f)

    def s(self, f):
        self.blk.scalar(f)

    def q(self, f):
        self.blk.sync(f)

    def g(self, f):
        self.blk.gpsimd(f)


def build():
    nc = bass.Bass("TRN2", target_bir_lowering=False)
    ap = {}

    def din(name, shape, dt):
        ap[name] = nc.dram_tensor(name, shape, dt, kind="ExternalInput").ap()

    din("x", [B_PER * T, C], F32)
    din("W_attn", [C, 3 * C], F32)
    din("W_proj", [C, C], F32)
    din("abel_w", [C, C], F32)
    din("ew1", [NE * C, HD], F32)
    din("ew2", [NE * HD, C], F32)
    din("gens", [3 * C, C], F32)
    din("idb", [128, 128], BF16)
    din("idf", [128, 128], F32)
    din("onescol_b", [128, 1], BF16)
    din("onescol_f", [128, 1], F32)
    din("onesrow_b", [1, 128], BF16)
    din("onesrow_f", [1, 64], F32)
    din("eye7w", [128, 512], F32)
    din("eye15w", [128, 512], F32)
    din("eye13w", [128, 512], F32)
    din("eyeA", [128, 8 * C], BF16)
    din("cid0", [128, 1024], F32)
    din("cid1", [128, 1024], F32)
    din("bc2", [2, 128], F32)
    din("gcw1b", [128, C], F32)
    din("gcw2b", [128, C], F32)
    din("gcbb", [128, 1], F32)
    din("epsA", [128, 1], F32)
    din("epsB", [128, 1], F32)
    din("gamma_t", [128, 8], F32)
    din("beta_t", [128, 8], F32)
    din("eb1t", [128, 128], F32)
    din("eb2row", [1, NE * C], F32)
    din("abelbrow", [1, C], F32)
    din("gatebrow", [1, 8], F32)
    din("gatew_t", [128, 64], F32)

    out_ap = nc.dram_tensor("out", [B_PER * T, C], F32, kind="ExternalOutput").ap()
    s_x2 = nc.dram_tensor("s_x2", [B_PER * T, C], F32, kind="Internal").ap()
    s_eL = nc.dram_tensor("s_eL", [8 * 128, T], BF16, kind="Internal").ap()
    s_abel = nc.dram_tensor("s_abel", [B_PER * T, C], BF16, kind="Internal").ap()
    s_wab = nc.dram_tensor("s_wab", [C, 4 * C], BF16, kind="Internal").ap()
    s_ew1b = nc.dram_tensor("s_ew1b", [NE * C, HD], BF16, kind="Internal").ap()
    s_ew2b = nc.dram_tensor("s_ew2b", [NE * HD, C], BF16, kind="Internal").ap()

    dbg_aps = {}
    dbg_live = {}
    _dbg_shapes = {
        "u0": ([128, 8 * T], BF16), "k_sb": ([128, 8 * T], BF16),
        "v_sb": ([128, 16 * C], BF16), "klf": ([128, 512], F32),
        "qlf": ([128, 512], F32), "pair": ([128, T], BF16),
        "recipL": ([128, 256], F32), "S_all": ([64, 1024], F32),
        "V_all": ([64, 1024], F32), "U_all": ([64, 1024], F32),
        "PNv": ([128, 512], BF16), "y_sb": ([128, 16 * C], BF16),
        "stats": ([128, 128], F32), "R": ([128, 8 * C], BF16),
        "wd": ([128, 256], F32), "gd1": ([128, 32], F32),
        "acc": ([128, 8 * C], F32), "xn2s": ([128, 8 * SLAB], BF16),
        "hsb": ([128, 16 * SLAB], BF16),
    }
    for nm in _DBG_REQ:
        shp, dt = _dbg_shapes[nm]
        dbg_aps[nm] = nc.dram_tensor("dbg_" + nm, shp, dt, kind="ExternalOutput").ap()

    st = contextlib.ExitStack()
    with st:
        E = st.enter_context
        idb = E(nc.sbuf_tensor("c_idb", [128, 128], BF16))
        idf = E(nc.sbuf_tensor("c_idf", [128, 128], F32))
        onescol_b = E(nc.sbuf_tensor("c_ocb", [128, 1], BF16))
        onescol_f = E(nc.sbuf_tensor("c_ocf", [128, 1], F32))
        onesrow_b = E(nc.sbuf_tensor("c_orb", [1, 128], BF16))
        gcw2b = E(nc.sbuf_tensor("c_g2", [128, C], BF16))
        gcbb = E(nc.sbuf_tensor("c_gcb", [128, 1], F32))
        epsA = E(nc.sbuf_tensor("c_eA", [128, 1], F32))
        epsB = E(nc.sbuf_tensor("c_eB", [128, 1], F32))
        gamma_t = E(nc.sbuf_tensor("c_gam", [128, 8], F32))
        beta_t = E(nc.sbuf_tensor("c_bet", [128, 8], F32))
        eb1t = E(nc.sbuf_tensor("c_eb1", [128, 128], F32))
        R_bf = E(nc.sbuf_tensor("p_R", [128, 8 * C], BF16))
        stats = E(nc.sbuf_tensor("p_st", [128, 128], F32))
        wd_sb = E(nc.sbuf_tensor("p_wd", [128, 32 * 8], F32))
        gdot1 = E(nc.sbuf_tensor("p_gd1", [128, 32], F32))
        lnst = E(nc.sbuf_tensor("p_ln", [128, 8], F32))
        estf = [E(nc.sbuf_tensor(f"c_ef{i}", [128, 1024], F32)) for i in range(2)]
        estb = [E(nc.sbuf_tensor(f"c_esb{i}", [128, 1024], BF16)) for i in range(2)]
        ps = [E(nc.psum_tensor(f"ps{i}", [128, 512], F32)) for i in range(6)]
        psb = [E(nc.psum_tensor(f"psb{i}", [128, 1024], BF16)) for i in range(2)]

        dS = Ctr(E(nc.semaphore("dS")))
        dG = Ctr(E(nc.semaphore("dG")))
        dE = Ctr(E(nc.semaphore("dE")))
        dP = Ctr(E(nc.semaphore("dP")))
        dQ = [Ctr(E(nc.semaphore("dQ0"))), Ctr(E(nc.semaphore("dQ1")))]
        dW0 = Ctr(E(nc.semaphore("dW0")))
        dW1 = Ctr(E(nc.semaphore("dW1")))
        dX = [Ctr(E(nc.semaphore("dX0"))), Ctr(E(nc.semaphore("dX1")))]
        oX = [Ctr(E(nc.semaphore("oX0"))), Ctr(E(nc.semaphore("oX1")))]
        pc = [Ctr(E(nc.semaphore("pc0"))), Ctr(E(nc.semaphore("pc1")))]
        pL = Ctr(E(nc.semaphore("pL")))
        pLv = Ctr(E(nc.semaphore("pLv")))
        pd = [Ctr(E(nc.semaphore("pd0"))), Ctr(E(nc.semaphore("pd1")))]
        eW = [Ctr(E(nc.semaphore("eW0"))), Ctr(E(nc.semaphore("eW1")))]
        vV = Ctr(E(nc.semaphore("vV")))
        sS = Ctr(E(nc.semaphore("sS")))
        hT = Ctr(E(nc.semaphore("hT")))
        hE = Ctr(E(nc.semaphore("hE")))
        oO = Ctr(E(nc.semaphore("oO")))

        def acc_group(t, out_sl, pairs, tp=None):
            n = len(pairs)
            for i, (lh, rh) in enumerate(pairs):
                ins = t.matmul(out_sl, lh, rh, start=(i == 0), stop=(i == n - 1),
                               tile_position=tp)
                if i == n - 1:
                    ins.then_inc(hT.h, 1)

        def end_fence(em):
            vals = [(c.h, c.n) for c in (dS, dG, dE, dP, dQ[0], dQ[1],
                                         dW0, dW1, dX[0], dX[1],
                                         oX[0], oX[1], pc[0], pc[1], eW[0], eW[1],
                                         pd[0], pd[1], pL, pLv, oO)]

            def _(q):
                for h_, n_ in vals:
                    q.wait_ge(h_, n_)
            em.q(_)

        def stop_block(phase):
            if _STOP_AFTER != phase:
                return False
            with nc.Block() as blk:
                em = Emit(blk)

                def _(q):
                    q.wait_ge(hT.h, hT.n)
                    q.wait_ge(vV.h, vV.n)
                    q.wait_ge(sS.h, sS.n)
                    q.wait_ge(hE.h, hE.n)
                    q.wait_ge(dG.h, dG.n)
                    q.wait_ge(dS.h, dS.n)
                    q.wait_ge(dE.h, dE.n)
                    q.wait_ge(dP.h, dP.n)
                    q.wait_ge(dQ[0].h, dQ[0].n)
                    q.wait_ge(dQ[1].h, dQ[1].n)
                    q.wait_ge(dW0.h, dW0.n)
                    q.wait_ge(dW1.h, dW1.n)
                    for c_ in dX + oX + pc + eW:
                        q.wait_ge(c_.h, c_.n)
                    q.wait_ge(oO.h, oO.n)
                    for nm in _DBG_REQ:
                        src = dbg_live.get(nm)
                        if src is None:
                            continue
                        q.dma_start(out=dbg_aps[nm], in_=src).then_inc(oO.h, 16)
                        oO.n += 16
                    q.wait_ge(oO.h, oO.n)
                em.q(_)
            return True

        ew1v = ap["ew1"].rearrange("a (b c) -> (a b) c", c=1024)
        s1v = s_ew1b.rearrange("a (b c) -> (a b) c", c=1024)
        NEST = 256
        est_st = {"i": 0, "c": 0}
        bpdE = [pd[0].n, pd[1].n]
        bpLE = pL.n

        def emit_stream(em, n):
            i0, i1 = est_st["i"], min(est_st["i"] + n, NEST + 2)
            if i1 <= i0:
                return

            def _(g, i0=i0, i1=i1):
                for i in range(i0, i1):
                    if i >= 2:
                        j = i - 2
                        g.wait_ge(pL.h, bpLE + j + 1)
                        dst = (s1v[j * 128:(j + 1) * 128, :] if j < 128 else
                               s_ew2b[(j - 128) * 128:(j - 127) * 128, :])
                        g.dma_start(out=dst, in_=estb[j % 2][:, :]
                                    ).then_inc(pd[j % 2].h, 16)
                    if i < NEST:
                        srcs = (ew1v[i * 128:(i + 1) * 128, :] if i < 128 else
                                ap["ew2"][(i - 128) * 128:(i - 127) * 128, :])
                        g.dma_start(out=estf[i % 2][:, :], in_=srcs
                                    ).then_inc(pd[i % 2].h, 16)
            em.g(_)
            for i in range(i0, i1):
                if i >= 2:
                    pd[(i - 2) % 2].n += 16
                if i < NEST:
                    pd[i % 2].n += 16
            est_st["i"] = i1

        def emit_bg_casts(em, n):
            j0 = est_st["c"]
            j1 = min(j0 + n, est_st["i"], NEST)
            if j1 <= j0:
                return

            def _(v, j0=j0, j1=j1):
                for j in range(j0, j1):
                    v.wait_ge(pd[j % 2].h, bpdE[j % 2] + 32 * (j // 2) + 16)
                    v.tensor_copy(estb[j % 2][:, :], estf[j % 2][:, :])
                    v.drain().then_inc(pL.h, 1)
            em.v(_)
            pL.n += (j1 - j0)
            est_st["c"] = j1

        def emit_bg(em, n=2):
            emit_stream(em, n)
            emit_bg_casts(em, n)

        def emit_rebuild(em, si, xcb2, x2st, xn2s, sp, st):
            def emit_load_xcb(j):
                bi = si * 8 + j
                tg = bi * 128

                def _(q, bi=bi, tg=tg, wv=st["xc"].get(bi - 2, 0), sp=sp):
                    q.wait_ge(oX[0].h, sp[0])
                    q.wait_ge(oX[1].h, sp[1])
                    if wv:
                        q.wait_ge(vV.h, wv)
                    q.dma_start(out=x2st[bi % 2][:, :],
                                in_=s_x2[tg:tg + 128, :]).then_inc(dX[bi % 2].h, 16)
                em.q(_)
                dX[bi % 2].n += 16

                def _(v, bi=bi, w=dX[bi % 2].n):
                    v.wait_ge(dX[bi % 2].h, w)
                    s4 = 4 * bi
                    v.tensor_scalar(xcb2[bi % 2][:, :], x2st[bi % 2][:, :],
                                    stats[:, s4 + 2:s4 + 3], stats[:, s4 + 3:s4 + 4],
                                    op0=ALU.mult, op1=ALU.add)
                    v.drain().then_inc(vV.h, 1)
                em.v(_)
                vV.n += 1
                st["xc"][bi] = vV.n

            emit_load_xcb(0)
            for j in range(8):
                bi = si * 8 + j
                btr = hT.n
                gb7p = st.get("gb7", 0)
                bevf = vV.n + (1 if j < 7 else 0)

                def _(t, bi=bi, w=st["xc"][bi], gb7p=gb7p, bevf=bevf):
                    t.wait_ge(vV.h, w)
                    for dc in range(8):
                        if dc < 2:
                            if gb7p:
                                t.wait_ge(vV.h, gb7p - (1 - dc))
                        else:
                            t.wait_ge(vV.h, bevf + dc - 1)
                        t.transpose(psb[dc % 2][:, 0:128],
                                    xcb2[bi % 2][:, dc * 128:(dc + 1) * 128],
                                    idb[:, :]).then_inc(hT.h, 1)
                em.t(_)
                hT.n += 8

                if j < 7:
                    emit_load_xcb(j + 1)

                def _(v, j=j, btr=btr):
                    for dc in range(8):
                        v.wait_ge(hT.h, btr + dc + 1)
                        v.tensor_scalar(xn2s[:, dc * SLAB + j * 128:dc * SLAB + (j + 1) * 128],
                                        psb[dc % 2][:, 0:128],
                                        gamma_t[:, dc:dc + 1], beta_t[:, dc:dc + 1],
                                        op0=ALU.mult, op1=ALU.add)
                        v.drain().then_inc(vV.h, 1)
                em.v(_)
                vV.n += 8
                st["gb7"] = vV.n

        with contextlib.ExitStack() as b0st:
            E0 = b0st.enter_context
            wst = [E0(nc.sbuf_tensor(f"w0_f{i}", [128, 4 * C], F32)) for i in range(2)]
            wsb = [E0(nc.sbuf_tensor(f"w0_b{i}", [128, 4 * C], BF16)) for i in range(2)]
            with nc.Block() as blk:
                em = Emit(blk)
                bv0 = vV.n

                def _(q):
                    for t_sb, srcn in [(idb, "idb"), (idf, "idf"), (onescol_b, "onescol_b"),
                                       (onescol_f, "onescol_f"), (onesrow_b, "onesrow_b"),
                                       (gcbb, "gcbb"), (epsA, "epsA"),
                                       (epsB, "epsB"), (gamma_t, "gamma_t"),
                                       (beta_t, "beta_t"), (eb1t, "eb1t")]:
                        q.dma_start(out=t_sb[:, :], in_=ap[srcn]).then_inc(dS.h, 16)
                        dS.n += 16
                    for i in range(10):
                        p2 = i % 2
                        if i >= 2:
                            j = i - 2
                            q.wait_ge(vV.h, bv0 + j + 1)
                            q.dma_start(out=s_wab[j * 128:(j + 1) * 128, :],
                                        in_=wsb[p2][:, :]).then_inc(pc[p2].h, 16)
                        if i < 8:
                            q.dma_start(out=wst[p2][:, 0:3 * C],
                                        in_=ap["W_attn"][i * 128:(i + 1) * 128, :]
                                        ).then_inc(pc[p2].h, 16)
                            q.dma_start(out=wst[p2][:, 3 * C:4 * C],
                                        in_=ap["W_proj"][i * 128:(i + 1) * 128, :]
                                        ).then_inc(pc[p2].h, 16)
                em.q(_)
                for i in range(10):
                    p2 = i % 2
                    if i >= 2:
                        pc[p2].n += 16
                    if i < 8:
                        pc[p2].n += 32

                for i in range(8):
                    p2 = i % 2

                    def _(v, p2=p2, w=48 * (i // 2) + 32):
                        v.wait_ge(pc[p2].h, w)
                        v.tensor_copy(wsb[p2][:, :], wst[p2][:, :])
                        v.drain().then_inc(vV.h, 1)
                    em.v(_)
                    vV.n += 1

                def _(g):
                    for t_sb, srcn in [(gcw2b, "gcw2b")]:
                        g.dma_start(out=t_sb[:, :], in_=ap[srcn]).then_inc(dG.h, 16)
                        dG.n += 16
                em.g(_)

                def _(t, w1=dS.n, w2=dG.n):
                    t.wait_ge(dS.h, w1)
                    t.wait_ge(dG.h, w2)
                em.t(_)
                end_fence(em)

        with contextlib.ExitStack() as ast:
            Ea = ast.enter_context
            Wb = [Ea(nc.sbuf_tensor(f"a_Wb{i}", [128, 8 * C], BF16)) for i in range(2)]
            eye7w = Ea(nc.sbuf_tensor("c_e7", [128, 512], F32))
            eye15w = Ea(nc.sbuf_tensor("c_e15", [128, 512], F32))
            eye13w = Ea(nc.sbuf_tensor("c_e13", [128, 512], F32))
            onesrow_f64 = Ea(nc.sbuf_tensor("c_orf", [1, 64], F32))
            k_sb = Ea(nc.sbuf_tensor("a_k", [128, 8 * T], BF16))
            v_sb = Ea(nc.sbuf_tensor("a_v", [128, 16 * C], BF16))
            q_l_f = Ea(nc.sbuf_tensor("a_qlf", [128, 512], F32))
            k_l_f = Ea(nc.sbuf_tensor("a_klf", [128, 512], F32))
            q_l_b = Ea(nc.sbuf_tensor("a_qlb", [128, 512], BF16))
            k_l_b = Ea(nc.sbuf_tensor("a_klb", [128, 512], BF16))
            U_all = Ea(nc.sbuf_tensor("a_U", [64, 1024], F32))
            PNv_b = Ea(nc.sbuf_tensor("a_PNv", [128, 512], BF16))
            recipL = Ea(nc.sbuf_tensor("a_rL", [128, 256], F32))
            rN = Ea(nc.sbuf_tensor("a_rN", [64, 16], F32))

            for b in range(B_PER):
                t0g = b * T

                with contextlib.ExitStack() as a1:
                    E1 = a1.enter_context
                    u0 = E1(nc.sbuf_tensor(f"a_u0_{b}", [128, 8 * T], BF16))
                    pairs = [E1(nc.sbuf_tensor(f"a_pair_{b}{i}", [128, T], BF16))
                             for i in range(2)]
                    tbufs = [E1(nc.sbuf_tensor(f"a_tb_{b}{i}", [128, T], BF16))
                             for i in range(2)]
                    xst = [E1(nc.sbuf_tensor(f"a_x{b}{i}", [128, C], F32)) for i in range(2)]
                    xnt_t = E1(nc.sbuf_tensor(f"a_n{b}", [128, C], BF16))
                    xnt = [xnt_t, xnt_t]

                    with nc.Block() as blk:
                        em = Emit(blk)
                        bx = dS.n
                        bv0 = vV.n

                        bX = [dX[0].n, dX[1].n]

                        def _(q, b=b, bX=bX):
                            if b == 0:
                                for t_sb, srcn in [(eye7w, "eye7w"), (eye15w, "eye15w"),
                                                   (eye13w, "eye13w"), (onesrow_f64, "onesrow_f")]:
                                    q.dma_start(out=t_sb[:, :], in_=ap[srcn]
                                                ).then_inc(dE.h, 16)
                            for i in range(16):
                                if i >= 2:
                                    q.wait_ge(vV.h, bv0 + 4 * (i - 1))
                                q.dma_start(out=xst[i % 2][:, :],
                                            in_=ap["x"][t0g + i * 128:t0g + (i + 1) * 128, :]
                                            ).then_inc(dX[i % 2].h, 16)
                        em.q(_)
                        if b == 0:
                            dE.n += 64
                        dX[0].n += 128
                        dX[1].n += 128

                        bW0 = dW0.n
                        bW1 = dW1.n
                        btM0 = hT.n

                        def _(g):
                            for dc in range(8):
                                g.dma_start(out=Wb[0][:, dc * C:(dc + 1) * C],
                                            in_=s_wab[dc * 128:(dc + 1) * 128, C:2 * C]
                                            ).then_inc(dW0.h, 16)
                            for dc in range(8):
                                g.dma_start(out=Wb[1][:, dc * C:(dc + 1) * C],
                                            in_=s_wab[dc * 128:(dc + 1) * 128, 0:C]
                                            ).then_inc(dW1.h, 16)
                            g.wait_ge(hT.h, btM0 + 128 + 32)
                            for dc in range(8):
                                g.dma_start(out=Wb[0][:, dc * C:(dc + 1) * C],
                                            in_=s_wab[dc * 128:(dc + 1) * 128, 2 * C:3 * C]
                                            ).then_inc(dW0.h, 16)
                            g.wait_ge(hT.h, btM0 + 160 + 7 * 44 + 4)
                            for dc in range(8):
                                g.dma_start(out=Wb[1][:, dc * C:(dc + 1) * C],
                                            in_=s_wab[dc * 128:(dc + 1) * 128, 3 * C:4 * C]
                                            ).then_inc(dW1.h, 16)
                        em.g(_)
                        dW0.n += 256
                        dW1.n += 256

                        bhT = hT.n
                        bhE = hE.n
                        for i in range(16):
                            def _(v, i=i):
                                v.wait_ge(dX[i % 2].h, bX[i % 2] + 16 * (i // 2 + 1))
                                v.reduce_sum(lnst[:, 0:1], xst[i % 2][:, :], axis=AX.X)
                                v.drain()
                                v.tensor_scalar(lnst[:, 1:2], lnst[:, 0:1], -1.0 / C, None,
                                                op0=ALU.mult)
                                v.drain().then_inc(vV.h, 1)
                            em.v(_)
                            vV.n += 1

                            def _(s, i=i):
                                s.wait_ge(dX[i % 2].h, bX[i % 2] + 16 * (i // 2 + 1))
                                s.activation(xnt[i % 2][:, :], xst[i % 2][:, :], AF.Square,
                                             scale=1.0 / 32.0, accum_out=lnst[:, 3:4])
                                s.drain().then_inc(sS.h, 1)
                            em.s(_)
                            sS.n += 1

                            def _(v, w=sS.n):
                                v.wait_ge(sS.h, w)
                                v.tensor_scalar(lnst[:, 2:3], lnst[:, 1:2], -1.0, None,
                                                op0=ALU.mult)
                                v.drain()
                                v.scalar_tensor_tensor(lnst[:, 4:5], lnst[:, 2:3], lnst[:, 1:2],
                                                       lnst[:, 3:4], op0=ALU.mult, op1=ALU.add)
                                v.drain().then_inc(vV.h, 1)
                            em.v(_)
                            vV.n += 1

                            def _(s, w=vV.n):
                                s.wait_ge(vV.h, w)
                                s.activation(lnst[:, 5:6], lnst[:, 4:5], AF.Ln, bias=epsA[:, 0:1])
                                s.drain()
                                s.activation(lnst[:, 6:7], lnst[:, 5:6], AF.Exp, scale=-0.5)
                                s.drain().then_inc(sS.h, 1)
                            em.s(_)
                            sS.n += 1

                            def _(v, i=i, w=sS.n):
                                v.wait_ge(sS.h, w)
                                if i >= 1:
                                    v.wait_ge(hT.h, bhT + 8 * i)
                                v.tensor_mul(lnst[:, 7:8], lnst[:, 1:2], lnst[:, 6:7])
                                v.drain()
                                v.tensor_scalar(xnt[i % 2][:, :], xst[i % 2][:, :],
                                                lnst[:, 6:7], lnst[:, 7:8],
                                                op0=ALU.mult, op1=ALU.add)
                                v.drain().then_inc(vV.h, 2)
                            em.v(_)
                            vV.n += 2

                            def _(t, i=i, w=vV.n):
                                t.wait_ge(vV.h, w)
                                for dc in range(8):
                                    j = i * 8 + dc
                                    t.wait_ge(hE.h, bhE + max(0, j - 1))
                                    t.transpose(psb[j % 2][:, 0:128],
                                                xnt[i % 2][:, dc * 128:(dc + 1) * 128],
                                                idb[:, :]).then_inc(hT.h, 1)
                            em.t(_)
                            hT.n += 8

                            def _(s, i=i):
                                for dc in range(8):
                                    j = i * 8 + dc
                                    s.wait_ge(hT.h, bhT + j + 1)
                                    s.copy(u0[:, dc * T + i * 128:dc * T + (i + 1) * 128],
                                           psb[j % 2][:, 0:128])
                                    s.drain().then_inc(hE.h, 1)
                            em.s(_)
                            hE.n += 8

                        bkT = hT.n
                        bkE = hE.n

                        def _(t):
                            t.wait_ge(hE.h, bkE)
                            t.wait_ge(dW0.h, bW0 + 128)
                            for u in range(32):
                                fc, ts = u // 4, u % 4
                                t.wait_ge(hE.h, bkE + max(0, u - 1))
                                acc_group(t, ps[u % 2][:, :],
                                          [(Wb[0][:, dc * C + fc * 128:dc * C + (fc + 1) * 128],
                                            u0[:, dc * T + ts * 512:dc * T + (ts + 1) * 512])
                                           for dc in range(8)])
                        em.t(_)
                        hT.n += 32

                        def _(s):
                            for u in range(32):
                                fc, ts = u // 4, u % 4
                                s.wait_ge(hT.h, bkT + u + 1)
                                s.copy(k_sb[:, fc * T + ts * 512:fc * T + (ts + 1) * 512],
                                       ps[u % 2][:, :])
                                s.drain().then_inc(hE.h, 1)
                        em.s(_)
                        hE.n += 32

                        def _(v, w=hE.n):
                            v.wait_ge(hE.h, w)
                            for fc in range(8):
                                v.reduce_sum(k_l_f[:, fc * 64:(fc + 1) * 64],
                                             k_sb[:, fc * T:(fc + 1) * T]
                                             .rearrange("p (l g) -> p l g", g=32), axis=AX.X)
                            v.drain()
                            v.tensor_copy(k_l_b[:, :], k_l_f[:, :])
                            v.drain().then_inc(vV.h, 1)
                        em.v(_)
                        vV.n += 1
                        klb_ready = vV.n

                        bqv = vV.n
                        bdQ = [dQ[0].n, dQ[1].n]
                        qb = {}

                        def emit_qmm(fc):
                            qb[fc] = (hT.n, hE.n)
                            bqE = hE.n

                            def _(t, fc=fc, bqE=bqE):
                                if fc == 0:
                                    t.wait_ge(dW1.h, bW1 + 128)
                                for ts in range(4):
                                    if fc == 0 or ts >= 2:
                                        t.wait_ge(hE.h, bqE + max(0, ts - 1))
                                    else:
                                        t.wait_ge(hE.h, bqE - 9 + ts)
                                    acc_group(t, ps[ts % 2][:, :],
                                              [(Wb[1][:, dc * C + fc * 128:dc * C + (fc + 1) * 128],
                                                u0[:, dc * T + ts * 512:dc * T + (ts + 1) * 512])
                                               for dc in range(8)])
                            em.t(_)
                            hT.n += 4

                        emit_qmm(0)
                        for fc in range(8):
                            tbuf = tbufs[fc % 2]
                            bqT, bqE = qb[fc]

                            def _(s, fc=fc, bqT=bqT, tbuf=tbuf):
                                if fc >= 2:
                                    s.wait_ge(vV.h, bqv + 2 * (fc - 2) + 1)
                                for ts in range(4):
                                    s.wait_ge(hT.h, bqT + ts + 1)
                                    s.copy(tbuf[:, ts * 512:(ts + 1) * 512], ps[ts % 2][:, :])
                                    s.drain().then_inc(hE.h, 1)
                            em.s(_)
                            hE.n += 4

                            def _(v, fc=fc, w=hE.n):
                                v.wait_ge(hE.h, w)
                                v.reduce_sum(q_l_f[:, fc * 64:(fc + 1) * 64],
                                             tbuf[:, :].rearrange("p (l g) -> p l g", g=32),
                                             axis=AX.X)
                                v.drain()
                                v.tensor_copy(q_l_b[:, fc * 64:(fc + 1) * 64],
                                              q_l_f[:, fc * 64:(fc + 1) * 64])
                                v.drain().then_inc(vV.h, 1)
                            em.v(_)
                            vV.n += 1

                            beT = hT.n
                            beE = hE.n

                            def _(t, fc=fc, beE=beE, w=hE.n):
                                t.wait_ge(vV.h, klb_ready)
                                t.wait_ge(hE.h, w)
                                for hh in range(2):
                                    p = ((2 * fc + hh) % 2) * 64
                                    for ts in range(4):
                                        u = hh * 4 + ts
                                        t.wait_ge(hE.h, beE + max(0, u - 1))
                                        t.matmul(ps[2 + (u % 2)][p:p + 64, :],
                                                 k_l_b[p:p + 64, fc * 64:(fc + 1) * 64],
                                                 tbuf[p:p + 64, ts * 512:(ts + 1) * 512],
                                                 start=True, stop=True,
                                                 tile_position=(p, p)).then_inc(hT.h, 1)
                            em.t(_)
                            hT.n += 8

                            def _(s, fc=fc, beT=beT, wd_=bdQ[fc % 2] + 16 * (fc // 2)):
                                s.wait_ge(dQ[fc % 2].h, wd_)
                                for hh in range(2):
                                    p = ((2 * fc + hh) % 2) * 64
                                    for ts in range(4):
                                        u = hh * 4 + ts
                                        s.wait_ge(hT.h, beT + u + 1)
                                        s.activation(pairs[fc % 2][p:p + 64, ts * 512:(ts + 1) * 512],
                                                     ps[2 + (u % 2)][p:p + 64, :],
                                                     AF.Exp, scale=1.0 / 32.0)
                                        s.drain().then_inc(hE.h, 1)
                            em.s(_)
                            hE.n += 8

                            if fc + 1 < 8:
                                emit_qmm(fc + 1)

                            def _(t, fc=fc, w=hE.n, wv=vV.n):
                                t.wait_ge(hE.h, w)
                                t.wait_ge(vV.h, wv)
                                for hh in range(2):
                                    h = 2 * fc + hh
                                    p = (h % 2) * 64
                                    for tc in range(16):
                                        t.matmul(ps[4][:, hh * 16 + tc:hh * 16 + tc + 1],
                                                 pairs[fc % 2][p:p + 64, tc * 128:(tc + 1) * 128],
                                                 onescol_b[p:p + 64, 0:1],
                                                 start=True, stop=True).then_inc(hT.h, 1)
                            em.t(_)
                            hT.n += 32

                            def _(v, fc=fc, w=hT.n):
                                v.wait_ge(hT.h, w)
                                for hh in range(2):
                                    h = 2 * fc + hh
                                    v.reciprocal(recipL[:, h * 16:(h + 1) * 16],
                                                 ps[4][:, hh * 16:hh * 16 + 16])
                                v.drain().then_inc(vV.h, 1)
                            em.v(_)
                            vV.n += 1

                            def _(q, fc=fc, w=vV.n):
                                q.wait_ge(vV.h, w)
                                q.dma_start(out=s_eL[fc * 128:(fc + 1) * 128, :],
                                            in_=pairs[fc % 2][:, :]).then_inc(dQ[fc % 2].h, 16)
                            em.q(_)
                            dQ[fc % 2].n += 16

                        bvT = hT.n
                        bvE = hE.n

                        def _(t):
                            t.wait_ge(dW0.h, bW0 + 256)
                            for u in range(32):
                                tc, n2 = u // 2, u % 2
                                t.wait_ge(hE.h, bvE + max(0, u - 1))
                                acc_group(t, ps[u % 2][:, :],
                                          [(u0[:, dc * T + tc * 128:dc * T + (tc + 1) * 128],
                                            Wb[0][:, dc * C + n2 * 512:dc * C + (n2 + 1) * 512])
                                           for dc in range(8)])
                        em.t(_)
                        hT.n += 32

                        def _(s):
                            for u in range(32):
                                tc, n2 = u // 2, u % 2
                                s.wait_ge(hT.h, bvT + u + 1)
                                s.copy(v_sb[:, tc * C + n2 * 512:tc * C + (n2 + 1) * 512],
                                       ps[u % 2][:, :])
                                s.drain().then_inc(hE.h, 1)
                        em.s(_)
                        hE.n += 32
                        end_fence(em)

                    if b == 0:
                        dbg_live.update(u0=u0[:, :], k_sb=k_sb[:, :], v_sb=v_sb[:, :],
                                        klf=k_l_f[:, :], qlf=q_l_f[:, :], pair=pairs[1][:, :],
                                        recipL=recipL[:, :])
                        if stop_block("a1b0"):
                            return nc, dbg_aps

                with contextlib.ExitStack() as a2:
                    E2 = a2.enter_context
                    ebuf = E2(nc.sbuf_tensor(f"a_eN_{b}", [128, 2048], BF16))
                    iS = E2(nc.sbuf_tensor(f"i_S_{b}", [64, 1024], F32))
                    iST = E2(nc.sbuf_tensor(f"i_ST_{b}", [64, 1024], F32))
                    iV = E2(nc.sbuf_tensor(f"i_V_{b}", [64, 1024], F32))
                    iVT = E2(nc.sbuf_tensor(f"i_VT_{b}", [64, 1024], F32))
                    iKVT = E2(nc.sbuf_tensor(f"i_KVT_{b}", [64, 1024], F32))
                    iT1 = E2(nc.sbuf_tensor(f"i_T1_{b}", [64, 1024], F32))
                    iT2 = E2(nc.sbuf_tensor(f"i_T2_{b}", [64, 1024], F32))
                    csum = E2(nc.sbuf_tensor(f"i_cs_{b}", [64, 16], F32))
                    rd = E2(nc.sbuf_tensor(f"i_rd_{b}", [16, 4], F32))
                    rdT = E2(nc.sbuf_tensor(f"i_rdT_{b}", [1, 16], F32))
                    denb = E2(nc.sbuf_tensor(f"i_db_{b}", [64, 16], F32))

                    with nc.Block() as blk:
                        em = Emit(blk)

                        def _(t):
                            for h in range(NH):
                                p = (h % 2) * 64
                                g2, j = h % 2, h // 2
                                t.matmul(ps[g2][0:64, j * 64:(j + 1) * 64],
                                         q_l_f[p:p + 64, (h // 2) * 64:(h // 2 + 1) * 64],
                                         k_l_f[p:p + 64, (h // 2) * 64:(h // 2 + 1) * 64],
                                         start=True, stop=True,
                                         tile_position=(p, 0)).then_inc(hT.h, 1)
                        em.t(_)
                        hT.n += 16

                        def _(s, w=hT.n):
                            s.wait_ge(hT.h, w)
                            import os as _os
                            if _os.environ.get("NOSTRIDE"):
                                s.activation(iS[:, 0:512], ps[0][0:64, :],
                                             AF.Exp, scale=1.0 / 1024.0)
                                s.drain()
                                s.activation(iS[:, 512:1024], ps[1][0:64, :],
                                             AF.Exp, scale=1.0 / 1024.0)
                            else:
                                iSv = iS[:, :].rearrange("p (a q m) -> p a q m", q=2, m=64)
                                s.activation(iSv[:, :, 0, :],
                                             ps[0][0:64, :].rearrange("p (a m) -> p a m", m=64),
                                             AF.Exp, scale=1.0 / 1024.0)
                                s.drain()
                                s.activation(iSv[:, :, 1, :],
                                             ps[1][0:64, :].rearrange("p (a m) -> p a m", m=64),
                                             AF.Exp, scale=1.0 / 1024.0)
                            s.drain().then_inc(sS.h, 1)
                        em.s(_)
                        sS.n += 1

                        def _(v, w=sS.n, we=dE.n):
                            v.wait_ge(sS.h, w)
                            v.wait_ge(dE.h, we)
                            v.reduce_sum(csum[:, 0:16],
                                         iS[:, :].rearrange("p (j m) -> p j m", m=64),
                                         axis=AX.X)
                            v.drain()
                            v.reciprocal(csum[:, 0:16], csum[:, 0:16])
                            v.drain()
                            for h in range(NH):
                                v.tensor_scalar(iS[:, h * 64:(h + 1) * 64],
                                                iS[:, h * 64:(h + 1) * 64],
                                                csum[:, h:h + 1], None, op0=ALU.mult)
                            v.drain().then_inc(vV.h, 1)
                        em.v(_)
                        vV.n += 1

                        def _(t, w=vV.n):
                            t.wait_ge(vV.h, w)
                            for h in range(NH):
                                g2, j = 2 + h // 8, h % 8
                                t.transpose(ps[g2][0:64, j * 64:(j + 1) * 64],
                                            iS[:, h * 64:(h + 1) * 64],
                                            idf[0:64, 0:64]).then_inc(hT.h, 1)
                        em.t(_)
                        hT.n += 16

                        def _(v, w=hT.n):
                            v.wait_ge(hT.h, w)
                            v.tensor_copy(iST[:, 0:512], ps[2][0:64, :])
                            v.tensor_copy(iST[:, 512:1024], ps[3][0:64, :])
                            v.drain()
                            v.reduce_sum(csum[:, 0:16],
                                         iST[:, :].rearrange("p (j m) -> p j m", m=64),
                                         axis=AX.X)
                            v.drain().then_inc(vV.h, 1)
                        em.v(_)
                        vV.n += 1

                        def _(t, w=vV.n):
                            t.wait_ge(vV.h, w)
                            t.transpose(ps[2][0:16, 0:64], csum[:, 0:16], idf[0:64, 0:64]
                                        ).then_inc(hT.h, 1)
                        em.t(_)
                        hT.n += 1

                        def _(v, w=hT.n):
                            v.wait_ge(hT.h, w)
                            v.reduce_max(rd[0:16, 0:1], ps[2][0:16, 0:64], axis=AX.X)
                            v.drain()
                            v.reciprocal(rd[0:16, 0:1], rd[0:16, 0:1])
                            v.drain().then_inc(vV.h, 1)
                        em.v(_)
                        vV.n += 1

                        def _(t, w=vV.n):
                            t.wait_ge(vV.h, w)
                            t.transpose(ps[3][0:1, 0:16], rd[0:16, 0:1], idf[0:16, 0:16]
                                        ).then_inc(hT.h, 1)
                        em.t(_)
                        hT.n += 1

                        def _(v, w=hT.n):
                            v.wait_ge(hT.h, w)
                            v.tensor_copy(rdT[0:1, 0:16], ps[3][0:1, 0:16])
                            v.drain().then_inc(vV.h, 1)
                        em.v(_)
                        vV.n += 1

                        def _(t, w=vV.n):
                            t.wait_ge(vV.h, w)
                            t.matmul(ps[2][0:64, 0:16], onesrow_f64[0:1, 0:64],
                                     rdT[0:1, 0:16], start=True, stop=True
                                     ).then_inc(hT.h, 1)
                        em.t(_)
                        hT.n += 1

                        def _(v, w=hT.n):
                            v.wait_ge(hT.h, w)
                            v.tensor_copy(denb[:, 0:16], ps[2][0:64, 0:16])
                            v.drain()
                            for h in range(NH):
                                v.tensor_scalar(iV[:, h * 64:(h + 1) * 64],
                                                iST[:, h * 64:(h + 1) * 64],
                                                denb[:, h:h + 1], None, op0=ALU.mult)
                                v.tensor_scalar(iVT[:, h * 64:(h + 1) * 64],
                                                iS[:, h * 64:(h + 1) * 64],
                                                denb[:, h:h + 1], None, op0=ALU.mult)
                            v.drain().then_inc(vV.h, 1)
                        em.v(_)
                        vV.n += 1

                        def istep_mm(t, dst0, lhs_t, rhs_t):
                            for h in range(NH):
                                g2, j = dst0 + h // 8, h % 8
                                t.matmul(ps[g2][0:64, j * 64:(j + 1) * 64],
                                         lhs_t[:, h * 64:(h + 1) * 64],
                                         rhs_t[:, h * 64:(h + 1) * 64],
                                         start=True, stop=True).then_inc(hT.h, 1)

                        def istep_tr(t, dst0, src_t):
                            for h in range(NH):
                                g2, j = dst0 + h // 8, h % 8
                                t.transpose(ps[g2][0:64, j * 64:(j + 1) * 64],
                                            src_t[:, h * 64:(h + 1) * 64],
                                            idf[0:64, 0:64]).then_inc(hT.h, 1)

                        for itn in range(_N_ITER):
                            def _(t, w=vV.n):
                                t.wait_ge(vV.h, w)
                                istep_mm(t, 2, iST, iV)
                                istep_mm(t, 4, iV, iST)
                            em.t(_)
                            hT.n += 32

                            def _(v, w=hT.n):
                                v.wait_ge(hT.h, w)
                                v.scalar_tensor_tensor(iT1[:, 0:512], ps[2][0:64, :], -1.0,
                                                       eye7w[0:64, :], op0=ALU.mult, op1=ALU.add)
                                v.scalar_tensor_tensor(iT1[:, 512:1024], ps[3][0:64, :], -1.0,
                                                       eye7w[0:64, :], op0=ALU.mult, op1=ALU.add)
                                v.tensor_copy(iKVT[:, 0:512], ps[4][0:64, :])
                                v.tensor_copy(iKVT[:, 512:1024], ps[5][0:64, :])
                                v.drain().then_inc(vV.h, 1)
                            em.v(_)
                            vV.n += 1

                            def _(t, w=vV.n):
                                t.wait_ge(vV.h, w)
                                istep_mm(t, 2, iKVT, iT1)
                            em.t(_)
                            hT.n += 16

                            def _(v, w=hT.n):
                                v.wait_ge(hT.h, w)
                                v.scalar_tensor_tensor(iT2[:, 0:512], ps[2][0:64, :], -1.0,
                                                       eye15w[0:64, :], op0=ALU.mult, op1=ALU.add)
                                v.scalar_tensor_tensor(iT2[:, 512:1024], ps[3][0:64, :], -1.0,
                                                       eye15w[0:64, :], op0=ALU.mult, op1=ALU.add)
                                v.drain().then_inc(vV.h, 1)
                            em.v(_)
                            vV.n += 1

                            def _(t, w=vV.n):
                                t.wait_ge(vV.h, w)
                                istep_mm(t, 4, iKVT, iT2)
                            em.t(_)
                            hT.n += 16

                            def _(v, w=hT.n):
                                v.wait_ge(hT.h, w)
                                v.scalar_tensor_tensor(iT1[:, 0:512], ps[4][0:64, :], -1.0,
                                                       eye13w[0:64, :], op0=ALU.mult, op1=ALU.add)
                                v.scalar_tensor_tensor(iT1[:, 512:1024], ps[5][0:64, :], -1.0,
                                                       eye13w[0:64, :], op0=ALU.mult, op1=ALU.add)
                                v.drain().then_inc(vV.h, 1)
                            em.v(_)
                            vV.n += 1

                            def _(t, w=vV.n):
                                t.wait_ge(vV.h, w)
                                istep_mm(t, 2, iVT, iT1)
                            em.t(_)
                            hT.n += 16

                            def _(v, w=hT.n):
                                v.wait_ge(hT.h, w)
                                v.tensor_scalar(iV[:, 0:512], ps[2][0:64, :], 0.25, None,
                                                op0=ALU.mult)
                                v.tensor_scalar(iV[:, 512:1024], ps[3][0:64, :], 0.25, None,
                                                op0=ALU.mult)
                                v.drain().then_inc(vV.h, 1)
                            em.v(_)
                            vV.n += 1

                            def _(t, w=vV.n):
                                t.wait_ge(vV.h, w)
                                istep_tr(t, 4, iV)
                            em.t(_)
                            hT.n += 16

                            def _(v, w=hT.n):
                                v.wait_ge(hT.h, w)
                                v.tensor_copy(iVT[:, 0:512], ps[4][0:64, :])
                                v.tensor_copy(iVT[:, 512:1024], ps[5][0:64, :])
                                v.drain().then_inc(vV.h, 1)
                            em.v(_)
                            vV.n += 1

                        for h in range(_N_EXPN):
                            emit_bg(em, 2)
                            p, cb = (h % 2) * 64, h // 2
                            half = (h % 2) * 1024
                            ubank = 3 + h // 8
                            bnT = hT.n
                            bnE = hE.n

                            pb = 0 if h % 2 == 0 else 4

                            def _(t, p=p, cb=cb, pb=pb, wv=vV.n, we=hE.n):
                                t.wait_ge(vV.h, wv)
                                t.wait_ge(hE.h, we)
                                for tc in range(16):
                                    ins = t.matmul(
                                        ps[pb + tc // 8][:, (tc % 8) * 64:(tc % 8 + 1) * 64],
                                        k_sb[p:p + 64, cb * T + tc * 128:cb * T + (tc + 1) * 128],
                                        q_l_b[p:p + 64, cb * 64:(cb + 1) * 64],
                                        start=True, stop=True, tile_position=(p, 0))
                                    if tc % 8 == 7:
                                        ins.then_inc(hT.h, 1)
                            em.t(_)
                            hT.n += 2

                            def _(s, bnT=bnT, half=half, pb=pb):
                                for g2 in range(2):
                                    s.wait_ge(hT.h, bnT + g2 + 1)
                                    s.activation(ebuf[:, half + g2 * 512:half + (g2 + 1) * 512],
                                                 ps[pb + g2][:, :],
                                                 AF.Exp, scale=1.0 / 32.0)
                                    s.drain().then_inc(hE.h, 1)
                            em.s(_)
                            hE.n += 2

                            def _(t, h=h, half=half, ubank=ubank, w=hE.n, wv=vV.n):
                                t.wait_ge(hE.h, w)
                                t.wait_ge(vV.h, wv)
                                acc_group(t, ps[2][0:64, h:h + 1],
                                          [(ebuf[:, half + tc * 64:half + (tc + 1) * 64],
                                            onescol_b[:, 0:1]) for tc in range(16)])
                                acc_group(t, ps[ubank][0:64, (h % 8) * 64:(h % 8 + 1) * 64],
                                          [(ebuf[:, half + tc * 64:half + (tc + 1) * 64],
                                            v_sb[:, tc * C + h * 64:tc * C + (h + 1) * 64])
                                           for tc in range(16)])
                            em.t(_)
                            hT.n += 2

                            def _(v, h=h, ubank=ubank, w=hT.n):
                                v.wait_ge(hT.h, w)
                                v.reciprocal(rN[:, h:h + 1], ps[2][0:64, h:h + 1])
                                v.drain()
                                v.tensor_scalar(U_all[:, h * 64:(h + 1) * 64],
                                                ps[ubank][0:64, (h % 8) * 64:(h % 8 + 1) * 64],
                                                rN[:, h:h + 1], None, op0=ALU.mult)
                                v.drain().then_inc(vV.h, 1)
                            em.v(_)
                            vV.n += 1

                        def _(t, w=vV.n):
                            t.wait_ge(vV.h, w)
                            for h in range(NH if _DO_PNV else 0):
                                p, cb = (h % 2) * 64, h // 2
                                t.matmul(ps[5][p:p + 64, cb * 64:(cb + 1) * 64],
                                         iVT[:, h * 64:(h + 1) * 64],
                                         U_all[:, h * 64:(h + 1) * 64],
                                         start=True, stop=True,
                                         tile_position=(0, p)).then_inc(hT.h, 1)
                        em.t(_)
                        hT.n += 16

                        def _(v, w=hT.n):
                            v.wait_ge(hT.h, w)
                            if _DO_PNV:
                                v.tensor_copy(PNv_b[:, :], ps[5][:, :])
                            v.drain().then_inc(vV.h, 1)
                        em.v(_)
                        vV.n += 1
                        end_fence(em)

                    if b == 0:
                        dbg_live.update(S_all=iS[:, :], V_all=iV[:, :], U_all=U_all[:, :],
                                        PNv=PNv_b[:, :])
                        if stop_block("a2b0"):
                            return nc, dbg_aps

                with contextlib.ExitStack() as a3:
                    E3 = a3.enter_context
                    y_sb = E3(nc.sbuf_tensor(f"a_y_{b}", [128, 16 * C], BF16))
                    prb = E3(nc.sbuf_tensor(f"a_pr0_{b}", [128, T], BF16))
                    ytT = E3(nc.sbuf_tensor(f"a_yt_{b}", [128, C], BF16))
                    xst = [E3(nc.sbuf_tensor(f"a_r{b}{i}", [128, C], F32)) for i in range(2)]
                    x2t = [E3(nc.sbuf_tensor(f"a_2{b}{i}", [128, C], F32)) for i in range(2)]

                    with nc.Block() as blk:
                        em = Emit(blk)
                        brl = dP.n
                        bye = hE.n

                        def _(q):
                            for fc in range(8):
                                if fc >= 1:
                                    q.wait_ge(hE.h, bye + 2 * fc)
                                q.dma_start(out=prb[:, :],
                                            in_=s_eL[fc * 128:(fc + 1) * 128, :]
                                            ).then_inc(dP.h, 16)
                        em.q(_)
                        dP.n += 8 * 16

                        byT = hT.n

                        def _(t):
                            u = 0
                            for fc in range(8):
                                t.wait_ge(dP.h, brl + 16 * (fc + 1))
                                for hh in range(2):
                                    h = 2 * fc + hh
                                    p, cb = (h % 2) * 64, h // 2
                                    t.wait_ge(hE.h, bye + max(0, u - 1))
                                    for tc in range(16):
                                        ins = t.matmul(
                                            ps[(u % 2) * 2 + tc // 8][:, (tc % 8) * 64:(tc % 8 + 1) * 64],
                                            prb[p:p + 64, tc * 128:(tc + 1) * 128],
                                            PNv_b[p:p + 64, cb * 64:(cb + 1) * 64],
                                            start=True, stop=True,
                                            tile_position=(p, 0))
                                        if tc == 15:
                                            ins.then_inc(hT.h, 1)
                                    u += 1
                        em.t(_)
                        hT.n += 16

                        def _(v):
                            u = 0
                            for fc in range(8):
                                for hh in range(2):
                                    h = 2 * fc + hh
                                    v.wait_ge(hT.h, byT + u + 1)
                                    for tc in range(16):
                                        v.tensor_scalar(
                                            y_sb[:, tc * C + h * 64:tc * C + (h + 1) * 64],
                                            ps[(u % 2) * 2 + tc // 8][:, (tc % 8) * 64:(tc % 8 + 1) * 64],
                                            recipL[:, h * 16 + tc:h * 16 + tc + 1],
                                            None, op0=ALU.mult)
                                    v.drain().then_inc(hE.h, 1)
                                    u += 1
                        em.v(_)
                        hE.n += 16
                        yq_done = hE.n

                        bX = [dX[0].n, dX[1].n]
                        boX = [oX[0].n, oX[1].n]
                        bcons = vV.n

                        def _(q, bX=bX):
                            for tc in range(2):
                                q.dma_start(out=xst[tc % 2][:, :],
                                            in_=ap["x"][t0g + tc * 128:t0g + (tc + 1) * 128, :]
                                            ).then_inc(dX[tc % 2].h, 16)
                        em.q(_)
                        dX[0].n += 16
                        dX[1].n += 16

                        for tc in range(16):
                            btr = hT.n
                            bev = sS.n

                            def _(t, tc=tc, bev=bev):
                                t.wait_ge(hE.h, yq_done)
                                for dc in range(8):
                                    t.wait_ge(sS.h, bev + max(0, dc - 1))
                                    t.transpose(psb[dc % 2][:, 0:128],
                                                y_sb[:, tc * C + dc * 128:tc * C + (dc + 1) * 128],
                                                idb[:, :]).then_inc(hT.h, 1)
                            em.t(_)
                            hT.n += 8

                            def _(s, btr=btr):
                                for dc in range(8):
                                    s.wait_ge(hT.h, btr + dc + 1)
                                    s.copy(ytT[:, dc * 128:(dc + 1) * 128],
                                           psb[dc % 2][:, 0:128])
                                    s.drain().then_inc(sS.h, 1)
                            em.s(_)
                            sS.n += 8

                            bpj = hT.n

                            def _(t, tc=tc, w=sS.n, ww=dW1.n):
                                t.wait_ge(sS.h, w)
                                t.wait_ge(dW1.h, ww)
                                if tc >= 2:
                                    t.wait_ge(vV.h, bcons + 3 * (tc - 2) + 1)
                                for n2 in range(2):
                                    acc_group(t, ps[4 + n2][:, :],
                                              [(ytT[:, dc * 128:(dc + 1) * 128],
                                                Wb[1][:, dc * C + n2 * 512:dc * C + (n2 + 1) * 512])
                                               for dc in range(8)])
                            em.t(_)
                            hT.n += 2

                            def _(v, tc=tc, bpj=bpj):
                                v.wait_ge(hT.h, bpj + 2)
                                v.wait_ge(dW1.h, dW1.n)
                                v.wait_ge(dX[tc % 2].h, bX[tc % 2] + 16 * (tc // 2 + 1))
                                if tc >= 2:
                                    v.wait_ge(oX[tc % 2].h, boX[tc % 2] + 16 * (tc // 2))
                                for n2 in range(2):
                                    v.tensor_add(x2t[tc % 2][:, n2 * 512:(n2 + 1) * 512],
                                                 ps[4 + n2][:, :],
                                                 xst[tc % 2][:, n2 * 512:(n2 + 1) * 512])
                                v.drain()
                                v.reduce_sum(lnst[:, 0:1], x2t[tc % 2][:, :], axis=AX.X)
                                v.drain()
                                v.tensor_scalar(lnst[:, 1:2], lnst[:, 0:1], -1.0 / C, None,
                                                op0=ALU.mult)
                                v.drain().then_inc(vV.h, 1)
                            em.v(_)
                            vV.n += 1

                            def _(s, tc=tc, w=vV.n):
                                s.wait_ge(vV.h, w)
                                s.activation(ytT[:, :], x2t[tc % 2][:, :], AF.Square,
                                             scale=1.0 / 32.0, accum_out=lnst[:, 3:4])
                                s.drain().then_inc(sS.h, 1)
                            em.s(_)
                            sS.n += 1

                            def _(v, w=sS.n):
                                v.wait_ge(sS.h, w)
                                v.tensor_scalar(lnst[:, 2:3], lnst[:, 1:2], -1.0, None,
                                                op0=ALU.mult)
                                v.drain()
                                v.scalar_tensor_tensor(lnst[:, 4:5], lnst[:, 2:3], lnst[:, 1:2],
                                                       lnst[:, 3:4], op0=ALU.mult, op1=ALU.add)
                                v.drain().then_inc(vV.h, 1)
                            em.v(_)
                            vV.n += 1

                            def _(s, w=vV.n):
                                s.wait_ge(vV.h, w)
                                s.activation(lnst[:, 5:6], lnst[:, 4:5], AF.Ln, bias=epsA[:, 0:1])
                                s.drain()
                                s.activation(lnst[:, 6:7], lnst[:, 5:6], AF.Exp, scale=-0.5)
                                s.drain()
                                s.activation(lnst[:, 5:6], lnst[:, 4:5], AF.Ln, scale=1.0 + EPS,
                                             bias=epsB[:, 0:1])
                                s.drain()
                                s.activation(lnst[:, 7:8], lnst[:, 5:6], AF.Exp, scale=-0.5)
                                s.drain().then_inc(sS.h, 1)
                            em.s(_)
                            sS.n += 1

                            def _(v, tc=tc, b=b, w=sS.n):
                                s4 = 4 * (b * 16 + tc)
                                v.wait_ge(sS.h, w)
                                v.tensor_scalar(stats[:, s4:s4 + 1], lnst[:, 6:7], 1.0, None,
                                                op0=ALU.add)
                                v.tensor_mul(stats[:, s4 + 1:s4 + 2], lnst[:, 1:2], lnst[:, 6:7])
                                v.tensor_copy(stats[:, s4 + 2:s4 + 3], lnst[:, 7:8])
                                v.tensor_mul(stats[:, s4 + 3:s4 + 4], lnst[:, 1:2], lnst[:, 7:8])
                                v.drain().then_inc(vV.h, 1)
                            em.v(_)
                            vV.n += 1

                            def _(q, tc=tc, w=vV.n):
                                q.wait_ge(vV.h, w)
                                q.dma_start(out=s_x2[t0g + tc * 128:t0g + (tc + 1) * 128, :],
                                            in_=x2t[tc % 2][:, :]).then_inc(oX[tc % 2].h, 16)
                                if tc + 2 < 16:
                                    q.wait_ge(vV.h, bcons + 3 * tc + 1)
                                    q.dma_start(out=xst[tc % 2][:, :],
                                                in_=ap["x"][t0g + (tc + 2) * 128:t0g + (tc + 3) * 128, :]
                                                ).then_inc(dX[tc % 2].h, 16)
                            em.q(_)
                            oX[tc % 2].n += 16
                            if tc + 2 < 16:
                                dX[tc % 2].n += 16

                            emit_bg(em, 2)
                        end_fence(em)

                    if b == 0:
                        dbg_live.update(y_sb=y_sb[:, :], stats=stats[:, :])
                        if stop_block("a3b0"):
                            return nc, dbg_aps

            if stop_block("attn"):
                return nc, dbg_aps

        with contextlib.ExitStack() as est:
            Ee = est.enter_context
            eyA = Ee(nc.sbuf_tensor("e_I", [128, 8 * C], BF16))
            mA = Ee(nc.sbuf_tensor("e_A", [128, 8 * C], BF16))
            mA2 = Ee(nc.sbuf_tensor("e_A2", [128, 8 * C], BF16))
            mA3 = Ee(nc.sbuf_tensor("e_A3", [128, 8 * C], BF16))
            mB = Ee(nc.sbuf_tensor("e_B", [128, 8 * C], BF16))
            mT = Ee(nc.sbuf_tensor("e_T", [128, 8 * C], BF16))
            mE1 = Ee(nc.sbuf_tensor("e_E1", [128, 8 * C], BF16))
            mE2 = Ee(nc.sbuf_tensor("e_E2", [128, 8 * C], BF16))
            mXT = Ee(nc.sbuf_tensor("e_XT", [128, 8 * C], BF16))
            cidf = Ee(nc.sbuf_tensor("e_cf", [128, 2048], F32))
            cidb = Ee(nc.sbuf_tensor("e_cb", [128, 13 * 128], BF16))

            with nc.Block() as blk:
                em = Emit(blk)

                def mm1024g(t, pair0, prs_fn, bE, bS):
                    u = 0
                    for mc in range(8):
                        for n2 in range(2):
                            if u == 0:
                                t.wait_ge(hE.h, bE)
                                t.wait_ge(sS.h, bS)
                            elif u >= 2:
                                if u - 2 < 8:
                                    t.wait_ge(hE.h, bE + u - 1)
                                else:
                                    t.wait_ge(sS.h, bS + u - 9)
                            acc_group(t, ps[pair0 + (u % 2)][:, :], prs_fn(mc, n2))
                            u += 1

                def mm1024(t, pair0, lhsT_sb, rhs_sb, bE, bS):
                    def prs(mc, n2):
                        return [(lhsT_sb[:, dc * C + mc * 128:dc * C + (mc + 1) * 128],
                                 rhs_sb[:, dc * C + n2 * 512:dc * C + (n2 + 1) * 512])
                                for dc in range(8)]
                    mm1024g(t, pair0, prs, bE, bS)

                def evac1024(v, dst_sb, pair0, bT, scale):
                    for u in range(8):
                        mc, n2 = u // 2, u % 2
                        v.wait_ge(hT.h, bT + u + 1)
                        v.tensor_scalar(dst_sb[:, mc * C + n2 * 512:mc * C + (n2 + 1) * 512],
                                        ps[pair0 + (u % 2)][:, :], scale, None,
                                        op0=ALU.mult)
                        v.drain().then_inc(hE.h, 1)

                def evac1024_s(s, dst_sb, pair0, bT, scale):
                    for u in range(8, 16):
                        mc, n2 = u // 2, u % 2
                        s.wait_ge(hT.h, bT + u + 1)
                        s.activation(dst_sb[:, mc * C + n2 * 512:mc * C + (n2 + 1) * 512],
                                     ps[pair0 + (u % 2)][:, :], AF.Copy, scale=scale)
                        s.drain().then_inc(sS.h, 1)

                def cid_b(k):
                    return cidb[:, k * 128:(k + 1) * 128]

                def chunk(x_sb, mc, n2):
                    return x_sb[:, mc * C + n2 * 512:mc * C + (n2 + 1) * 512]

                def _(g):
                    g.dma_start(out=eyA[:, :], in_=ap["eyeA"]).then_inc(dG.h, 16)
                    for hh in range(2):
                        g.dma_start(out=cidf[:, hh * 1024:(hh + 1) * 1024],
                                    in_=ap["cid0" if hh == 0 else "cid1"]
                                    ).then_inc(dG.h, 16)
                em.g(_)
                dG.n += 48

                def _(v, w=dG.n):
                    v.wait_ge(dG.h, w)
                    v.tensor_copy(cidb[:, :], cidf[:, 0:13 * 128])
                    v.drain().then_inc(vV.h, 1)
                em.v(_)
                vV.n += 1
                cid_ready = vV.n

                for gi in range(3):
                    def _(g, gi=gi, wt=hT.n, wd=dG.n):
                        g.wait_ge(dG.h, wd)
                        g.wait_ge(hT.h, wt)
                        for dc in range(8):
                            g.dma_start(out=mA[:, dc * C:(dc + 1) * C],
                                        in_=ap["gens"][gi * C + dc * 128:gi * C + (dc + 1) * 128, :]
                                        ).then_inc(dG.h, 16)
                    em.g(_)
                    dG.n += 128

                    b1, b2, b3 = hT.n, hE.n, sS.n

                    def _(t, w=dG.n, b2=b2, b3=b3):
                        t.wait_ge(dG.h, w)
                        t.wait_ge(vV.h, cid_ready)
                        mm1024(t, 0, mA, mA, b2, b3)
                    em.t(_)
                    hT.n += 16

                    def _(v, b1=b1):
                        evac1024(v, mA2, 0, b1, -1.0)
                    em.v(_)

                    def _(s, b1=b1):
                        evac1024_s(s, mA2, 0, b1, -1.0)
                    em.s(_)
                    hE.n += 8
                    sS.n += 8

                    emit_bg(em, 2)

                    b1, b2, b3 = hT.n, hE.n, sS.n

                    def _(t, b2=b2, b3=b3):
                        mm1024(t, 2, mA2, mA, b2, b3)
                    em.t(_)
                    hT.n += 16

                    def _(v, b1=b1):
                        evac1024(v, mA3, 2, b1, 1.0)
                    em.v(_)

                    def _(s, b1=b1):
                        evac1024_s(s, mA3, 2, b1, 1.0)
                    em.s(_)
                    hE.n += 8
                    sS.n += 8

                    emit_bg(em, 2)

                    b1, b2, b3 = hT.n, hE.n, sS.n

                    def _(t, b2=b2, b3=b3):
                        def prs(mc, n2):
                            return [(cid_b(12), chunk(mA3, mc, n2)),
                                    (cid_b(11), chunk(mA2, mc, n2)),
                                    (cid_b(10), chunk(mA, mc, n2)),
                                    (cid_b(9), chunk(eyA, mc, n2))]
                        mm1024g(t, 0, prs, b2, b3)
                    em.t(_)
                    hT.n += 16

                    def _(v, b1=b1):
                        evac1024(v, mB, 0, b1, 1.0)
                    em.v(_)

                    def _(s, b1=b1):
                        evac1024_s(s, mB, 0, b1, 1.0)
                    em.s(_)
                    hE.n += 8
                    sS.n += 8

                    emit_bg(em, 2)

                    for stepi, (kI, kA, kA2) in enumerate([(6, 7, 8), (3, 4, 5), (0, 1, 2)]):
                        rhs_t = mB if stepi == 0 else (mT if stepi == 1 else mB)
                        dst_t = mT if stepi == 0 else (mB if stepi == 1 else mT)
                        b1, b2, b3 = hT.n, hE.n, sS.n

                        def _(t, rhs_t=rhs_t, kI=kI, kA=kA, kA2=kA2, b2=b2, b3=b3):
                            def prs(mc, n2):
                                lst = [(mA3[:, dc * C + mc * 128:dc * C + (mc + 1) * 128],
                                        rhs_t[:, dc * C + n2 * 512:dc * C + (n2 + 1) * 512])
                                       for dc in range(8)]
                                lst.append((cid_b(kA2), chunk(mA2, mc, n2)))
                                lst.append((cid_b(kA), chunk(mA, mc, n2)))
                                lst.append((cid_b(kI), chunk(eyA, mc, n2)))
                                return lst
                            mm1024g(t, 0, prs, b2, b3)
                        em.t(_)
                        hT.n += 16

                        def _(s, dst_t=dst_t, b1=b1):
                            evac1024_s(s, dst_t, 0, b1, -1.0)
                        em.s(_)

                        def _(v, dst_t=dst_t, b1=b1):
                            evac1024(v, dst_t, 0, b1, -1.0)
                        em.v(_)
                        hE.n += 8
                        sS.n += 8

                        emit_bg(em, 2)

                    if gi < 2:
                        dst_e = mE1 if gi == 0 else mE2

                        def _(v, dst_e=dst_e, w2=hE.n, w3=sS.n):
                            v.wait_ge(hE.h, w2)
                            v.wait_ge(sS.h, w3)
                            for mc in range(8):
                                sl = slice(mc * C, (mc + 1) * C)
                                v.tensor_copy(dst_e[:, sl], mT[:, sl])
                            v.drain().then_inc(vV.h, 1)
                        em.v(_)
                        vV.n += 1
                    else:
                        btr, bte = hT.n, sS.n

                        def _(t, w=vV.n, w2=hE.n, bte=bte):
                            t.wait_ge(vV.h, w)
                            t.wait_ge(hE.h, w2)
                            for u in range(64):
                                mc, dc = u // 8, u % 8
                                t.wait_ge(sS.h, bte + max(0, u - 1))
                                t.transpose(psb[u % 2][:, 0:128],
                                            mE1[:, mc * C + dc * 128:mc * C + (dc + 1) * 128],
                                            idb[:, :]).then_inc(hT.h, 1)
                        em.t(_)
                        hT.n += 64

                        def _(s, btr=btr):
                            for u in range(64):
                                mc, dc = u // 8, u % 8
                                s.wait_ge(hT.h, btr + u + 1)
                                s.copy(mXT[:, dc * C + mc * 128:dc * C + (mc + 1) * 128],
                                       psb[u % 2][:, 0:128])
                                s.drain().then_inc(sS.h, 1)
                        em.s(_)
                        sS.n += 64

                        b1, b2, b3 = hT.n, hE.n, sS.n

                        def _(t, w=sS.n, b2=b2, b3=b3):
                            t.wait_ge(sS.h, w)
                            mm1024(t, 0, mXT, mE2, b2, b3)
                        em.t(_)
                        hT.n += 16

                        def _(s, b1=b1):
                            evac1024_s(s, mB, 0, b1, 1.0)
                        em.s(_)

                        def _(v, b1=b1):
                            evac1024(v, mB, 0, b1, 1.0)
                        em.v(_)
                        hE.n += 8
                        sS.n += 8

                        emit_bg(em, 2)

                        btr, bte = hT.n, sS.n

                        def _(t, w2=hE.n, bte=bte):
                            t.wait_ge(hE.h, w2)
                            for u in range(64):
                                mc, dc = u // 8, u % 8
                                t.wait_ge(sS.h, bte + max(0, u - 1))
                                t.transpose(psb[u % 2][:, 0:128],
                                            mB[:, mc * C + dc * 128:mc * C + (dc + 1) * 128],
                                            idb[:, :]).then_inc(hT.h, 1)
                        em.t(_)
                        hT.n += 64

                        def _(s, btr=btr):
                            for u in range(64):
                                mc, dc = u // 8, u % 8
                                s.wait_ge(hT.h, btr + u + 1)
                                s.copy(mXT[:, dc * C + mc * 128:dc * C + (mc + 1) * 128],
                                       psb[u % 2][:, 0:128])
                                s.drain().then_inc(sS.h, 1)
                        em.s(_)
                        sS.n += 64

                        b1, b2, b3 = hT.n, hE.n, sS.n

                        def _(t, w=sS.n, b2=b2, b3=b3):
                            t.wait_ge(sS.h, w)
                            mm1024(t, 0, mXT, mT, b2, b3)
                        em.t(_)
                        hT.n += 16

                        def _(s, b1=b1):
                            evac1024_s(s, R_bf, 0, b1, 1.0)
                        em.s(_)

                        def _(v, b1=b1):
                            evac1024(v, R_bf, 0, b1, 1.0)
                        em.v(_)
                        hE.n += 8
                        sS.n += 8

                        emit_bg(em, 2)

                end_fence(em)

            dbg_live.update(R=R_bf[:, :])
            if stop_block("expm"):
                return nc, dbg_aps

        with contextlib.ExitStack() as fst:
            Ef = fst.enter_context
            abw = Ef(nc.sbuf_tensor("f_aw", [128, 8 * C], BF16))
            gcw1b = Ef(nc.sbuf_tensor("c_g1", [128, C], BF16))
            abelbrow = Ef(nc.sbuf_tensor("c_abr", [1, C], BF16))
            gatebrow = Ef(nc.sbuf_tensor("c_gbr", [1, 8], BF16))
            gatew_t = Ef(nc.sbuf_tensor("c_gwt", [128, 64], BF16))
            xn2s = Ef(nc.sbuf_tensor("f_xn", [128, 8 * SLAB], BF16))
            x2st = [Ef(nc.sbuf_tensor(f"f_x{i}", [128, C], F32)) for i in range(2)]
            xcb2 = [Ef(nc.sbuf_tensor(f"f_xc{i}", [128, C], BF16)) for i in range(2)]
            abt = [Ef(nc.sbuf_tensor(f"f_ab{i}", [128, C], BF16)) for i in range(2)]
            prod = Ef(nc.sbuf_tensor("f_pr", [128, C], F32))
            glog = Ef(nc.sbuf_tensor("f_gl", [128, 8], F32))
            gw2 = Ef(nc.sbuf_tensor("f_g2", [128, 8], F32))
            gmsk = Ef(nc.sbuf_tensor("f_gm", [128, 8], F32))
            gw1s = Ef(nc.sbuf_tensor("f_g1s", [128, 64], F32))
            gw2ms = Ef(nc.sbuf_tensor("f_g2s", [128, 64], F32))
            gex = Ef(nc.sbuf_tensor("f_gx", [128, 48], F32))

            with nc.Block() as blk:
                em = Emit(blk)
                x2sp = [oX[0].n, oX[1].n]
                bXf = [dX[0].n, dX[1].n]
                bax = [oX[0].n, oX[1].n]

                def _(g):
                    for t_sb, srcn in [(gcw1b, "gcw1b"), (abelbrow, "abelbrow"),
                                       (gatebrow, "gatebrow"), (gatew_t, "gatew_t")]:
                        g.dma_start(out=t_sb[:, :], in_=ap[srcn]).then_inc(dG.h, 16)
                    for dc in range(8):
                        g.dma_start(out=abw[:, dc * C:(dc + 1) * C],
                                    in_=ap["abel_w"][dc * 128:(dc + 1) * 128, :]
                                    ).then_inc(dG.h, 16)
                em.g(_)
                dG.n += 128 + 64
                abw_ready = dG.n

                ps2free = {-1: 0, -2: 0}
                rbst = {"xc": {}}
                for si in range(NSLAB):
                    emit_rebuild(em, si, xcb2, x2st, xn2s, x2sp, rbst)
                    ws_reb = vV.n

                    for j in range(8):
                        bi = si * 8 + j
                        bab = hT.n

                        def _(t, j=j, ws=ws_reb, whe=hE.n, wvv=ps2free[si * 8 + j - 2]):
                            t.wait_ge(vV.h, ws)
                            t.wait_ge(dG.h, abw_ready)
                            t.wait_ge(hE.h, whe)
                            t.wait_ge(vV.h, wvv)
                            for n2 in range(2):
                                prs = [(xn2s[:, dc * SLAB + j * 128:dc * SLAB + (j + 1) * 128],
                                        abw[:, dc * C + n2 * 512:dc * C + (n2 + 1) * 512])
                                       for dc in range(8)]
                                prs.append((onesrow_b[0:1, 0:128],
                                            abelbrow[0:1, n2 * 512:(n2 + 1) * 512]))
                                acc_group(t, ps[n2][:, :], prs)
                            gprs = [(xn2s[:, dc * SLAB + j * 128:dc * SLAB + (j + 1) * 128],
                                     gatew_t[:, dc * 8:(dc + 1) * 8]) for dc in range(8)]
                            gprs.append((onesrow_b[0:1, 0:128], gatebrow[0:1, 0:8]))
                            acc_group(t, ps[2 + (j % 2)][:, 0:8], gprs)
                        em.t(_)
                        hT.n += 3

                        def _(s, j=j, bab=bab, wo=oX[j % 2].n):
                            s.wait_ge(oX[j % 2].h, wo)
                            for n2 in range(2):
                                s.wait_ge(hT.h, bab + n2 + 1)
                                s.activation(abt[j % 2][:, n2 * 512:(n2 + 1) * 512],
                                             ps[n2][:, :], AF.Gelu)
                                s.drain().then_inc(hE.h, 1)
                        em.s(_)
                        hE.n += 2

                        ps2free[bi] = vV.n + 1

                        def _(v, j=j, bi=bi, bab=bab, w=hE.n):
                            v.wait_ge(hT.h, bab + 3)
                            v.tensor_copy(glog[:, 0:8], ps[2 + (j % 2)][:, 0:8])
                            v.drain().then_inc(vV.h, 1)
                            v.wait_ge(hE.h, w)
                            v.tensor_mul(prod[:, :], abt[j % 2][:, :], gcw1b[:, :])
                            v.drain()
                            v.reduce_sum(gdot1[:, bi:bi + 1], prod[:, :], axis=AX.X)
                            v.drain().then_inc(vV.h, 1)
                            v.reduce_max(gex[:, j:j + 1], glog[:, 0:8], axis=AX.X)
                            v.drain()
                            v.tensor_scalar(gw1s[:, j * 8:(j + 1) * 8], glog[:, 0:8],
                                            gex[:, j:j + 1], None, op0=ALU.is_ge)
                            v.drain()
                            v.scalar_tensor_tensor(gw2[:, 0:8], gw1s[:, j * 8:(j + 1) * 8],
                                                   -1e30, glog[:, 0:8],
                                                   op0=ALU.mult, op1=ALU.add)
                            v.drain()
                            v.reduce_max(gex[:, 8 + j:9 + j], gw2[:, 0:8], axis=AX.X)
                            v.drain()
                            v.tensor_scalar(gw2ms[:, j * 8:(j + 1) * 8], gw2[:, 0:8],
                                            gex[:, 8 + j:9 + j], None, op0=ALU.is_ge)
                            v.drain().then_inc(vV.h, 1)
                        em.v(_)
                        vV.n += 3

                        emit_bg(em, 2)

                        def _(q, bi=bi, j=j, w=vV.n):
                            q.wait_ge(vV.h, w - 1)
                            q.dma_start(out=s_abel[bi * 128:(bi + 1) * 128, :],
                                        in_=abt[j % 2][:, :]).then_inc(oX[j % 2].h, 16)
                        em.q(_)
                        oX[j % 2].n += 16

                    def _(v, w=vV.n):
                        v.wait_ge(vV.h, w)
                        v.tensor_tensor(gex[:, 16:24], gex[:, 8:16], gex[:, 0:8],
                                        op=ALU.subtract)
                        v.drain().then_inc(vV.h, 1)
                    em.v(_)
                    vV.n += 1

                    def _(s, w=vV.n):
                        s.wait_ge(vV.h, w)
                        s.activation(gex[:, 24:32], gex[:, 16:24], AF.Exp)
                        s.drain().then_inc(sS.h, 1)
                    em.s(_)
                    sS.n += 1

                    def _(v, si=si, w=sS.n):
                        v.wait_ge(sS.h, w)
                        v.tensor_scalar(gex[:, 32:40], gex[:, 24:32], 1.0, None, op0=ALU.add)
                        v.drain()
                        v.reciprocal(gex[:, 32:40], gex[:, 32:40])
                        v.drain()
                        v.tensor_scalar(gex[:, 40:48], gex[:, 32:40], -1.0, 1.0,
                                        op0=ALU.mult, op1=ALU.add)
                        v.drain()
                        for j in range(8):
                            bi = si * 8 + j
                            v.tensor_scalar(gmsk[:, 0:8], gw2ms[:, j * 8:(j + 1) * 8],
                                            gex[:, 40 + j:41 + j], None, op0=ALU.mult)
                            v.drain()
                            v.scalar_tensor_tensor(wd_sb[:, bi * 8:(bi + 1) * 8],
                                                   gw1s[:, j * 8:(j + 1) * 8],
                                                   gex[:, 32 + j:33 + j], gmsk[:, 0:8],
                                                   op0=ALU.mult, op1=ALU.add)
                            v.drain()
                        v.drain().then_inc(vV.h, 1)
                    em.v(_)
                    vV.n += 1

                while est_st["i"] < NEST + 2 or est_st["c"] < NEST:
                    emit_bg(em, 4)
                end_fence(em)

            dbg_live.update(wd=wd_sb[:, :], gd1=gdot1[:, :], xn2s=xn2s[:, :])
            if stop_block("fa"):
                return nc, dbg_aps

        with contextlib.ExitStack() as fbt:
            Eb = fbt.enter_context
            ew1s = Eb(nc.sbuf_tensor("b_w1", [128, 8 * HD], BF16))
            eb2row = Eb(nc.sbuf_tensor("c_eb2", [1, C], BF16))
            ew2s = Eb(nc.sbuf_tensor("b_w2", [128, 16 * C], BF16))
            xn2s = Eb(nc.sbuf_tensor("b_xn", [128, 8 * SLAB], BF16))
            h_sb = Eb(nc.sbuf_tensor("b_h", [128, 16 * SLAB], BF16))
            acc = Eb(nc.sbuf_tensor("b_acc", [128, 8 * C], F32))
            x2st = [Eb(nc.sbuf_tensor(f"b_x{i}", [128, C], F32)) for i in range(2)]
            abr = [Eb(nc.sbuf_tensor(f"b_ar{i}", [128, C], BF16)) for i in range(2)]
            xcb0 = Eb(nc.sbuf_tensor("b_xc0", [128, C], BF16))
            accT = Eb(nc.sbuf_tensor("b_aT", [128, C], BF16))
            accB = Eb(nc.sbuf_tensor("b_aB", [128, C], BF16))
            xcb2 = [xcb0, accB]
            otile = [Eb(nc.sbuf_tensor(f"b_o{i}", [128, C], F32)) for i in range(2)]
            prod = Eb(nc.sbuf_tensor("b_pr", [128, 512], F32))

            with nc.Block() as blk:
                em = Emit(blk)
                spx = [oX[0].n, oX[1].n]
                rbstb = {"xc": {}}

                def emit_expert_loads(e, gate_hT):
                    def _(q, e=e, gate_hT=gate_hT):
                        q.wait_ge(hT.h, gate_hT)
                        for dc in range(8):
                            q.dma_start(out=ew1s[:, dc * HD:(dc + 1) * HD],
                                        in_=s_ew1b[e * C + dc * 128:e * C + (dc + 1) * 128, :]
                                        ).then_inc(eW[0].h, 16)
                        for hc in range(16):
                            q.dma_start(out=ew2s[:, hc * C:(hc + 1) * C],
                                        in_=s_ew2b[e * HD + hc * 128:e * HD + (hc + 1) * 128, :]
                                        ).then_inc(eW[1].h, 16)
                    em.q(_)
                    eW[0].n += 128
                    eW[1].n += 256

                def _(g):
                    g.dma_start(out=eb2row[:, :], in_=ap["eb2row"][0:1, 0:C]
                                ).then_inc(dG.h, 16)
                em.g(_)
                dG.n += 16

                emit_expert_loads(0, 0)

                for si in range(NSLAB):
                    emit_rebuild(em, si, xcb2, x2st, xn2s, spx, rbstb)

                    xn2_ready = vV.n

                    for e in range(NE):
                        bhT, bhE = hT.n, hE.n

                        def _(t, w1=eW[0].n, bhE=bhE, xn2_ready=xn2_ready, wv=vV.n):
                            t.wait_ge(vV.h, max(xn2_ready, wv))
                            t.wait_ge(eW[0].h, w1)
                            for hc in range(16):
                                pb = 0 if hc % 2 == 0 else 2
                                t.wait_ge(hE.h, bhE + max(0, hc - 1))
                                for tb2 in range(2):
                                    for dc in range(8):
                                        ins = t.matmul(
                                            ps[pb + tb2][:, :],
                                            ew1s[:, dc * HD + hc * 128:dc * HD + (hc + 1) * 128],
                                            xn2s[:, dc * SLAB + tb2 * 512:dc * SLAB + (tb2 + 1) * 512],
                                            start=(dc == 0), stop=(dc == 7))
                                        if dc == 7 and tb2 == 1:
                                            ins.then_inc(hT.h, 1)
                        em.t(_)
                        hT.n += 16
                        h_mm_done = hT.n

                        def _(s, e=e, bhT=bhT):
                            for hc in range(16):
                                pb = 0 if hc % 2 == 0 else 2
                                s.wait_ge(hT.h, bhT + hc + 1)
                                for tb2 in range(2):
                                    s.activation(h_sb[:, hc * SLAB + tb2 * 512:hc * SLAB + (tb2 + 1) * 512],
                                                 ps[pb + tb2][:, :], AF.Gelu,
                                                 bias=eb1t[:, e * 16 + hc:e * 16 + hc + 1])
                                s.drain().then_inc(hE.h, 1)
                        em.s(_)
                        hE.n += 16
                        h_done = hE.n

                        nxt = (e + 1, si) if e < NE - 1 else (0, si + 1)
                        if nxt[1] < NSLAB:
                            def _(q, ne=nxt[0], gate=h_mm_done):
                                q.wait_ge(hT.h, gate)
                                for dc in range(8):
                                    q.dma_start(out=ew1s[:, dc * HD:(dc + 1) * HD],
                                                in_=s_ew1b[ne * C + dc * 128:ne * C + (dc + 1) * 128, :]
                                                ).then_inc(eW[0].h, 16)
                            em.q(_)
                            eW[0].n += 128

                        beT, beE = hT.n, vV.n

                        def _(t, e=e, h_done=h_done, beE=beE, w2=eW[1].n, wg=dG.n):
                            t.wait_ge(hE.h, h_done)
                            t.wait_ge(eW[1].h, w2)
                            t.wait_ge(dG.h, wg)
                            for tc in range(8):
                                pb = 4 if tc % 2 == 0 else 2
                                t.wait_ge(vV.h, beE + max(0, tc - 1))
                                for n2 in range(2):
                                    for hc in range(16):
                                        t.matmul(
                                            ps[pb + n2][:, :],
                                            h_sb[:, hc * SLAB + tc * 128:hc * SLAB + (tc + 1) * 128],
                                            ew2s[:, hc * C + n2 * 512:hc * C + (n2 + 1) * 512],
                                            start=(hc == 0), stop=False)
                                    ins = t.matmul(ps[pb + n2][:, :], onesrow_b[0:1, 0:128],
                                                   eb2row[0:1, n2 * 512:(n2 + 1) * 512],
                                                   start=False, stop=True)
                                    if n2 == 1:
                                        ins.then_inc(hT.h, 1)
                        em.t(_)
                        hT.n += 8
                        eo_mm_done = hT.n

                        def _(v, e=e, si=si, beT=beT):
                            for tc in range(8):
                                pb = 4 if tc % 2 == 0 else 2
                                v.wait_ge(hT.h, beT + tc + 1)
                                for n2 in range(2):
                                    asl = acc[:, tc * C + n2 * 512:tc * C + (n2 + 1) * 512]
                                    wsl = wd_sb[:, (si * 8 + tc) * 8 + e:(si * 8 + tc) * 8 + e + 1]
                                    if e == 0:
                                        v.tensor_scalar(asl, ps[pb + n2][:, :], wsl, None,
                                                        op0=ALU.mult)
                                    else:
                                        v.scalar_tensor_tensor(asl, ps[pb + n2][:, :], wsl,
                                                               asl, op0=ALU.mult, op1=ALU.add)
                                v.drain().then_inc(vV.h, 1)
                        em.v(_)
                        vV.n += 8

                        if nxt[1] < NSLAB:
                            def _(g, ne=nxt[0], gate=eo_mm_done):
                                g.wait_ge(hT.h, gate)
                                g.dma_start(out=eb2row[:, :],
                                            in_=ap["eb2row"][0:1, ne * C:(ne + 1) * C]
                                            ).then_inc(dG.h, 16)
                            em.g(_)
                            dG.n += 16

                            def _(q, ne=nxt[0], gate=eo_mm_done):
                                q.wait_ge(hT.h, gate)
                                for hc in range(16):
                                    q.dma_start(out=ew2s[:, hc * C:(hc + 1) * C],
                                                in_=s_ew2b[ne * HD + hc * 128:ne * HD + (hc + 1) * 128, :]
                                                ).then_inc(eW[1].h, 16)
                            em.q(_)
                            eW[1].n += 256

                    brm = vV.n

                    for tc in range(8):
                        bi = si * 8 + tc
                        tg = bi * 128
                        rp = 4 if tc % 2 == 0 else 2
                        btr, bev = hT.n, sS.n

                        def _(q, tc=tc, tg=tg):
                            q.wait_ge(vV.h, brm if tc < 2 else brm + 3 * (tc - 2) + 3)
                            q.dma_start(out=x2st[tc % 2][:, :],
                                        in_=s_x2[tg:tg + 128, :]).then_inc(dX[tc % 2].h, 16)
                            q.dma_start(out=abr[tc % 2][:, :],
                                        in_=s_abel[tg:tg + 128, :]).then_inc(dX[tc % 2].h, 16)
                        em.q(_)
                        dX[tc % 2].n += 32

                        def _(v, tc=tc, w=vV.n if tc == 0 else brm, wt=hT.n):
                            v.wait_ge(vV.h, w)
                            v.wait_ge(hT.h, wt)
                            v.tensor_copy(accB[:, :], acc[:, tc * C:(tc + 1) * C])
                            v.drain().then_inc(vV.h, 1)
                        em.v(_)
                        vV.n += 1

                        def _(t, tc=tc, w=vV.n, bev=bev):
                            t.wait_ge(vV.h, w)
                            for dc in range(8):
                                t.wait_ge(sS.h, bev + max(0, dc - 1))
                                t.transpose(psb[dc % 2][:, 0:128],
                                            accB[:, dc * 128:(dc + 1) * 128],
                                            idb[:, :]).then_inc(hT.h, 1)
                        em.t(_)
                        hT.n += 8

                        def _(s, btr=btr):
                            for dc in range(8):
                                s.wait_ge(hT.h, btr + dc + 1)
                                s.copy(accT[:, dc * 128:(dc + 1) * 128],
                                       psb[dc % 2][:, 0:128])
                                s.drain().then_inc(sS.h, 1)
                        em.s(_)
                        sS.n += 8

                        brt = hT.n

                        def _(t, tc=tc, rp=rp, w=sS.n):
                            t.wait_ge(sS.h, w)
                            if tc >= 2:
                                t.wait_ge(vV.h, brm + 3 * (tc - 2) + 3)
                            for n2 in range(2):
                                acc_group(t, ps[rp + n2][:, :],
                                          [(accT[:, dc * 128:(dc + 1) * 128],
                                            R_bf[:, dc * C + n2 * 512:dc * C + (n2 + 1) * 512])
                                           for dc in range(8)])
                        em.t(_)
                        hT.n += 2

                        def _(v, bi=bi, rp=rp, brt=brt, w=dX[tc % 2].n, tc=tc):
                            v.wait_ge(hT.h, brt + 2)
                            v.wait_ge(dX[tc % 2].h, w)
                            v.tensor_tensor(prod[:, :], ps[rp][:, :], gcw2b[:, 0:512],
                                            op=ALU.mult)
                            v.drain()
                            v.reduce_sum(lnst[:, 0:1], prod[:, :], axis=AX.X)
                            v.drain()
                            v.tensor_tensor(prod[:, :], ps[rp + 1][:, :], gcw2b[:, 512:1024],
                                            op=ALU.mult)
                            v.drain()
                            v.reduce_sum(lnst[:, 4:5], prod[:, :], axis=AX.X)
                            v.drain()
                            v.tensor_tensor(lnst[:, 1:2], lnst[:, 0:1], lnst[:, 4:5],
                                            op=ALU.add)
                            v.drain()
                            v.tensor_tensor(lnst[:, 1:2], lnst[:, 1:2], gdot1[:, bi:bi + 1],
                                            op=ALU.add)
                            v.drain().then_inc(vV.h, 1)
                        em.v(_)
                        vV.n += 1

                        def _(s, w=vV.n):
                            s.wait_ge(vV.h, w)
                            s.activation(lnst[:, 2:3], lnst[:, 1:2], AF.Sigmoid,
                                         bias=gcbb[:, 0:1])
                            s.drain().then_inc(sS.h, 1)
                        em.s(_)
                        sS.n += 1

                        def _(v, bi=bi, tc=tc, rp=rp, w=sS.n, wo=oX[tc % 2].n):
                            v.wait_ge(sS.h, w)
                            v.wait_ge(oX[tc % 2].h, wo)
                            v.tensor_scalar(lnst[:, 3:4], lnst[:, 2:3], -1.0, 1.0,
                                            op0=ALU.mult, op1=ALU.add)
                            s4 = 4 * bi
                            v.tensor_scalar(otile[tc % 2][:, :], x2st[tc % 2][:, :],
                                            stats[:, s4:s4 + 1], stats[:, s4 + 1:s4 + 2],
                                            op0=ALU.mult, op1=ALU.add)
                            v.drain()
                            v.scalar_tensor_tensor(otile[tc % 2][:, :], abr[tc % 2][:, :],
                                                   lnst[:, 3:4], otile[tc % 2][:, :],
                                                   op0=ALU.mult, op1=ALU.add)
                            v.drain()
                            for n2 in range(2):
                                v.scalar_tensor_tensor(otile[tc % 2][:, n2 * 512:(n2 + 1) * 512],
                                                       ps[rp + n2][:, :], lnst[:, 2:3],
                                                       otile[tc % 2][:, n2 * 512:(n2 + 1) * 512],
                                                       op0=ALU.mult, op1=ALU.add)
                            v.drain().then_inc(vV.h, 1)
                        em.v(_)
                        vV.n += 1

                        def _(q, tc=tc, tg=tg, w=vV.n):
                            q.wait_ge(vV.h, w)
                            q.dma_start(out=out_ap[tg:tg + 128, :],
                                        in_=otile[tc % 2][:, :]).then_inc(oX[tc % 2].h, 16)
                        em.q(_)
                        oX[tc % 2].n += 16

                end_fence(em)

            dbg_live.update(acc=acc[:, :], hsb=h_sb[:, :])
            stop_block(None)

    return nc, dbg_aps


def _cid_const():
    import math
    cid = np.zeros((128, 2048), np.float32)
    for k in range(13):
        s = 1.0 / math.factorial(k)
        if k <= 8:
            s = -s
        cid[:, k * 128:(k + 1) * 128] = s * np.eye(128, dtype=np.float32)
    return cid


def _consts():
    import ml_dtypes
    i64 = np.eye(64, dtype=np.float32)
    eyeA = np.zeros((128, 8 * C), np.float32)
    for dc in range(8):
        eyeA[:, dc * C + dc * 128:dc * C + (dc + 1) * 128] = np.eye(128)
    bc2 = np.zeros((2, 128), np.float32)
    bc2[0, 0:64] = 1.0
    bc2[1, 64:128] = 1.0
    d = {
        "idb": np.eye(128, dtype=np.float32).astype(ml_dtypes.bfloat16),
        "idf": np.eye(128, dtype=np.float32),
        "onescol_b": np.ones((128, 1), np.float32).astype(ml_dtypes.bfloat16),
        "onescol_f": np.ones((128, 1), np.float32),
        "onesrow_b": np.ones((1, 128), np.float32).astype(ml_dtypes.bfloat16),
        "onesrow_f": np.ones((1, 64), np.float32),
        "eye7w": np.tile(7.0 * i64, (2, 8)).astype(np.float32),
        "eye15w": np.tile(15.0 * i64, (2, 8)).astype(np.float32),
        "eye13w": np.tile(13.0 * i64, (2, 8)).astype(np.float32),
        "eyeA": eyeA.astype(ml_dtypes.bfloat16),
        "cid0": _cid_const()[:, 0:1024].copy(),
        "cid1": _cid_const()[:, 1024:2048].copy(),
        "bc2": bc2,
        "epsA": np.full((128, 1), 1e-5, np.float32),
        "epsB": np.full((128, 1), 1e-10, np.float32),
    }
    return d


def kernel(L0_attn_x, L0_attn_W_attn, L0_attn_W_proj, L0_ffn_ln_gamma,
           L0_ffn_ln_beta, L0_ffn_abel_w, L0_ffn_abel_b, L0_ffn_moe_gate_w,
           L0_ffn_moe_gate_b, L0_ffn_ew1, L0_ffn_eb1, L0_ffn_ew2, L0_ffn_eb2,
           L0_ffn_gens, L0_ffn_gc_w, L0_ffn_gc_b):
    x = np.ascontiguousarray(np.asarray(L0_attn_x, np.float32))
    consts = _consts()
    g = np.asarray(L0_ffn_ln_gamma, np.float32)
    be = np.asarray(L0_ffn_ln_beta, np.float32)
    gamma_t = np.ascontiguousarray(g.reshape(8, 128).T)
    beta_t = np.ascontiguousarray(be.reshape(8, 128).T)
    eb1 = np.asarray(L0_ffn_eb1, np.float32)
    eb1t = np.ascontiguousarray(
        eb1.reshape(8, 16, 128).transpose(2, 0, 1).reshape(128, 128))
    eb2row = np.ascontiguousarray(np.asarray(L0_ffn_eb2, np.float32).reshape(1, NE * C))
    abelbrow = np.ascontiguousarray(np.asarray(L0_ffn_abel_b, np.float32).reshape(1, C))
    gatebrow = np.ascontiguousarray(
        np.asarray(L0_ffn_moe_gate_b, np.float32).reshape(1, 8))
    gw = np.asarray(L0_ffn_moe_gate_w, np.float32)
    gatew_t = np.ascontiguousarray(
        gw.reshape(8, 128, 8).transpose(1, 0, 2).reshape(128, 64))
    gcw = np.asarray(L0_ffn_gc_w, np.float32)
    gcw1b = np.ascontiguousarray(np.broadcast_to(gcw[:C, 0][None, :], (128, C)))
    gcw2b = np.ascontiguousarray(np.broadcast_to(gcw[C:, 0][None, :], (128, C)))
    gcbb = np.full((128, 1), float(np.asarray(L0_ffn_gc_b).reshape(())), np.float32)

    shared = dict(consts)
    shared.update(
        W_attn=np.ascontiguousarray(np.asarray(L0_attn_W_attn, np.float32)),
        W_proj=np.ascontiguousarray(np.asarray(L0_attn_W_proj, np.float32)),
        abel_w=np.ascontiguousarray(np.asarray(L0_ffn_abel_w, np.float32)),
        ew1=np.ascontiguousarray(np.asarray(L0_ffn_ew1, np.float32).reshape(NE * C, HD)),
        ew2=np.ascontiguousarray(np.asarray(L0_ffn_ew2, np.float32).reshape(NE * HD, C)),
        gens=np.ascontiguousarray(np.asarray(L0_ffn_gens, np.float32).reshape(3 * C, C)),
        gamma_t=gamma_t, beta_t=beta_t, eb1t=eb1t, eb2row=eb2row,
        abelbrow=abelbrow, gatebrow=gatebrow, gatew_t=gatew_t,
        gcw1b=gcw1b, gcw2b=gcw2b, gcbb=gcbb,
    )
    nc, _dbg = build()
    in_maps = []
    for c in range(8):
        m = dict(shared)
        m["x"] = np.ascontiguousarray(x[2 * c:2 * c + 2].reshape(B_PER * T, C))
        in_maps.append(m)
    res = run_bass_kernel_spmd(nc, in_maps, core_ids=list(range(8)))
    outs = [r["out"].reshape(B_PER, T, C) for r in res.results]
    return np.concatenate(outs, axis=0)
